# version 14
# baseline (speedup 1.0000x reference)
import numpy as np
import concourse.bass as bass
import concourse.mybir as mybir
from concourse.bass_utils import run_bass_kernel_spmd

F32 = mybir.dt.float32
BF16 = mybir.dt.bfloat16
AF = mybir.ActivationFunctionType
ALU = mybir.AluOpType
AX = mybir.AxisListType

D = 1024
T = 2048
NFC = 8
NQT = 16
NQB = 4
MAGIC = 12582912.0
BIGNEG = -1.0e8
QMAX16 = 32767.0
QMAX4 = 7.0
NCORE = 8

SUMX, SUMSQ, M_, VAR, RSTD, MR, RSQ = 0, 1, 2, 3, 4, 5, 6
QSQ0 = 8
KSQ0 = 12
RS0 = 14

SC_CC0 = 0
SC_SX, SC_IX = 4, 5
SC_SU, SC_IU = 6, 7
SC_SG, SC_IG = 8, 9
SC_SGC = 10
SC_S2, SC_I2 = 11, 12
SC_C3 = 13
SC_SO, SC_IO = 14, 15


def _tile_om(wT):
    K_, M_ = wT.shape
    return np.ascontiguousarray(
        wT.reshape(K_ // 128, 128, M_ // 128, 128).transpose(1, 2, 0, 3))


def _tile_mv(wT):
    K_, N_ = wT.shape
    return np.ascontiguousarray(wT.reshape(K_ // 128, 128, N_).transpose(1, 0, 2))


def _pcol(v):
    return np.ascontiguousarray(v.reshape(-1, 128).T)


class Sem:
    def __init__(self, nc, name):
        self.h = nc.semaphore(name).__enter__()
        self.n = 0

    def inc(self, ins, k=1):
        ins.then_inc(self.h, k)
        self.n += k
        return self.n

    def dinc(self, eng, k=1):
        eng.maybe_drain_then_inc((self.h, k))
        self.n += k
        return self.n


class Builder:
    def __init__(self, debug_taps=(), n_cores=NCORE):
        self.n_cores = n_cores
        self.debug_taps = list(debug_taps)
        self.nc = bass.Bass("TRN2", target_bir_lowering=False,
                            detect_race_conditions=False)
        self.dbg = {}
        self._declare_dram()
        self._declare_sbuf()
        for name, shape, dt in self.debug_taps:
            self.dbg[name] = self.nc.dram_tensor(
                "dbg_" + name, list(shape), dt, kind="ExternalOutput").ap()
        self._block_cm = self.nc.Block()
        self.block = self._block_cm.__enter__()
        nc = self.nc
        self.sem_w = Sem(nc, "sw")
        self.sem_x = Sem(nc, "sx")
        self.sem_t = Sem(nc, "stp")
        self.sem_pe = Sem(nc, "spe")
        self.sem_v = Sem(nc, "sv")
        self.sem_s = Sem(nc, "ssc")
        self.sem_g = Sem(nc, "sg")
        self.sem_cc = Sem(nc, "scc")
        self.bank_free = [None] * 8

    def finish(self):
        self._block_cm.__exit__(None, None, None)
        return self.nc

    def _declare_dram(self):
        nc = self.nc

        def din(name, shape, dt=F32):
            setattr(self, name, nc.dram_tensor(name, list(shape), dt,
                                               kind="ExternalInput").ap())

        din("xT", (D, T))
        din("wq_t", (128, 8, 8, 128)); din("wk_t", (128, 4, 8, 128))
        din("wv_m", (128, 8, 512)); din("wo_t", (128, 8, 8, 128))
        din("nw_p", (128, 8)); din("gate_s", (1, 1)); din("ltemp_s", (1, 1))
        din("w1_t", (128, 32, 8, 128)); din("w2_t", (128, 32, 8, 128))
        din("h1_t", (128, 32, 32, 128)); din("h2_t", (128, 32, 32, 128))
        din("w3_t", (128, 8, 32, 128))
        din("s1_p", (128, 32)); din("s2_p", (128, 32))
        din("sh1_p", (128, 32)); din("sh2_p", (128, 32)); din("s3_p", (128, 8))
        din("qkvq_t", (128, 8, 8, 128)); din("qkvk_t", (128, 8, 8, 128))
        din("qkvv_m", (128, 8, 1024))
        din("bq_p", (128, 8)); din("bk_p", (128, 8)); din("bv_row", (1024,))
        din("wp_t", (128, 8, 8, 128)); din("bp_p", (128, 8))
        din("wup_n", (3072, 1024)); din("wgate_n", (3072, 1024))
        din("wdown_n", (1024, 3072))
        din("ident_in", (128, 128)); din("cmask_in", (128, 4, 512), BF16)

        self.out = nc.dram_tensor("out", [D, T], F32, kind="ExternalOutput").ap()

        def scr(name, shape, dt=F32):
            setattr(self, name, nc.dram_tensor(name, list(shape), dt).ap())

        scr("acc0_pk", (D, T)); scr("attn0_pk", (D, T), BF16)
        scr("xn0_pk", (D, T), BF16); scr("x1_pk", (D, T))
        scr("attn1_pk", (D, T), BF16); scr("xq_pk", (D, T), BF16)
        scr("q1_pk", (D, T), BF16)
        scr("upp_pk", (3072, T), BF16); scr("gatep_pk", (3072, T), BF16)
        scr("g2p_pk", (3072, T), BF16); scr("opp_pk", (3072, T), BF16)
        scr("oq_pk", (3072, T), BF16)
        scr("row_bnc", (8, T)); scr("cc_in", (1, 4)); scr("cc_out", (1, 4))
        scr("red_bnc", (4, 128))

    def _declare_sbuf(self):
        nc = self.nc

        def sb(name, shape, dt=F32):
            setattr(self, name, nc.alloc_sbuf_tensor(name, list(shape), dt).ap())

        sb("ident", (128, 128)); sb("cmask", (128, 4, 512), BF16)
        sb("nw_sb", (128, 8)); sb("s1_sb", (128, 32)); sb("s2_sb", (128, 32))
        sb("sh1_sb", (128, 32)); sb("sh2_sb", (128, 32)); sb("s3_sb", (128, 8))
        sb("bq_sb", (128, 8)); sb("bk_sb", (128, 8)); sb("bp_sb", (128, 8))
        sb("ones_sb", (128, 1)); sb("gate_sb", (128, 1))
        sb("tco", (128, 8))
        sb("scal_sb", (128, 16))
        sb("st_sb", (128, 16, 16))
        sb("rowst", (128, 128))
        sb("red_sb", (128, 8))
        sb("ccst", (128, 2))
        sb("dqs", (128, 96)); sb("dqi", (128, 96))
        sb("bv_sb", (128, 1024), BF16)
        sb("FS", (128, 4096))
        sb("XCH", (128, 2048))
        sb("BSH", (128, 2048))
        sb("BB_A", (128, 32768), BF16)
        sb("BB_B", (128, 32768), BF16)
        sb("BB_C", (128, 16384), BF16)
        sb("WB", (128, 2048), BF16)
        self.PS = nc.alloc_psum_tensor("PS", [128, 8, 512], F32).ap()
        self.PSrow = self.PS.rearrange("p a b -> p (a b)")

    def fs(self, lo, hi, shape=None):
        ap = self.FS[:, lo:hi]
        return ap if shape is None else _resh(ap, shape)

    def bba(self, lo, hi, shape=None):
        ap = self.BB_A[:, lo:hi]
        return ap if shape is None else _resh(ap, shape)

    def bbb(self, lo, hi, shape=None):
        ap = self.BB_B[:, lo:hi]
        return ap if shape is None else _resh(ap, shape)

    def bbc(self, lo, hi, shape=None):
        ap = self.BB_C[:, lo:hi]
        return ap if shape is None else _resh(ap, shape)

    def wb(self, lo, hi, shape=None):
        ap = self.WB[:, lo:hi]
        return ap if shape is None else _resh(ap, shape)

    def sec(self, eng):
        return getattr(self.block, eng)

    def tap(self, name, src_fn, dep_sem=None, dep_tok=None, chunks=None):
        if name in self.dbg:
            ap = self.dbg[name]
            if chunks is not None:
                ap = ap.rearrange("(c p) t -> p c t", p=128)
            sx = self.sem_x

            @self.sec("sync")
            def _(s):
                if dep_sem is not None:
                    s.wait_ge(dep_sem.h, dep_tok)
                sx.inc(s.dma_start(out=ap, in_=src_fn()), 16)

    def pe_guard(self, t, banks):
        for b in banks:
            if self.bank_free[b] is not None:
                sem, cnt = self.bank_free[b]
                t.wait_ge(sem.h, cnt)
                self.bank_free[b] = None

    def bank_done(self, banks, sem):
        for b in banks:
            self.bank_free[b] = (sem, sem.n)

    def bounce_rows(self, src_cols_ap, ncols, rows_at):
        PS, ident, rowst = self.PS, self.ident, self.rowst
        src2 = src_cols_ap.rearrange("p a b -> p (a b)")

        @self.sec("tensor")
        def _(t):
            t.wait_ge(self.sem_v.h, self.sem_v.n)
            t.wait_ge(self.sem_s.h, self.sem_s.n)
            t.wait_ge(self.sem_w.h, self.consts_tok)
            self.pe_guard(t, [5])
            t.transpose(PS[0:16 * ncols, 5, 0:128], src2, ident[:])
            self.sem_pe.dinc(t)
        tr = self.sem_pe.n

        @self.sec("vector")
        def _(v):
            v.wait_ge(self.sem_pe.h, tr)
            v.tensor_copy(rowst[0:16 * ncols, :], PS[0:16 * ncols, 5, 0:128])
            self.sem_v.dinc(v)
        self.bank_done([5], self.sem_v)
        cp = self.sem_v.n

        if rows_at == 0:
            self.tap("st3", lambda: src2, self.sem_pe, tr)
            self.tap("rowst48", lambda: rowst[0:48, :], self.sem_v, cp)

        @self.sec("sync")
        def _(s):
            s.wait_ge(self.sem_v.h, cp)
            for j in range(ncols):
                self.sem_x.inc(s.dma_start(
                    out=self.row_bnc[rows_at + j:rows_at + j + 1, :]
                        .rearrange("r (a b) -> (r a) b", b=128),
                    in_=rowst[16 * j:16 * (j + 1), :]), 16)
        return self.sem_x.n

    def load_bcast(self, dst_ap, row, width=T):
        @self.sec("sync")
        def _(s):
            s.wait_ge(self.sem_x.h, self.sem_x.n)
            self.sem_x.inc(s.dma_start(
                out=dst_ap,
                in_=self.row_bnc[row:row + 1, 0:width].broadcast_to([128, width])),
                16)
        return self.sem_x.n


def _resh(ap, shape):
    if len(shape) == 2:
        return ap
    letters = "abcde"[: len(shape) - 1]
    spec = "p (" + " ".join(letters) + ") -> p " + " ".join(letters)
    kw = {l: s for l, s in zip(letters, shape[1:])}
    return ap.rearrange(spec, **kw)


def _v_marker(B, sem, tok):
    @B.sec("vector")
    def _(v):
        v.wait_ge(sem.h, tok)
        B.sem_v.dinc(v)
    return B.sem_v.n



def p0_consts(B):
    sw = B.sem_w

    @B.sec("sync")
    def _(s):
        pairs = [(B.ident, B.ident_in), (B.cmask, B.cmask_in),
                 (B.nw_sb, B.nw_p), (B.s1_sb, B.s1_p), (B.s2_sb, B.s2_p),
                 (B.sh1_sb, B.sh1_p), (B.sh2_sb, B.sh2_p), (B.s3_sb, B.s3_p),
                 (B.bq_sb, B.bq_p), (B.bk_sb, B.bk_p), (B.bp_sb, B.bp_p)]
        for o, i in pairs:
            sw.inc(s.dma_start(out=o[:], in_=i[:]), 16)
        sw.inc(s.dma_start(out=B.gate_sb[:],
                           in_=B.gate_s[0:1, :].broadcast_to([128, 1])), 16)
        sw.inc(s.dma_start(out=B.tco[:, 0:1],
                           in_=B.ltemp_s[0:1, :].broadcast_to([128, 1])), 16)
    B.consts_tok = sw.n

    @B.sec("vector")
    def _(v):
        v.wait_ge(sw.h, B.consts_tok)
        v.memset(B.ones_sb[:], 1.0)
        v.memset(B.tco[:, 3:4], 8.0)
        v.memset(B.tco[:, 4:5], MAGIC)
        B.sem_v.dinc(v)
    B.ones_tok = B.sem_v.n

    @B.sec("scalar")
    def _(sc):
        sc.wait_ge(sw.h, B.consts_tok)
        sc.activation(out=B.tco[:, 1:2], in_=B.tco[:, 0:1], func=AF.Exp)
        B.sem_s.dinc(sc)
    t_tok = B.sem_s.n

    @B.sec("vector")
    def _(v):
        v.wait_ge(B.sem_s.h, t_tok)
        v.tensor_scalar_mul(B.tco[:, 2:3], B.tco[:, 1:2], -1.0)
        B.sem_v.dinc(v)
    B.negt_tok = B.sem_v.n

    @B.sec("gpsimd")
    def _(gp):
        B.sem_g.inc(gp.dma_start(
            out=B.bv_sb[:],
            in_=B.bv_row[None, :].broadcast_to([128, 1024])), 16)
        gp.wait_ge(sw.h, B.consts_tok)
    B.bv_tok = B.sem_g.n

    @B.sec("tensor")
    def _(t):
        t.wait_ge(sw.h, B.consts_tok)


def ln_stats(B, src_pk, layer, xwait=None):
    PS, st = B.PS, B.st_sb
    xch = B.XCH
    sq = B.fs(0, 2048)
    prev_pe = 0
    for fc in range(NFC):
        @B.sec("sync")
        def _(s, fc=fc, prev=prev_pe):
            if fc == 0 and xwait is not None:
                s.wait_ge(xwait[0].h, xwait[1])
            if prev:
                s.wait_ge(B.sem_pe.h, prev)
            B.sem_x.inc(s.dma_start(
                out=xch[:],
                in_=src_pk.rearrange("(c p) t -> p c t", p=128)[:, fc, :]), 16)
        ld = B.sem_x.n

        @B.sec("scalar")
        def _(sc, fc=fc, ld=ld, prev=prev_pe):
            sc.wait_ge(B.sem_x.h, ld)
            if prev:
                sc.wait_ge(B.sem_pe.h, prev)
            sc.activation(out=sq, in_=xch[:], func=AF.Square)
            B.sem_s.dinc(sc)
        sq_tok = B.sem_s.n

        @B.sec("tensor")
        def _(t, fc=fc, sq_tok=sq_tok, ld=ld):
            t.wait_ge(B.sem_s.h, sq_tok)
            t.wait_ge(B.sem_x.h, ld)
            if fc == 0:
                t.wait_ge(B.sem_v.h, B.ones_tok)
            B.pe_guard(t, [6, 7])
            ins = None
            for qt in range(NQT):
                t.matmul(PS[:, 6, qt:qt + 1], xch[:, qt * 128:(qt + 1) * 128],
                         B.ones_sb[:], start=True, stop=True)
                ins = t.matmul(PS[:, 7, qt:qt + 1], sq[:, qt * 128:(qt + 1) * 128],
                               B.ones_sb[:], start=True, stop=True)
            B.sem_pe.inc(ins)
        prev_pe = B.sem_pe.n

        @B.sec("vector")
        def _(v, fc=fc, pe=prev_pe):
            v.wait_ge(B.sem_pe.h, pe)
            if fc == 0:
                v.tensor_copy(st[:, SUMX, :], PS[:, 6, 0:16])
                v.tensor_copy(st[:, SUMSQ, :], PS[:, 7, 0:16])
                B.sem_v.dinc(v)
            else:
                v.tensor_add(st[:, SUMX, :], st[:, SUMX, :], PS[:, 6, 0:16])
                v.tensor_add(st[:, SUMSQ, :], st[:, SUMSQ, :],
                                         PS[:, 7, 0:16])
                B.sem_v.dinc(v)
        B.bank_done([6, 7], B.sem_v)

    @B.sec("vector")
    def _(v):
        v.tensor_scalar_mul(st[:, M_, :], st[:, SUMX, :], 1.0 / D)
        v.tensor_scalar_mul(st[:, VAR, :], st[:, SUMSQ, :], 1.0 / D)
        v.drain()
        v.tensor_mul(st[:, RSTD, :], st[:, M_, :], st[:, M_, :])
        v.drain()
        v.tensor_sub(st[:, VAR, :], st[:, VAR, :], st[:, RSTD, :])
        v.drain()
        v.tensor_scalar_add(st[:, RSTD, :], st[:, VAR, :], 1e-5)
        v.drain()
        v.reciprocal(st[:, MR, :], st[:, RSTD, :])
        B.sem_v.dinc(v)
    inv_tok = B.sem_v.n

    @B.sec("scalar")
    def _(sc):
        sc.wait_ge(B.sem_v.h, inv_tok)
        sc.activation(out=st[:, RSTD, :], in_=st[:, MR, :], func=AF.Sqrt)
        B.sem_s.dinc(sc)
    rstd_tok = B.sem_s.n

    if layer == 0:
        @B.sec("vector")
        def _(v):
            v.wait_ge(B.sem_s.h, rstd_tok)
            v.tensor_mul(st[:, RSQ, :], st[:, VAR, :], st[:, MR, :])
            v.drain()
            v.tensor_scalar_add(st[:, RSQ, :], st[:, RSQ, :], 1e-6)
            v.drain()
            v.reciprocal(st[:, RSQ, :], st[:, RSQ, :])
            B.sem_v.dinc(v)
        t3 = B.sem_v.n

        @B.sec("scalar")
        def _(sc):
            sc.wait_ge(B.sem_v.h, t3)
            sc.activation(out=st[:, RSQ, :], in_=st[:, RSQ, :], func=AF.Sqrt)
            B.sem_s.dinc(sc)

    @B.sec("vector")
    def _(v):
        v.wait_ge(B.sem_s.h, B.sem_s.n)
        v.tensor_mul(st[:, MR, :], st[:, M_, :], st[:, RSTD, :])
        B.sem_v.dinc(v)

    if layer == 0:
        B.tap("st8", lambda: st[:, 0:8, :].rearrange("p a b -> p (a b)"),
              B.sem_v, B.sem_v.n)
    ncols = 3 if layer == 0 else 2
    B.bounce_rows(st[:, RSTD:RSTD + ncols, :], ncols, 0)


def p1_xn_acc(B):
    chunk = B.XCH[:]
    xnf = B.fs(0, 2048)
    mr_b = B.fs(2048, 4096)
    r_b = B.BSH
    hh = B.bbc(0, 16384, [128, 8, 2048])
    B.h0 = hh
    xnst = [B.bbb(i * 2048, (i + 1) * 2048) for i in range(2)]

    B.load_bcast(r_b[:], 0)
    B.load_bcast(mr_b, 1)
    bc_tok = B.sem_x.n

    xn_st_tok = [0, 0]
    prev_v = 0
    for fc in range(NFC):
        @B.sec("sync")
        def _(s, fc=fc, pv=prev_v):
            s.wait_ge(B.sem_x.h, bc_tok)
            if pv:
                s.wait_ge(B.sem_v.h, pv)
            B.sem_x.inc(s.dma_start(
                out=chunk,
                in_=B.xT.rearrange("(c p) t -> p c t", p=128)[:, fc, :]), 16)
        ld = B.sem_x.n

        sl = fc % 2

        @B.sec("vector")
        def _(v, fc=fc, ld=ld, sl=sl, stt=xn_st_tok[sl]):
            v.wait_ge(B.sem_x.h, ld)
            if stt:
                v.wait_ge(B.sem_x.h, stt)
            v.tensor_mul(xnf, chunk, r_b[:])
            v.tensor_sub(xnf, xnf, mr_b)
            v.tensor_copy(xnst[sl], xnf)
            v.tensor_scalar_mul(hh[:, fc, :], xnf, B.nw_sb[:, fc:fc + 1])
            v.tensor_add(chunk, chunk, xnf)
            B.sem_v.dinc(v)
        prev_v = B.sem_v.n

        @B.sec("sync")
        def _(s, fc=fc, sl=sl, pv=prev_v):
            s.wait_ge(B.sem_v.h, pv)
            B.sem_x.inc(s.dma_start(
                out=B.xn0_pk.rearrange("(c p) t -> p c t", p=128)[:, fc, :],
                in_=xnst[sl]), 16)
            B.sem_x.inc(s.dma_start(
                out=B.acc0_pk.rearrange("(c p) t -> p c t", p=128)[:, fc, :],
                in_=chunk), 16)
        xn_st_tok[sl] = B.sem_x.n
    B.park_tok = B.sem_x.n
    B.h_ready_tok = B.sem_v.n
    B.tap("h0", lambda: B.h0[:], B.sem_v, B.h_ready_tok, chunks=8)


def mm_proj(B, w_dram, n_oc, out_writer, moving_fn, extra_wait=None,
            n_kc=NFC, nb=NQB, wname="w"):
    pe_start = B.sem_pe.n
    assert n_kc * 128 * 2 <= 2048
    wslots = [B.wb(i * n_kc * 128, (i + 1) * n_kc * 128, [128, n_kc, 128])
              for i in range(2)]
    slot_load = [0, 0]
    slot_use = [pe_start, pe_start]
    for oc in range(n_oc):
        sl = oc % 2

        @B.sec("gpsimd")
        def _(gp, oc=oc, sl=sl, u=slot_use[sl]):
            if u:
                gp.wait_ge(B.sem_pe.h, u)
            B.sem_g.inc(gp.dma_start(out=wslots[sl][:], in_=w_dram[:, oc, :, :]), 16)
        slot_load[sl] = B.sem_g.n

        for b in range(nb):
            bank = (oc * nb + b) % 4

            @B.sec("tensor")
            def _(t, oc=oc, b=b, bank=bank, sl=sl, ld=slot_load[sl]):
                t.wait_ge(B.sem_g.h, ld)
                if extra_wait is not None:
                    t.wait_ge(extra_wait[0].h, extra_wait[1])
                B.pe_guard(t, [bank])
                ins = None
                for kc in range(n_kc):
                    ins = t.matmul(B.PS[:, bank, :], wslots[sl][:, kc, :],
                                   moving_fn(kc, b),
                                   start=(kc == 0), stop=(kc == n_kc - 1))
                B.sem_pe.inc(ins)
            slot_use[sl] = B.sem_pe.n
            out_writer(oc, b, bank, B.sem_pe.n)


def p2_qkv(B):
    PS, st = B.PS, B.st_sb
    hh = B.h0
    q_bf = B.bba(0, 16384, [128, 8, 2048])
    k_bf = B.bba(16384, 24576, [128, 4, 2048])
    v_bf = B.bba(24576, 32768, [128, 16, 512])
    B.q0, B.k0, B.v0 = q_bf, k_bf, v_bf
    h_ready = (B.sem_v, B.h_ready_tok)

    rsq_b = B.BSH
    B.load_bcast(rsq_b[:], 2)
    rsq_tok = B.sem_x.n
    B.tap("rsqb", lambda: rsq_b[:], B.sem_x, rsq_tok)

    def evac_rsq(dst):
        def w(oc, b, bank, pe_tok):
            @B.sec("vector")
            def _(v, oc=oc, b=b, bank=bank, pe_tok=pe_tok):
                v.wait_ge(B.sem_pe.h, pe_tok)
                v.wait_ge(B.sem_x.h, rsq_tok)
                v.tensor_mul(dst(oc, b), PS[:, bank, :],
                                         rsq_b[:, b * 512:(b + 1) * 512])
                B.sem_v.dinc(v)
            B.bank_done([bank], B.sem_v)
        return w

    mm_proj(B, B.wq_t, 8,
            evac_rsq(lambda oc, b: q_bf[:, oc, b * 512:(b + 1) * 512]),
            lambda kc, b: hh[:, kc, b * 512:(b + 1) * 512],
            extra_wait=h_ready, wname="wq")
    mm_proj(B, B.wk_t, 4,
            evac_rsq(lambda oc, b: k_bf[:, oc, b * 512:(b + 1) * 512]),
            lambda kc, b: hh[:, kc, b * 512:(b + 1) * 512],
            extra_wait=h_ready, wname="wk")

    if getattr(B, "stop_after", None) == "min2":
        B.trq_tok = B.sem_v.n
        B.khat_tok = B.sem_v.n
        return
    wv_sb = B.bbb(14336, 18432, [128, 8, 512])
    wv_use = B.sem_pe.n

    @B.sec("gpsimd")
    def _(gp, u=wv_use):
        gp.wait_ge(B.sem_pe.h, u)
        B.sem_g.inc(gp.dma_start(out=wv_sb[:], in_=B.wv_m[:]), 16)
    wv_tok = B.sem_g.n

    for tt in range(NQT):
        bank = tt % 4

        @B.sec("tensor")
        def _(t, tt=tt, bank=bank):
            t.wait_ge(B.sem_g.h, wv_tok)
            B.pe_guard(t, [bank])
            ins = None
            for kc in range(NFC):
                ins = t.matmul(PS[:, bank, :], hh[:, kc, tt * 128:(tt + 1) * 128],
                               wv_sb[:, kc, :], start=(kc == 0), stop=(kc == NFC - 1))
            B.sem_pe.inc(ins)
        pe_tok = B.sem_pe.n

        @B.sec("vector")
        def _(v, tt=tt, bank=bank, pe_tok=pe_tok):
            v.wait_ge(B.sem_pe.h, pe_tok)
            v.tensor_scalar_mul(v_bf[:, tt, :], PS[:, bank, :],
                                            B.st_sb[:, RSQ, tt:tt + 1])
            B.sem_v.dinc(v)
        B.bank_done([bank], B.sem_v)
    qkv_tok = B.sem_v.n

    sq2 = B.BSH
    prev_stat_pe = 0
    for (which, n_oc, col0) in (("q", 8, QSQ0), ("k", 4, KSQ0)):
        src = q_bf if which == "q" else k_bf
        for oc in range(n_oc):
            @B.sec("scalar")
            def _(sc, oc=oc, src=src, prev=prev_stat_pe):
                if prev == 0:
                    sc.wait_ge(B.sem_v.h, qkv_tok)
                else:
                    sc.wait_ge(B.sem_pe.h, prev)
                sc.activation(out=sq2[:], in_=src[:, oc, :],
                                          func=AF.Square)
                B.sem_s.dinc(sc)
            sq_tok = B.sem_s.n

            @B.sec("tensor")
            def _(t, sq_tok=sq_tok):
                t.wait_ge(B.sem_s.h, sq_tok)
                B.pe_guard(t, [6])
                ins = None
                for qt in range(NQT):
                    ins = t.matmul(PS[:, 6, qt:qt + 1],
                                   sq2[:, qt * 128:(qt + 1) * 128], B.ones_sb[:],
                                   start=True, stop=True)
                B.sem_pe.inc(ins)
            prev_stat_pe = B.sem_pe.n

            @B.sec("vector")
            def _(v, oc=oc, col0=col0, pe_tok=prev_stat_pe):
                v.wait_ge(B.sem_pe.h, pe_tok)
                col = col0 + oc // 2
                if oc % 2 == 0:
                    v.tensor_copy(st[:, col, :], PS[:, 6, 0:16])
                    B.sem_v.dinc(v)
                else:
                    v.tensor_add(st[:, col, :], st[:, col, :],
                                             PS[:, 6, 0:16])
                    B.sem_v.dinc(v)
            B.bank_done([6], B.sem_v)
    nsq_tok = B.sem_v.n

    @B.sec("scalar")
    def _(sc):
        sc.wait_ge(B.sem_v.h, nsq_tok)
        a = st[:, QSQ0:QSQ0 + 6, :].rearrange("p a b -> p (a b)")
        sc.activation(out=a, in_=a, func=AF.Sqrt)
        B.sem_s.dinc(sc)
    sq_tok = B.sem_s.n

    @B.sec("vector")
    def _(v):
        v.wait_ge(B.sem_s.h, sq_tok)
        v.wait_ge(B.sem_v.h, B.negt_tok)
        a = st[:, QSQ0:QSQ0 + 6, :].rearrange("p a b -> p (a b)")
        v.tensor_scalar_max(a, a, 1e-12)
        v.drain()
        v.reciprocal(a, a)
        v.drain()
        qa = st[:, QSQ0:QSQ0 + 4, :].rearrange("p a b -> p (a b)")
        v.tensor_scalar_mul(qa, qa, B.tco[:, 1:2])
        B.sem_v.dinc(v)
    B.trq_tok = B.sem_v.n

    B.bounce_rows(st[:, KSQ0:KSQ0 + 2, :], 2, 3)
    for kv in range(2):
        rk_b = B.BSH
        B.load_bcast(rk_b[:], 3 + kv)
        bt = B.sem_x.n

        @B.sec("vector")
        def _(v, kv=kv, bt=bt):
            v.wait_ge(B.sem_x.h, bt)
            for j in range(2):
                v.tensor_mul(k_bf[:, 2 * kv + j, :],
                                         k_bf[:, 2 * kv + j, :], rk_b[:])
                B.sem_v.dinc(v)
    B.khat_tok = B.sem_v.n
    B.tap("q0", lambda: B.q0[:], B.sem_v, qkv_tok, chunks=8)
    B.tap("k0", lambda: B.k0[:], B.sem_v, B.khat_tok, chunks=4)
    B.tap("v0", lambda: B.v0[:], B.sem_v, qkv_tok, chunks=16)


def p2_attn(B):
    PS, PSrow, st = B.PS, B.PSrow, B.st_sb
    q_bf, k_bf, v_bf = B.q0, B.k0, B.v0
    Pt = [B.bbb(i * 2048, (i + 1) * 2048) for i in range(2)]
    PT = B.bbb(4096, 12288, [128, 16, 512])
    attn = B.bbc(0, 16384, [128, 8, 2048])
    B.attn0 = attn

    p_free = [0, 0]
    prev_pv = 0
    first = True
    only1 = getattr(B, "stop_after", None) in ("s00", "nostats", "min2")
    for h in range(1 if only1 else 4):
        kv = h // 2
        for qb in range(1 if only1 else NQB):
            nkc = 4 * qb + 4
            tr_last = 0
            for qi in range(1 if only1 else 4):
                qt = 4 * qb + qi
                nb = qt // 4 + 1
                psl = qt % 2

                @B.sec("tensor")
                def _(t, qt=qt, nb=nb, h=h, kv=kv, first=first,
                      bar=(B.sem_x.n, B.sem_w.n, B.sem_t.n,
                           B.sem_s.n, B.sem_g.n)):
                    if first:
                        t.wait_ge(B.sem_v.h, B.khat_tok)
                        dbgbar = getattr(B, "dbg_bar", "")
                        for nm, sem, cnt in (("x", B.sem_x, bar[0]),
                                             ("w", B.sem_w, bar[1]),
                                             ("t", B.sem_t, bar[2]),
                                             ("s", B.sem_s, bar[3]),
                                             ("g", B.sem_g, bar[4])):
                            if nm in dbgbar:
                                t.wait_ge(sem.h, cnt)
                    B.pe_guard(t, list(range(nb)))
                    ins = None
                    for b in range(nb):
                        for j in range(2):
                            ins = t.matmul(PS[:, b, :],
                                           q_bf[:, 2 * h + j, qt * 128:(qt + 1) * 128],
                                           k_bf[:, 2 * kv + j, b * 512:(b + 1) * 512],
                                           start=(j == 0), stop=(j == 1))
                    B.sem_pe.inc(ins)
                s_tok = B.sem_pe.n

                if h == 0 and qb == 0 and qi == 0 and "S00pre" in B.dbg:
                    sstg2 = B.fs(3584, 4096)

                    @B.sec("vector")
                    def _(v, tok=s_tok):
                        v.wait_ge(B.sem_pe.h, tok)
                        v.tensor_copy(sstg2, PS[:, 0, :])
                        B.sem_v.dinc(v)
                    B.tap("S00pre", lambda: sstg2, B.sem_v, B.sem_v.n)

                @B.sec("vector")
                def _(v, qt=qt, nb=nb, tok=s_tok):
                    v.wait_ge(B.sem_pe.h, tok)
                    v.tensor_add(PS[:, nb - 1, :], PS[:, nb - 1, :],
                                 B.cmask[:, qt % 4, :])
                    B.sem_v.dinc(v)
                m_tok = B.sem_v.n
                if h == 0 and qb == 0 and qi == 0 and "S00" in B.dbg:
                    sstg = B.fs(3072, 3584)

                    @B.sec("vector")
                    def _(v):
                        v.tensor_copy(sstg, PS[:, 0, :])
                        B.sem_v.dinc(v)
                    m_tok = B.sem_v.n
                    B.tap("S00", lambda: sstg, B.sem_v, m_tok)

                @B.sec("scalar")
                def _(sc, qt=qt, qi=qi, psl=psl, nb=nb, h=h, tok=m_tok,
                      pp=p_free[psl]):
                    sc.wait_ge(B.sem_v.h, tok)
                    if pp:
                        sc.wait_ge(B.sem_t.h, pp)
                    sc.activation(
                        out=Pt[psl][:, 0:nb * 512], in_=PSrow[:, 0:nb * 512],
                        func=AF.Exp, scale=st[:, QSQ0 + h, qt:qt + 1],
                        bias=B.tco[:, 2:3],
                        accum_out=st[:, RS0, qi:qi + 1])
                    B.sem_s.dinc(sc)
                e_tok = B.sem_s.n
                B.bank_done(list(range(nb)), B.sem_s)
                first = False

                @B.sec("vector")
                def _(v, qi=qi, psl=psl, nb=nb, tok=e_tok):
                    v.wait_ge(B.sem_s.h, tok)
                    v.reciprocal(st[:, RS0, qi:qi + 1], st[:, RS0, qi:qi + 1])
                    v.drain()
                    v.tensor_scalar_mul(Pt[psl][:, 0:nb * 512],
                                        Pt[psl][:, 0:nb * 512],
                                        st[:, RS0, qi:qi + 1])
                    B.sem_v.dinc(v)
                n_tok = B.sem_v.n

                @B.sec("vector")
                def _(v, qt=qt, qi=qi, nkc=nkc, pv=prev_pv):
                    if qi == 0 and pv:
                        v.wait_ge(B.sem_pe.h, pv)
                    for kc in range(qt + 1, nkc):
                        v.memset(PT[:, kc, qi * 128:(qi + 1) * 128], 0.0)
                    B.sem_v.dinc(v)
                z_tok = B.sem_v.n

                @B.sec("sync")
                def _(s, qt=qt, qi=qi, psl=psl, zt=z_tok, nt=n_tok):
                    s.wait_ge(B.sem_v.h, zt)
                    s.wait_ge(B.sem_v.h, nt)
                    B.sem_t.inc(s.dma_start_transpose(
                        PT[:, 0:qt + 1, qi * 128:(qi + 1) * 128],
                        Pt[psl][:, 0:(qt + 1) * 128]), 16)
                p_free[psl] = B.sem_t.n
                tr_last = B.sem_t.n
                if h == 0 and qb == 0 and qi == 0:
                    B.tap("P00e", lambda: Pt[0][:, 0:512], B.sem_s, e_tok)
                    B.tap("rs00", lambda: B.st_sb[:, RS0, :], B.sem_s, e_tok)
                    B.tap("trq2", lambda: B.st_sb[:, QSQ0:QSQ0 + 6, :]
                          .rearrange("p a b -> p (a b)"), B.sem_v, B.trq_tok)
                    B.tap("P00", lambda: Pt[0][:, 0:512], B.sem_v, n_tok)
                    B.tap("PT00", lambda: PT[:, 0, 0:128], B.sem_t, tr_last)
                    B.tap("rs0", lambda: B.st_sb[:, RS0, :], B.sem_s, e_tok)
                    B.tap("trq", lambda: B.st_sb[:, QSQ0:QSQ0 + 6, :]
                          .rearrange("p a b -> p (a b)"), B.sem_v, B.trq_tok)
                    B.tap("tco", lambda: B.tco[:, :], B.sem_v, B.negt_tok)
                    B.tap("cmask", lambda: B.cmask.rearrange("p a b -> p (a b)"),
                          B.sem_w, B.consts_tok)

            for vc in range(0 if only1 else 2):
                bank = 4 + (2 * qb + vc) % 2

                @B.sec("tensor")
                def _(t, h=h, qb=qb, vc=vc, bank=bank, nkc=nkc, tt=tr_last):
                    t.wait_ge(B.sem_t.h, tt)
                    B.pe_guard(t, [bank])
                    ins = None
                    for kc in range(nkc):
                        ins = t.matmul(PS[:, bank, :],
                                       v_bf[:, kc, vc * 128:(vc + 1) * 128],
                                       PT[:, kc, :],
                                       start=(kc == 0), stop=(kc == nkc - 1))
                    B.sem_pe.inc(ins)
                pv_tok = B.sem_pe.n
                prev_pv = pv_tok

                @B.sec("vector")
                def _(v, h=h, qb=qb, vc=vc, bank=bank, tok=pv_tok):
                    v.wait_ge(B.sem_pe.h, tok)
                    v.tensor_copy(
                        attn[:, 2 * h + vc, qb * 512:(qb + 1) * 512], PS[:, bank, :])
                    B.sem_v.dinc(v)
                B.bank_done([bank], B.sem_v)
    B.attn_tok = B.sem_v.n
    B.tap("attn0", lambda: B.attn0[:], B.sem_v, B.attn_tok, chunks=8)
    B.tap("k0b", lambda: B.k0[:], B.sem_v, B.attn_tok, chunks=4)
    B.tap("q0b", lambda: B.q0[:], B.sem_v, B.attn_tok, chunks=8)


def p3_wo_park(B):
    attn = B.attn0
    stg = [B.bbb(12288 + i * 512, 12288 + (i + 1) * 512) for i in range(4)]
    aw = (B.sem_v, B.attn_tok)
    stage_tok = [0, 0, 0, 0]

    def writer(oc, b, bank, pe_tok):
        sl = (oc * NQB + b) % 4

        @B.sec("vector")
        def _(v, bank=bank, pe_tok=pe_tok, sl=sl, stt=stage_tok[sl]):
            v.wait_ge(B.sem_pe.h, pe_tok)
            if stt:
                v.wait_ge(B.sem_x.h, stt)
            v.tensor_scalar_mul(stg[sl], B.PS[:, bank, :],
                                            B.gate_sb[:, 0:1])
            B.sem_v.dinc(v)
        B.bank_done([bank], B.sem_v)
        ev = B.sem_v.n

        @B.sec("sync")
        def _(s, oc=oc, b=b, sl=sl, ev=ev):
            s.wait_ge(B.sem_v.h, ev)
            B.sem_x.inc(s.dma_start(
                out=B.attn0_pk.rearrange("(c p) t -> p c t", p=128)[
                    :, oc, b * 512:(b + 1) * 512],
                in_=stg[sl]), 16)
        stage_tok[sl] = B.sem_x.n

    mm_proj(B, B.wo_t, 8, writer,
            lambda kc, b: attn[:, kc, b * 512:(b + 1) * 512],
            extra_wait=aw, wname="wo")
    B.attn_park_tok = B.sem_x.n


def p4_ffn0(B):
    PS = B.PS
    pe_entry = B.sem_pe.n
    hblk = B.bba(0, 16384, [128, 32, 512])
    gblk = B.bbb(0, 16384, [128, 32, 512])
    xnb = B.bbc(0, 4096, [128, 8, 512])
    attst = [B.bbc(4096 + i * 512, 4096 + (i + 1) * 512) for i in range(2)]
    silu = [B.fs(i * 512, (i + 1) * 512) for i in range(2)]
    accst = [B.fs(1024 + i * 512, 1024 + (i + 1) * 512) for i in range(2)]
    outst = [B.fs(2048 + i * 512, 2048 + (i + 1) * 512) for i in range(2)]
    w12s = [B.bbc(5120 + i * 1024, 5120 + (i + 1) * 1024, [128, 8, 128])
            for i in range(4)]
    h1s = [B.bbb(16384 + i * 4096, 16384 + (i + 1) * 4096, [128, 32, 128])
           for i in range(2)]

    w12_load = [0] * 4
    w12_use = [pe_entry] * 4
    h1_load = [0, 0]
    h1_use = [pe_entry, pe_entry]
    stage_tok = [0, 0]
    att_tok = [0, 0]
    xnb_prev_pe = pe_entry
    for bt in range(4):
        @B.sec("sync")
        def _(s, bt=bt, pp=xnb_prev_pe):
            s.wait_ge(B.sem_x.h, B.park_tok)
            s.wait_ge(B.sem_pe.h, pp)
            B.sem_x.inc(s.dma_start(
                out=xnb[:],
                in_=B.xn0_pk.rearrange("(c p) t -> p c t", p=128)[
                    :, :, bt * 512:(bt + 1) * 512]), 16)
        xnb_tok = B.sem_x.n

        for oc in range(32):
            for wi, wt in ((0, B.w1_t), (1, B.w2_t)):
                sl = wi * 2 + oc % 2

                @B.sec("gpsimd")
                def _(gp, oc=oc, sl=sl, wt=wt, u=w12_use[sl]):
                    if u:
                        gp.wait_ge(B.sem_pe.h, u)
                    B.sem_g.inc(gp.dma_start(out=w12s[sl][:],
                                             in_=wt[:, oc, :, :]), 16)
                w12_load[sl] = B.sem_g.n
            bank1 = (2 * oc) % 4
            bank2 = bank1 + 1
            sl1, sl2 = oc % 2, 2 + oc % 2

            @B.sec("tensor")
            def _(t, bank1=bank1, bank2=bank2, sl1=sl1, sl2=sl2,
                  xt=xnb_tok, l1=w12_load[sl1], l2=w12_load[sl2]):
                t.wait_ge(B.sem_g.h, max(l1, l2))
                t.wait_ge(B.sem_x.h, xt)
                B.pe_guard(t, [bank1, bank2])
                for kc in range(NFC):
                    t.matmul(PS[:, bank1, :], w12s[sl1][:, kc, :], xnb[:, kc, :],
                             start=(kc == 0), stop=(kc == NFC - 1))
                ins = None
                for kc in range(NFC):
                    ins = t.matmul(PS[:, bank2, :], w12s[sl2][:, kc, :],
                                   xnb[:, kc, :],
                                   start=(kc == 0), stop=(kc == NFC - 1))
                B.sem_pe.inc(ins)
            pe_tok = B.sem_pe.n
            w12_use[sl1] = pe_tok
            w12_use[sl2] = pe_tok
            ssl = oc % 2

            @B.sec("scalar")
            def _(sc, oc=oc, bank1=bank1, ssl=ssl, tok=pe_tok):
                sc.wait_ge(B.sem_pe.h, tok)
                sc.activation(out=silu[ssl], in_=PS[:, bank1, :],
                                          func=AF.Silu,
                                          scale=B.s1_sb[:, oc:oc + 1])
                B.sem_s.dinc(sc)
            si_tok = B.sem_s.n

            @B.sec("vector")
            def _(v, oc=oc, bank2=bank2, ssl=ssl, tok=si_tok):
                v.wait_ge(B.sem_s.h, tok)
                v.scalar_tensor_tensor(
                    hblk[:, oc, :], PS[:, bank2, :], B.s2_sb[:, oc:oc + 1],
                    silu[ssl], ALU.mult, ALU.mult)
                B.sem_v.dinc(v)
            B.bank_done([bank1], B.sem_s)
            B.bank_done([bank2], B.sem_v)
        h_tok = B.sem_v.n

        def big_mm(wt, src, dst_writer, n_oc, in_tok):
            for oc in range(n_oc):
                sl = oc % 2

                @B.sec("gpsimd")
                def _(gp, oc=oc, sl=sl, wt=wt, u=h1_use[sl]):
                    gp.wait_ge(B.sem_pe.h, u)
                    B.sem_g.inc(gp.dma_start(out=h1s[sl][:],
                                             in_=wt[:, oc, :, :]), 16)
                h1_load[sl] = B.sem_g.n
                bank = oc % 4

                @B.sec("tensor")
                def _(t, oc=oc, bank=bank, sl=sl, ld=h1_load[sl]):
                    t.wait_ge(B.sem_g.h, ld)
                    t.wait_ge(B.sem_v.h, in_tok)
                    B.pe_guard(t, [bank])
                    ins = None
                    for kc in range(32):
                        ins = t.matmul(PS[:, bank, :], h1s[sl][:, kc, :],
                                       src[:, kc, :],
                                       start=(kc == 0), stop=(kc == 31))
                    B.sem_pe.inc(ins)
                h1_use[sl] = B.sem_pe.n
                dst_writer(oc, bank, B.sem_pe.n)

        def gelu_writer(dst, ssb):
            def w(oc, bank, pe_tok):
                @B.sec("scalar")
                def _(sc, oc=oc, bank=bank, tok=pe_tok):
                    sc.wait_ge(B.sem_pe.h, tok)
                    sc.activation(
                        out=dst[:, oc, :], in_=PS[:, bank, :],
                        func=AF.Gelu, scale=ssb[:, oc:oc + 1])
                    B.sem_s.dinc(sc)
                B.bank_done([bank], B.sem_s)
            return w

        big_mm(B.h1_t, hblk, gelu_writer(gblk, B.sh1_sb), 32, h_tok)
        g_tok = _v_marker(B, B.sem_s, B.sem_s.n)
        big_mm(B.h2_t, gblk, gelu_writer(hblk, B.sh2_sb), 32, g_tok)
        g2_tok = _v_marker(B, B.sem_s, B.sem_s.n)

        def w3_writer(oc, bank, pe_tok):
            sl = oc % 2

            @B.sec("sync")
            def _(s, oc=oc, sl=sl, bt=bt, stt=stage_tok[sl], at=att_tok[sl]):
                if stt:
                    s.wait_ge(B.sem_v.h, stt)
                if at:
                    s.wait_ge(B.sem_v.h, at)
                col = bt * 512
                B.sem_x.inc(s.dma_start(
                    out=accst[sl],
                    in_=B.acc0_pk.rearrange("(c p) t -> p c t", p=128)[
                        :, oc, col:col + 512]), 16)
                B.sem_x.inc(s.dma_start(
                    out=attst[sl],
                    in_=B.attn0_pk.rearrange("(c p) t -> p c t", p=128)[
                        :, oc, col:col + 512]), 16)
            rb_tok = B.sem_x.n

            @B.sec("vector")
            def _(v, oc=oc, bank=bank, sl=sl, tok=pe_tok, rb=rb_tok):
                v.wait_ge(B.sem_pe.h, tok)
                v.wait_ge(B.sem_x.h, rb)
                v.scalar_tensor_tensor(outst[sl], PS[:, bank, :],
                                       B.s3_sb[:, oc:oc + 1], attst[sl],
                                       ALU.mult, ALU.add)
                v.tensor_add(outst[sl], outst[sl], accst[sl])
                B.sem_v.dinc(v)
            B.bank_done([bank], B.sem_v)
            ev = B.sem_v.n
            stage_tok[sl] = ev
            att_tok[sl] = ev

            @B.sec("sync")
            def _(s, oc=oc, sl=sl, bt=bt, ev=ev):
                s.wait_ge(B.sem_v.h, ev)
                col = bt * 512
                B.sem_x.inc(s.dma_start(
                    out=B.x1_pk.rearrange("(c p) t -> p c t", p=128)[
                        :, oc, col:col + 512],
                    in_=outst[sl]), 16)

        big_mm(B.w3_t, hblk, w3_writer, 8, g2_tok)
        xnb_prev_pe = B.sem_pe.n
    B.x1_tok = B.sem_x.n


def collective_max(B, src_col, ncols, wait_sem, wait_tok):
    PS = B.PS

    @B.sec("tensor")
    def _(t):
        t.wait_ge(wait_sem.h, wait_tok)
        B.pe_guard(t, [6])
        t.transpose(PS[0:ncols, 6, 0:128],
                    B.red_sb[:, src_col:src_col + ncols], B.ident[:])
        B.sem_pe.dinc(t)
    tr = B.sem_pe.n

    @B.sec("vector")
    def _(v):
        v.wait_ge(B.sem_pe.h, tr)
        v.memset(B.rowst[0:4, 0:1], 0.0)
        v.tensor_reduce(B.rowst[0:ncols, 0:1],
                        PS[0:ncols, 6, 0:128], AX.X, ALU.max)
        B.sem_v.dinc(v)
    B.bank_done([6], B.sem_v)
    red = B.sem_v.n

    @B.sec("gpsimd")
    def _(gp):
        gp.wait_ge(B.sem_v.h, red)
        B.sem_g.inc(gp.dma_start(out=B.cc_in[0:1, 0:4],
                                 in_=B.rowst[0:4, 0:1]), 16)
        gp.wait_ge(B.sem_g.h, B.sem_g.n)
        B.sem_cc.inc(gp.collective_compute(
            "AllReduce", ALU.max, replica_groups=[list(range(B.n_cores))],
            ins=[B.cc_in.opt()], outs=[B.cc_out.opt()]))
    cc_tok = B.sem_cc.n

    @B.sec("sync")
    def _(s):
        s.wait_ge(B.sem_cc.h, cc_tok)
        B.sem_x.inc(s.dma_start(
            out=B.scal_sb[:, SC_CC0:SC_CC0 + ncols],
            in_=B.cc_out[0:1, 0:ncols].broadcast_to([128, ncols])), 16)
    return B.sem_x.n


def p5_ln1(B):
    ln_stats(B, B.x1_pk, layer=1, xwait=(B.sem_x, B.x1_tok))

    xn1 = B.bba(0, 16384, [128, 8, 2048])
    B.xn1 = xn1
    chunk = B.XCH[:]
    xnf = B.fs(0, 2048)
    mr_b = B.fs(2048, 4096)
    r_b = B.BSH
    B.load_bcast(r_b[:], 0)
    B.load_bcast(mr_b, 1)
    bc_tok = B.sem_x.n

    pe_now = B.sem_pe.n
    prev_v = 0
    for fc in range(NFC):
        @B.sec("sync")
        def _(s, fc=fc, pv=prev_v):
            s.wait_ge(B.sem_x.h, bc_tok)
            if pv:
                s.wait_ge(B.sem_v.h, pv)
            B.sem_x.inc(s.dma_start(
                out=chunk,
                in_=B.x1_pk.rearrange("(c p) t -> p c t", p=128)[:, fc, :]), 16)
        ld = B.sem_x.n

        @B.sec("vector")
        def _(v, fc=fc, ld=ld, pe=pe_now):
            v.wait_ge(B.sem_x.h, ld)
            if fc == 0:
                v.wait_ge(B.sem_pe.h, pe)
            v.tensor_mul(xnf, chunk, r_b[:])
            v.tensor_sub(xnf, xnf, mr_b)
            v.tensor_copy(xn1[:, fc, :], xnf)
            v.tensor_reduce(B.red_sb[:, 7:8], xnf, AX.X, ALU.max,
                            apply_absolute_value=True)
            v.drain()
            if fc == 0:
                v.tensor_copy(B.red_sb[:, 0:1], B.red_sb[:, 7:8])
            else:
                v.tensor_max(B.red_sb[:, 0:1], B.red_sb[:, 0:1],
                             B.red_sb[:, 7:8])
            B.sem_v.dinc(v)
        prev_v = B.sem_v.n
    B.xn1_tok = B.sem_v.n
    B.tap("xn1", lambda: B.xn1[:], B.sem_v, B.xn1_tok, chunks=8)

    cx = collective_max(B, 0, 1, B.sem_v, B.xn1_tok)

    @B.sec("vector")
    def _(v):
        v.wait_ge(B.sem_x.h, cx)
        v.tensor_scalar(B.scal_sb[:, SC_SX:SC_SX + 1],
                        B.scal_sb[:, SC_CC0:SC_CC0 + 1],
                        1.0 / QMAX16, 1e-12, ALU.mult, ALU.max)
        v.drain()
        v.reciprocal(B.scal_sb[:, SC_IX:SC_IX + 1],
                     B.scal_sb[:, SC_SX:SC_SX + 1])
        B.sem_v.dinc(v)
    B.sx_tok = B.sem_v.n


def p6_attn1(B):
    PS, PSrow = B.PS, B.PSrow
    xn1 = B.xn1
    k1 = B.bbb(0, 16384, [128, 8, 2048])
    v1 = B.bba(16384, 32768, [128, 16, 1024])
    xw = (B.sem_v, B.xn1_tok)

    qstg = [B.bbb(16384 + i * 512, 16384 + (i + 1) * 512) for i in range(4)]
    qstg_tok = [0, 0, 0, 0]

    def q_writer(oc, b, bank, pe_tok):
        sl = (oc * NQB + b) % 4

        @B.sec("vector")
        def _(v, oc=oc, bank=bank, sl=sl, pe_tok=pe_tok, stt=qstg_tok[sl]):
            v.wait_ge(B.sem_pe.h, pe_tok)
            if stt:
                v.wait_ge(B.sem_x.h, stt)
            v.tensor_scalar_add(qstg[sl], B.PS[:, bank, :],
                                            B.bq_sb[:, oc:oc + 1])
            B.sem_v.dinc(v)
        B.bank_done([bank], B.sem_v)
        ev = B.sem_v.n

        @B.sec("sync")
        def _(s, oc=oc, b=b, sl=sl, ev=ev):
            s.wait_ge(B.sem_v.h, ev)
            B.sem_x.inc(s.dma_start(
                out=B.q1_pk.rearrange("(c p) t -> p c t", p=128)[
                    :, oc, b * 512:(b + 1) * 512],
                in_=qstg[sl]), 16)
        qstg_tok[sl] = B.sem_x.n

    mm_proj(B, B.qkvq_t, 8, q_writer,
            lambda kc, b: xn1[:, kc, b * 512:(b + 1) * 512],
            extra_wait=xw, wname="q1")

    def k_writer(oc, b, bank, pe_tok):
        @B.sec("vector")
        def _(v, oc=oc, b=b, bank=bank, pe_tok=pe_tok):
            v.wait_ge(B.sem_pe.h, pe_tok)
            v.tensor_scalar_add(
                k1[:, oc, b * 512:(b + 1) * 512], B.PS[:, bank, :],
                B.bk_sb[:, oc:oc + 1])
            B.sem_v.dinc(v)
        B.bank_done([bank], B.sem_v)

    mm_proj(B, B.qkvk_t, 8, k_writer,
            lambda kc, b: xn1[:, kc, b * 512:(b + 1) * 512],
            extra_wait=xw, wname="k1")

    wv1 = B.bbb(20480, 24576, [128, 8, 512])
    wv1_use = B.sem_pe.n
    for half in range(2):
        @B.sec("gpsimd")
        def _(gp, half=half, u=wv1_use):
            gp.wait_ge(B.sem_pe.h, u)
            B.sem_g.inc(gp.dma_start(out=wv1[:],
                                     in_=B.qkvv_m[:, :, half * 512:(half + 1) * 512]),
                        16)
        wv_tok = B.sem_g.n

        for tt in range(NQT):
            bank = tt % 4

            @B.sec("tensor")
            def _(t, tt=tt, bank=bank, wv_tok=wv_tok):
                t.wait_ge(B.sem_g.h, wv_tok)
                B.pe_guard(t, [bank])
                ins = None
                for kc in range(NFC):
                    ins = t.matmul(PS[:, bank, :],
                                   xn1[:, kc, tt * 128:(tt + 1) * 128],
                                   wv1[:, kc, :], start=(kc == 0),
                                   stop=(kc == NFC - 1))
                B.sem_pe.inc(ins)
            pe_tok = B.sem_pe.n
            wv1_use = pe_tok

            @B.sec("vector")
            def _(v, tt=tt, bank=bank, half=half, pe_tok=pe_tok):
                v.wait_ge(B.sem_pe.h, pe_tok)
                if tt == 0 and half == 0:
                    v.wait_ge(B.sem_g.h, B.bv_tok)
                v.tensor_add(v1[:, tt, half * 512:(half + 1) * 512],
                                         PS[:, bank, :],
                                         B.bv_sb[:, half * 512:(half + 1) * 512])
                B.sem_v.dinc(v)
            B.bank_done([bank], B.sem_v)
    B.qkv1_tok = B.sem_v.n

    xqst = [B.bbc(i * 2048, (i + 1) * 2048) for i in range(2)]
    fscr = B.BSH
    xq_st_tok = [0, 0]
    prev_a = 0
    for fc in range(NFC):
        @B.sec("scalar")
        def _(sc, fc=fc, pa=prev_a):
            if fc == 0:
                sc.wait_ge(B.sem_v.h, B.sx_tok)
            else:
                sc.wait_ge(B.sem_v.h, pa)
            sc.activation(out=fscr[:], in_=xn1[:, fc, :],
                                      func=AF.Identity,
                                      scale=B.scal_sb[:, SC_IX:SC_IX + 1],
                                      bias=B.tco[:, 4:5])
            B.sem_s.dinc(sc)
        a_tok = B.sem_s.n
        sl = fc % 2

        @B.sec("vector")
        def _(v, sl=sl, a_tok=a_tok, pstt=xq_st_tok[sl]):
            v.wait_ge(B.sem_s.h, a_tok)
            if pstt:
                v.wait_ge(B.sem_x.h, pstt)
            v.tensor_scalar_sub(fscr[:], fscr[:], MAGIC)
            v.tensor_scalar_mul(xqst[sl], fscr[:],
                                            B.scal_sb[:, SC_SX:SC_SX + 1])
            B.sem_v.dinc(v)
        prev_a = B.sem_v.n
        r_tok = B.sem_v.n

        @B.sec("sync")
        def _(s, fc=fc, sl=sl, r_tok=r_tok):
            s.wait_ge(B.sem_v.h, r_tok)
            B.sem_x.inc(s.dma_start(
                out=B.xq_pk.rearrange("(c p) t -> p c t", p=128)[:, fc, :],
                in_=xqst[sl]), 16)
        xq_st_tok[sl] = B.sem_x.n
    B.xq_park_tok = B.sem_x.n

    q1s = [B.bbb(16384 + i * 2048, 16384 + (i + 1) * 2048) for i in range(2)]
    Pc = [B.bbb(20480 + i * 2048, 20480 + (i + 1) * 2048) for i in range(2)]
    PT1 = B.bbb(24576, 32768, [128, 16, 512])
    pstg = [B.bbc(4096 + i * 512, 4096 + (i + 1) * 512) for i in range(4)]
    red = B.red_sb
    attn_pe0 = B.sem_pe.n
    q1_free = [attn_pe0, attn_pe0]
    p_free = [0, 0]
    prev_pv = 0
    first = True
    park_toks = [0, 0, 0, 0]
    for h in range(16):
        hp, hc = h % 2, h // 2
        qsl = h % 2

        @B.sec("sync")
        def _(s, hp=hp, hc=hc, qsl=qsl, u=q1_free[qsl]):
            s.wait_ge(B.sem_x.h, B.xq_park_tok)
            s.wait_ge(B.sem_pe.h, u)
            B.sem_x.inc(s.dma_start(
                out=q1s[qsl][hp * 64:(hp + 1) * 64, :],
                in_=B.q1_pk.rearrange("(c p) t -> p c t", p=128)[
                    hp * 64:(hp + 1) * 64, hc, :]), 16)
        q_ld = B.sem_x.n

        for qb in range(NQB):
            tr_last = 0
            for qi in range(4):
                qt = 4 * qb + qi
                psl = qt % 2

                @B.sec("tensor")
                def _(t, qt=qt, hp=hp, hc=hc, qsl=qsl, q_ld=q_ld, first=first):
                    t.wait_ge(B.sem_x.h, q_ld)
                    if first:
                        t.wait_ge(B.sem_v.h, B.qkv1_tok)
                    B.pe_guard(t, [0, 1, 2, 3])
                    ins = None
                    for b in range(4):
                        ins = t.matmul(PS[:, b, :],
                                       q1s[qsl][hp * 64:(hp + 1) * 64,
                                                qt * 128:(qt + 1) * 128],
                                       k1[hp * 64:(hp + 1) * 64, hc,
                                          b * 512:(b + 1) * 512],
                                       start=True, stop=True)
                    B.sem_pe.inc(ins)
                s_tok = B.sem_pe.n

                @B.sec("vector")
                def _(v, tok=s_tok):
                    v.wait_ge(B.sem_pe.h, tok)
                    v.tensor_reduce(red[:, 2:3], PSrow[:, 0:2048], AX.X, ALU.max)
                    v.drain()
                    v.tensor_scalar_mul(red[:, 3:4], red[:, 2:3], -8.0)
                    B.sem_v.dinc(v)
                m_tok = B.sem_v.n

                @B.sec("scalar")
                def _(sc, psl=psl, tok=m_tok, pp=p_free[psl]):
                    sc.wait_ge(B.sem_v.h, tok)
                    if pp:
                        sc.wait_ge(B.sem_t.h, pp)
                    sc.activation(
                        out=Pc[psl][:], in_=PSrow[:, 0:2048], func=AF.Exp,
                        scale=B.tco[:, 3:4], bias=red[:, 3:4],
                        accum_out=red[:, 5:6])
                    B.sem_s.dinc(sc)
                e_tok = B.sem_s.n
                B.bank_done([0, 1, 2, 3], B.sem_s)
                first = False

                @B.sec("vector")
                def _(v, psl=psl, tok=e_tok):
                    v.wait_ge(B.sem_s.h, tok)
                    v.reciprocal(red[:, 5:6], red[:, 5:6])
                    v.drain()
                    v.tensor_scalar_mul(Pc[psl][:], Pc[psl][:], red[:, 5:6])
                    B.sem_v.dinc(v)
                n_tok = B.sem_v.n

                @B.sec("sync")
                def _(s, qi=qi, psl=psl, nt=n_tok, pv=prev_pv):
                    s.wait_ge(B.sem_v.h, nt)
                    if qi == 0 and pv:
                        s.wait_ge(B.sem_pe.h, pv)
                    B.sem_t.inc(s.dma_start_transpose(
                        PT1[:, 0:16, qi * 128:(qi + 1) * 128],
                        Pc[psl][:, 0:2048]), 16)
                p_free[psl] = B.sem_t.n
                tr_last = B.sem_t.n
                
            bank = 4 + (h * 4 + qb) % 2
            sl = (h * 4 + qb) % 4

            @B.sec("tensor")
            def _(t, h=h, bank=bank, tt=tr_last):
                t.wait_ge(B.sem_t.h, tt)
                B.pe_guard(t, [bank])
                ins = None
                for kc in range(16):
                    ins = t.matmul(PS[0:64, bank, :],
                                   v1[:, kc, h * 64:(h + 1) * 64],
                                   PT1[:, kc, :], start=(kc == 0), stop=(kc == 15))
                B.sem_pe.inc(ins)
            pv_tok = B.sem_pe.n
            prev_pv = pv_tok
            q1_free[qsl] = pv_tok

            @B.sec("vector")
            def _(v, bank=bank, sl=sl, tok=pv_tok, pk=park_toks[sl]):
                v.wait_ge(B.sem_pe.h, tok)
                if pk:
                    v.wait_ge(B.sem_x.h, pk)
                v.tensor_copy(pstg[sl][0:64, :], PS[0:64, bank, :])
                B.sem_v.dinc(v)
            B.bank_done([bank], B.sem_v)
            ev = B.sem_v.n

            @B.sec("sync")
            def _(s, hp=hp, hc=hc, qb=qb, sl=sl, ev=ev):
                s.wait_ge(B.sem_v.h, ev)
                B.sem_x.inc(s.dma_start(
                    out=B.attn1_pk.rearrange("(c p) t -> p c t", p=128)[
                        hp * 64:(hp + 1) * 64, hc, qb * 512:(qb + 1) * 512],
                    in_=pstg[sl][0:64, :]), 16)
            park_toks[sl] = B.sem_x.n
    B.attn1_park = B.sem_x.n

    av = B.bba(0, 16384, [128, 8, 2048])
    last_pe = B.sem_pe.n

    @B.sec("sync")
    def _(s, lp=last_pe):
        s.wait_ge(B.sem_x.h, B.attn1_park)
        s.wait_ge(B.sem_pe.h, lp)
        B.sem_x.inc(s.dma_start(
            out=av[:], in_=B.attn1_pk.rearrange("(c p) t -> p c t", p=128)), 16)
    av_tok = B.sem_x.n

    x1st = [B.fs(i * 512, (i + 1) * 512) for i in range(2)]
    outst = [B.fs(1024 + i * 512, 1024 + (i + 1) * 512) for i in range(2)]
    stage_tok = [0, 0]

    def wp_writer(oc, b, bank, pe_tok):
        sl = (oc * NQB + b) % 2

        @B.sec("sync")
        def _(s, oc=oc, b=b, sl=sl, stt=stage_tok[sl]):
            if stt:
                s.wait_ge(B.sem_v.h, stt)
            B.sem_x.inc(s.dma_start(
                out=x1st[sl],
                in_=B.x1_pk.rearrange("(c p) t -> p c t", p=128)[
                    :, oc, b * 512:(b + 1) * 512]), 16)
        rb_tok = B.sem_x.n

        @B.sec("vector")
        def _(v, oc=oc, bank=bank, sl=sl, tok=pe_tok, rb=rb_tok):
            v.wait_ge(B.sem_pe.h, tok)
            v.wait_ge(B.sem_x.h, rb)
            v.scalar_tensor_tensor(
                outst[sl], B.PS[:, bank, :], B.bp_sb[:, oc:oc + 1],
                x1st[sl], ALU.add, ALU.add)
            B.sem_v.dinc(v)
        B.bank_done([bank], B.sem_v)
        ev = B.sem_v.n
        stage_tok[sl] = ev

        @B.sec("sync")
        def _(s, oc=oc, b=b, sl=sl, ev=ev):
            s.wait_ge(B.sem_v.h, ev)
            B.sem_x.inc(s.dma_start(
                out=B.x1_pk.rearrange("(c p) t -> p c t", p=128)[
                    :, oc, b * 512:(b + 1) * 512],
                in_=outst[sl]), 16)

    mm_proj(B, B.wp_t, 8, wp_writer,
            lambda kc, b: av[:, kc, b * 512:(b + 1) * 512],
            extra_wait=(B.sem_x, av_tok), wname="wp")
    B.acc1_tok = B.sem_x.n


def dequant_slice(B, nat, wqn, nblk, wait_sem, wait_tok, extra=None):
    n = nblk * 32
    nat3 = nat[:, 0:n].rearrange("p (a b) -> p a b", b=32)
    wqn3 = wqn[:, 0:n].rearrange("p (a b) -> p a b", b=32)

    @B.sec("vector")
    def _(v):
        v.wait_ge(wait_sem.h, wait_tok)
        v.tensor_reduce(B.dqs[:, 0:nblk], nat3, AX.X, ALU.max,
                        apply_absolute_value=True)
        v.drain()
        v.tensor_scalar(B.dqs[:, 0:nblk], B.dqs[:, 0:nblk], 1.0 / QMAX4, 1e-12,
                        ALU.mult, ALU.max)
        v.drain()
        v.reciprocal(B.dqi[:, 0:nblk], B.dqs[:, 0:nblk])
        B.sem_v.dinc(v)
    dv = B.sem_v.n

    @B.sec("gpsimd")
    def _(gp):
        gp.wait_ge(B.sem_v.h, dv)
        if extra is not None and extra[1]:
            gp.wait_ge(extra[0].h, extra[1])
        gp.tensor_mul(nat3, nat3,
                      B.dqi[:, 0:nblk].rearrange("p (a o) -> p a o", o=1)
                      .broadcast_to([128, nblk, 32]))
        gp.tensor_scalar(nat[:, 0:n], nat[:, 0:n], MAGIC, MAGIC, ALU.add,
                         ALU.subtract)
        gp.tensor_mul(
            wqn3, nat3,
            B.dqs[:, 0:nblk].rearrange("p (a o) -> p a o", o=1)
            .broadcast_to([128, nblk, 32]))
        B.sem_g.dinc(gp)
    return B.sem_g.n


def p7_ffn1(B):
    PS = B.PS
    xq = B.bbc(0, 16384, [128, 8, 2048])
    wp_done_pe = B.sem_pe.n

    @B.sec("sync")
    def _(s):
        s.wait_ge(B.sem_pe.h, wp_done_pe)
        s.wait_ge(B.sem_x.h, B.xq_park_tok)
        B.sem_x.inc(s.dma_start(
            out=xq[:], in_=B.xq_pk.rearrange("(c p) t -> p c t", p=128)), 16)
    xq_tok = B.sem_x.n

    nat_s = [B.fs(i * 1024, (i + 1) * 1024) for i in range(2)]
    wqn_s = [B.wb(i * 1024, (i + 1) * 1024) for i in range(2)]
    wqt_s = [B.bbb(5120 + i * 1024, 5120 + (i + 1) * 1024, [128, 8, 128])
             for i in range(2)]
    stg_bf = [B.bbb(i * 512, (i + 1) * 512) for i in range(4)]

    vmark = _v_marker(B, B.sem_pe, wp_done_pe)
    nat_use = [(B.sem_v, vmark), (B.sem_v, vmark)]
    wqn_use = [(B.sem_v, vmark), (B.sem_v, vmark)]
    wqt_use = [wp_done_pe, wp_done_pe]
    stg_use = [0, 0, 0, 0]
    for wi, (wdram, pk, redcol) in enumerate(
            ((B.wup_n, B.upp_pk, 1), (B.wgate_n, B.gatep_pk, 2))):
        for oc in range(24):
            sl = oc % 2

            @B.sec("sync")
            def _(s, oc=oc, sl=sl, wdram=wdram, u=nat_use[sl]):
                s.wait_ge(u[0].h, u[1])
                B.sem_w.inc(s.dma_start(out=nat_s[sl],
                                        in_=wdram[oc * 128:(oc + 1) * 128, :]), 16)
            nat_ld = B.sem_w.n

            dq_tok = dequant_slice(B, nat_s[sl], wqn_s[sl], 32,
                                   B.sem_w, nat_ld, extra=wqn_use[sl])
            nat_use[sl] = (B.sem_g, dq_tok)

            @B.sec("sync")
            def _(s, sl=sl, dq=dq_tok, u=wqt_use[sl]):
                s.wait_ge(B.sem_g.h, dq)
                s.wait_ge(B.sem_pe.h, u)
                B.sem_t.inc(s.dma_start_transpose(
                    wqt_s[sl][:, 0:8, :], wqn_s[sl][:, 0:1024]), 16)
            tr_tok = B.sem_t.n
            wqn_use[sl] = (B.sem_t, tr_tok)

            for b in range(NQB):
                bank = (oc * NQB + b) % 4

                @B.sec("tensor")
                def _(t, b=b, bank=bank, sl=sl, tr=tr_tok, wi=wi, oc=oc):
                    t.wait_ge(B.sem_t.h, tr)
                    if oc == 0 and b == 0 and wi == 0:
                        t.wait_ge(B.sem_x.h, xq_tok)
                    B.pe_guard(t, [bank])
                    ins = None
                    for kc in range(8):
                        ins = t.matmul(PS[:, bank, :], wqt_s[sl][:, kc, :],
                                       xq[:, kc, b * 512:(b + 1) * 512],
                                       start=(kc == 0), stop=(kc == 7))
                    B.sem_pe.inc(ins)
                pe_tok = B.sem_pe.n
                wqt_use[sl] = pe_tok
                ssl = (oc * NQB + b) % 4

                @B.sec("vector")
                def _(v, bank=bank, ssl=ssl, tok=pe_tok, rc=redcol,
                      firstc=(oc == 0 and b == 0), pk_tok=stg_use[ssl]):
                    v.wait_ge(B.sem_pe.h, tok)
                    if pk_tok:
                        v.wait_ge(B.sem_x.h, pk_tok)
                    v.tensor_copy(stg_bf[ssl], PS[:, bank, :])
                    v.tensor_reduce(B.red_sb[:, 7:8], PS[:, bank, :], AX.X,
                                    ALU.max, apply_absolute_value=True)
                    v.drain()
                    if firstc:
                        v.tensor_copy(B.red_sb[:, rc:rc + 1], B.red_sb[:, 7:8])
                    else:
                        v.tensor_max(B.red_sb[:, rc:rc + 1],
                                     B.red_sb[:, rc:rc + 1], B.red_sb[:, 7:8])
                    B.sem_v.dinc(v)
                B.bank_done([bank], B.sem_v)
                ev = B.sem_v.n

                @B.sec("sync")
                def _(s, oc=oc, b=b, ssl=ssl, ev=ev, pk=pk):
                    s.wait_ge(B.sem_v.h, ev)
                    B.sem_x.inc(s.dma_start(
                        out=pk.rearrange("(c p) t -> p c t", p=128)[
                            :, oc, b * 512:(b + 1) * 512],
                        in_=stg_bf[ssl]), 16)
                stg_use[ssl] = B.sem_x.n
    upg_tok = B.sem_v.n

    c2 = collective_max(B, 1, 2, B.sem_v, upg_tok)

    @B.sec("vector")
    def _(v):
        v.wait_ge(B.sem_x.h, c2)
        sc = B.scal_sb
        v.tensor_scalar(sc[:, SC_SU:SC_SU + 1], sc[:, SC_CC0:SC_CC0 + 1],
                        1.0 / QMAX16, 1e-12, ALU.mult, ALU.max)
        v.tensor_scalar(sc[:, SC_SG:SC_SG + 1], sc[:, SC_CC0 + 1:SC_CC0 + 2],
                        1.0 / QMAX16, 1e-12, ALU.mult, ALU.max)
        v.drain()
        v.reciprocal(sc[:, SC_IU:SC_IU + 1], sc[:, SC_SU:SC_SU + 1])
        v.drain()
        v.reciprocal(sc[:, SC_IG:SC_IG + 1], sc[:, SC_SG:SC_SG + 1])
        v.tensor_scalar_mul(sc[:, SC_SGC:SC_SGC + 1],
                            sc[:, SC_SG:SC_SG + 1], 1.0 / 65536.0)
        B.sem_v.dinc(v)
    c2s_tok = B.sem_v.n

    ctA = [B.bbb(8192 + i * 2048, 8192 + (i + 1) * 2048) for i in range(2)]
    ctB = [B.bbb(12288 + i * 2048, 12288 + (i + 1) * 2048) for i in range(2)]
    ots = [B.bbb(16384 + i * 2048, 16384 + (i + 1) * 2048) for i in range(2)]
    f0w = B.fs(2048, 4096)
    f1w = B.XCH[:]

    wb_ready = _v_marker(B, B.sem_pe, B.sem_pe.n)

    def chain_pass(emit_tile):
        for oc in range(24):
            emit_tile(oc, oc % 2)

    ldB_use = [wb_ready, wb_ready]
    stB_use = [0, 0]
    f0_guard = [0]

    def tileB(oc, sl):
        @B.sec("sync")
        def _(s, oc=oc, sl=sl, u=ldB_use[sl]):
            s.wait_ge(B.sem_v.h, u)
            B.sem_w.inc(s.dma_start(
                out=ctA[sl],
                in_=B.gatep_pk.rearrange("(c p) t -> p c t", p=128)[:, oc, :]),
                16)
        ld = B.sem_w.n

        @B.sec("vector")
        def _(v, sl=sl, ld=ld, fg=f0_guard[0], first=(oc == 0)):
            v.wait_ge(B.sem_w.h, ld)
            if first:
                v.wait_ge(B.sem_v.h, c2s_tok)
            if fg:
                v.wait_ge(B.sem_s.h, fg)
            v.tensor_scalar_mul(f0w, ctA[sl], B.scal_sb[:, SC_IG:SC_IG + 1])
            v.tensor_scalar_add(f0w, f0w, MAGIC)
            v.tensor_scalar_sub(f0w, f0w, MAGIC)
            B.sem_v.dinc(v)
        r1 = B.sem_v.n
        ldB_use[sl] = r1

        @B.sec("scalar")
        def _(sc_, r1=r1):
            sc_.wait_ge(B.sem_v.h, r1)
            sc_.activation(out=f1w, in_=f0w, func=AF.Sigmoid,
                           scale=B.scal_sb[:, SC_SG:SC_SG + 1])
            B.sem_s.dinc(sc_)
        a2 = B.sem_s.n
        f0_guard[0] = a2

        @B.sec("vector")
        def _(v, sl=sl, a2=a2, u=stB_use[sl], firstc=(oc == 0)):
            v.wait_ge(B.sem_s.h, a2)
            if u:
                v.wait_ge(B.sem_x.h, u)
            v.tensor_scalar(f1w, f1w, 65536.0, MAGIC, ALU.mult, ALU.add)
            v.tensor_scalar(f1w, f1w, MAGIC, 65535.0, ALU.subtract, ALU.min)
            v.scalar_tensor_tensor(ctB[sl], f0w,
                                   B.scal_sb[:, SC_SGC:SC_SGC + 1], f1w,
                                   ALU.mult, ALU.mult)
            v.tensor_reduce(B.red_sb[:, 7:8], ctB[sl], AX.X, ALU.max,
                            apply_absolute_value=True)
            v.drain()
            if firstc:
                v.tensor_copy(B.red_sb[:, 3:4], B.red_sb[:, 7:8])
            else:
                v.tensor_max(B.red_sb[:, 3:4], B.red_sb[:, 3:4],
                             B.red_sb[:, 7:8])
            B.sem_v.dinc(v)
        ev = B.sem_v.n

        @B.sec("sync")
        def _(s, oc=oc, sl=sl, ev=ev):
            s.wait_ge(B.sem_v.h, ev)
            B.sem_x.inc(s.dma_start(
                out=B.g2p_pk.rearrange("(c p) t -> p c t", p=128)[:, oc, :],
                in_=ctB[sl]), 16)
        stB_use[sl] = B.sem_x.n

    chain_pass(tileB)
    c3 = collective_max(B, 3, 1, B.sem_v, B.sem_v.n)

    @B.sec("vector")
    def _(v):
        v.wait_ge(B.sem_x.h, c3)
        sc = B.scal_sb
        v.tensor_scalar(sc[:, SC_S2:SC_S2 + 1], sc[:, SC_CC0:SC_CC0 + 1],
                        1.0 / QMAX16, 1e-12, ALU.mult, ALU.max)
        v.drain()
        v.reciprocal(sc[:, SC_I2:SC_I2 + 1], sc[:, SC_S2:SC_S2 + 1])
        v.tensor_mul(sc[:, SC_C3:SC_C3 + 1], sc[:, SC_S2:SC_S2 + 1],
                     sc[:, SC_SU:SC_SU + 1])
        B.sem_v.dinc(v)
    c3s_tok = B.sem_v.n

    ldC_use = [B.sem_v.n, B.sem_v.n]
    stC_use = [0, 0]

    def tileC(oc, sl):
        @B.sec("sync")
        def _(s, oc=oc, sl=sl, u=ldC_use[sl]):
            s.wait_ge(B.sem_v.h, u)
            B.sem_w.inc(s.dma_start(
                out=ctA[sl],
                in_=B.upp_pk.rearrange("(c p) t -> p c t", p=128)[:, oc, :]),
                16)
            B.sem_w.inc(s.dma_start(
                out=ctB[sl],
                in_=B.g2p_pk.rearrange("(c p) t -> p c t", p=128)[:, oc, :]),
                16)
        ld = B.sem_w.n

        @B.sec("vector")
        def _(v, sl=sl, ld=ld, u=stC_use[sl], firstc=(oc == 0)):
            v.wait_ge(B.sem_w.h, ld)
            if firstc:
                v.wait_ge(B.sem_v.h, c3s_tok)
            if u:
                v.wait_ge(B.sem_x.h, u)
            v.tensor_scalar_mul(f0w, ctA[sl], B.scal_sb[:, SC_IU:SC_IU + 1])
            v.tensor_scalar_add(f0w, f0w, MAGIC)
            v.tensor_scalar_sub(f0w, f0w, MAGIC)
            v.tensor_scalar_mul(f1w, ctB[sl], B.scal_sb[:, SC_I2:SC_I2 + 1])
            v.tensor_scalar_add(f1w, f1w, MAGIC)
            v.tensor_scalar_sub(f1w, f1w, MAGIC)
            v.scalar_tensor_tensor(ctB[sl], f1w,
                                   B.scal_sb[:, SC_C3:SC_C3 + 1], f0w,
                                   ALU.mult, ALU.mult)
            v.tensor_reduce(B.red_sb[:, 7:8], ctB[sl], AX.X, ALU.max,
                            apply_absolute_value=True)
            v.drain()
            if firstc:
                v.tensor_copy(B.red_sb[:, 4:5], B.red_sb[:, 7:8])
            else:
                v.tensor_max(B.red_sb[:, 4:5], B.red_sb[:, 4:5],
                             B.red_sb[:, 7:8])
            B.sem_v.dinc(v)
        ev = B.sem_v.n
        ldC_use[sl] = ev

        @B.sec("sync")
        def _(s, oc=oc, sl=sl, ev=ev):
            s.wait_ge(B.sem_v.h, ev)
            B.sem_x.inc(s.dma_start(
                out=B.opp_pk.rearrange("(c p) t -> p c t", p=128)[:, oc, :],
                in_=ctB[sl]), 16)
        stC_use[sl] = B.sem_x.n

    chain_pass(tileC)
    c4 = collective_max(B, 4, 1, B.sem_v, B.sem_v.n)

    @B.sec("vector")
    def _(v):
        v.wait_ge(B.sem_x.h, c4)
        sc = B.scal_sb
        v.tensor_scalar(sc[:, SC_SO:SC_SO + 1], sc[:, SC_CC0:SC_CC0 + 1],
                        1.0 / QMAX16, 1e-12, ALU.mult, ALU.max)
        v.drain()
        v.reciprocal(sc[:, SC_IO:SC_IO + 1], sc[:, SC_SO:SC_SO + 1])
        B.sem_v.dinc(v)
    c4s_tok = B.sem_v.n
    B.tap("scal", lambda: B.scal_sb[0:1, :], B.sem_v, c4s_tok)
    B.tap("red", lambda: B.red_sb[:, :], B.sem_v, c4s_tok)

    ldD_use = [B.sem_v.n, B.sem_v.n]
    stD_use = [0, 0]

    def tileD(oc, sl):
        @B.sec("sync")
        def _(s, oc=oc, sl=sl, u=ldD_use[sl]):
            s.wait_ge(B.sem_v.h, u)
            B.sem_w.inc(s.dma_start(
                out=ctA[sl],
                in_=B.opp_pk.rearrange("(c p) t -> p c t", p=128)[:, oc, :]),
                16)
        ld = B.sem_w.n

        @B.sec("vector")
        def _(v, sl=sl, ld=ld, u=stD_use[sl], first=(oc == 0)):
            v.wait_ge(B.sem_w.h, ld)
            if first:
                v.wait_ge(B.sem_v.h, c4s_tok)
            if u:
                v.wait_ge(B.sem_x.h, u)
            v.tensor_scalar_mul(f0w, ctA[sl], B.scal_sb[:, SC_IO:SC_IO + 1])
            v.tensor_scalar_add(f0w, f0w, MAGIC)
            v.tensor_scalar_sub(ots[sl], f0w, MAGIC)
            B.sem_v.dinc(v)
        ev = B.sem_v.n
        ldD_use[sl] = ev

        @B.sec("sync")
        def _(s, oc=oc, sl=sl, ev=ev):
            s.wait_ge(B.sem_v.h, ev)
            B.sem_x.inc(s.dma_start(
                out=B.oq_pk.rearrange("(c p) t -> p c t", p=128)[:, oc, :],
                in_=ots[sl]), 16)
        stD_use[sl] = B.sem_x.n

    chain_pass(tileD)
    oq_tok = B.sem_x.n

    wdT = B.bba(0, 24576, [128, 8, 24, 128])
    attn_done_pe = B.sem_pe.n
    nat_use2 = [(B.sem_v, B.sem_v.n), (B.sem_v, B.sem_v.n)]
    wqn_use2 = [(B.sem_v, B.sem_v.n), (B.sem_v, B.sem_v.n)]
    first_tr = True
    for oc in range(8):
        for third in range(3):
            sl = (oc * 3 + third) % 2

            @B.sec("sync")
            def _(s, oc=oc, third=third, sl=sl, u=nat_use2[sl]):
                s.wait_ge(u[0].h, u[1])
                B.sem_w.inc(s.dma_start(
                    out=nat_s[sl],
                    in_=B.wdown_n[oc * 128:(oc + 1) * 128,
                                  third * 1024:(third + 1) * 1024]), 16)
            ld = B.sem_w.n

            dq_tok = dequant_slice(B, nat_s[sl], wqn_s[sl], 32, B.sem_w, ld,
                                   extra=wqn_use2[sl])
            nat_use2[sl] = (B.sem_g, dq_tok)

            @B.sec("sync")
            def _(s, oc=oc, third=third, sl=sl, dq=dq_tok, ft=first_tr,
                  ap=attn_done_pe):
                s.wait_ge(B.sem_g.h, dq)
                if ft:
                    s.wait_ge(B.sem_pe.h, ap)
                B.sem_t.inc(s.dma_start_transpose(
                    wdT[:, oc, third * 8:(third + 1) * 8, :],
                    wqn_s[sl][:, 0:1024]), 16)
            wqn_use2[sl] = (B.sem_t, B.sem_t.n)
            first_tr = False
    wdT_tok = B.sem_t.n

    x1st2 = [B.XCH[:, i * 512:(i + 1) * 512] for i in range(2)]
    outst2 = [B.XCH[:, 1024 + i * 512:1024 + (i + 1) * 512] for i in range(2)]
    oqst = [B.bbb(i * 512, (i + 1) * 512) for i in range(2)]
    ost_use = [0, 0]
    x1_use = [0, 0]
    ot_use = [B.sem_pe.n, B.sem_pe.n]
    for nb in range(NQB):
        for kc in range(24):
            sl = kc % 2

            @B.sec("sync")
            def _(s, nb=nb, kc=kc, sl=sl, u=ot_use[sl], oqt=oq_tok):
                s.wait_ge(B.sem_pe.h, u)
                if kc == 0 and nb == 0:
                    s.wait_ge(B.sem_x.h, oqt)
                B.sem_w.inc(s.dma_start(
                    out=oqst[sl],
                    in_=B.oq_pk.rearrange("(c p) t -> p c t", p=128)[
                        :, kc, nb * 512:(nb + 1) * 512]), 16)
            ld = B.sem_w.n

            @B.sec("tensor")
            def _(t, nb=nb, kc=kc, sl=sl, ld=ld, wt=wdT_tok):
                t.wait_ge(B.sem_w.h, ld)
                if kc == 0 and nb == 0:
                    t.wait_ge(B.sem_t.h, wt)
                if kc == 0:
                    B.pe_guard(t, list(range(8)))
                ins = None
                for oc in range(8):
                    ins = t.matmul(PS[:, oc, :], wdT[:, oc, kc, :], oqst[sl],
                                   start=(kc == 0), stop=(kc == 23))
                B.sem_pe.inc(ins)
            ot_use[sl] = B.sem_pe.n
        mm_last = B.sem_pe.n

        for oc in range(8):
            sl = oc % 2

            @B.sec("sync")
            def _(s, nb=nb, oc=oc, sl=sl, u=x1_use[sl]):
                if u:
                    s.wait_ge(B.sem_v.h, u)
                B.sem_x.inc(s.dma_start(
                    out=x1st2[sl],
                    in_=B.x1_pk.rearrange("(c p) t -> p c t", p=128)[
                        :, oc, nb * 512:(nb + 1) * 512]), 16)
            rb_tok = B.sem_x.n

            @B.sec("vector")
            def _(v, oc=oc, sl=sl, mm=mm_last, rb=rb_tok, u=ost_use[sl]):
                v.wait_ge(B.sem_pe.h, mm)
                v.wait_ge(B.sem_x.h, rb)
                if u:
                    v.wait_ge(B.sem_x.h, u)
                v.scalar_tensor_tensor(
                    outst2[sl], PS[:, oc, :], B.scal_sb[:, SC_SO:SC_SO + 1],
                    x1st2[sl], ALU.mult, ALU.add)
                B.sem_v.dinc(v)
            B.bank_done([oc], B.sem_v)
            ev = B.sem_v.n
            x1_use[sl] = ev

            @B.sec("sync")
            def _(s, oc=oc, nb=nb, sl=sl, ev=ev):
                s.wait_ge(B.sem_v.h, ev)
                B.sem_x.inc(s.dma_start(
                    out=B.out.rearrange("(c p) t -> p c t", p=128)[
                        :, oc, nb * 512:(nb + 1) * 512],
                    in_=outst2[sl]), 16)
            ost_use[sl] = B.sem_x.n

    @B.sec("sync")
    def _(s):
        s.wait_ge(B.sem_x.h, B.sem_x.n)



_CACHE = {}


def build(debug_taps=(), n_cores=NCORE, stop_after=None, dbg_bar=""):
    key = (tuple(t[0] for t in debug_taps), n_cores, stop_after, dbg_bar)
    if key in _CACHE:
        return _CACHE[key]
    B = Builder(debug_taps, n_cores)
    B.stop_after = stop_after
    B.dbg_bar = dbg_bar
    p0_consts(B)
    ln_stats(B, B.xT, layer=0)
    p1_xn_acc(B)
    p2_qkv(B)
    p2_attn(B)
    if stop_after in ("nostats", "min2"):
        pass
    elif stop_after is None:
        p3_wo_park(B)
        p4_ffn0(B)
        p5_ln1(B)
        p6_attn1(B)
        p7_ffn1(B)
    @B.sec("sync")
    def _(s):
        s.wait_ge(B.sem_x.h, B.sem_x.n)
    nc = B.finish()
    _CACHE[key] = (nc, B)
    return nc, B


def stage_inputs(inputs):
    f32 = lambda a: np.ascontiguousarray(np.asarray(a, dtype=np.float32))
    x = f32(inputs["L0_attn_x"])
    qkvT = np.transpose(f32(inputs["L1_attn_Wqkv"]), (1, 0, 2)).reshape(1024, 3072)
    bqkv = f32(inputs["L1_attn_bqkv"]).reshape(3072)
    n_idx = np.arange(512, dtype=np.float32)[None, None, :]
    p_idx = np.arange(128, dtype=np.float32)[:, None, None]
    r_idx = np.arange(4, dtype=np.float32)[None, :, None]
    import ml_dtypes
    cmask = np.where(n_idx <= 128.0 * r_idx + p_idx, 0.0, BIGNEG).astype(
        ml_dtypes.bfloat16)
    shared = {
        "wq_t": _tile_om(f32(inputs["L0_attn_Wq"])),
        "wk_t": _tile_om(f32(inputs["L0_attn_Wk"])),
        "wv_m": _tile_mv(f32(inputs["L0_attn_Wv"])),
        "wo_t": _tile_om(f32(inputs["L0_attn_Wo"])),
        "nw_p": _pcol(f32(inputs["L0_attn_norm_w"])),
        "gate_s": f32(inputs["L0_attn_gate"]).reshape(1, 1),
        "ltemp_s": f32(inputs["L0_attn_log_temperature"]).reshape(1, 1),
        "w1_t": _tile_om(f32(inputs["L0_ffn_w1_w"]).T),
        "w2_t": _tile_om(f32(inputs["L0_ffn_w2_w"]).T),
        "h1_t": _tile_om(f32(inputs["L0_ffn_h1_w"]).T),
        "h2_t": _tile_om(f32(inputs["L0_ffn_h2_w"]).T),
        "w3_t": _tile_om(f32(inputs["L0_ffn_w3_w"]).T),
        "s1_p": _pcol(f32(inputs["L0_ffn_w1_s"])),
        "s2_p": _pcol(f32(inputs["L0_ffn_w2_s"])),
        "sh1_p": _pcol(f32(inputs["L0_ffn_h1_s"])),
        "sh2_p": _pcol(f32(inputs["L0_ffn_h2_s"])),
        "s3_p": _pcol(f32(inputs["L0_ffn_w3_s"])),
        "qkvq_t": _tile_om(np.ascontiguousarray(qkvT[:, 0:1024])),
        "qkvk_t": _tile_om(np.ascontiguousarray(qkvT[:, 1024:2048])),
        "qkvv_m": _tile_mv(np.ascontiguousarray(qkvT[:, 2048:3072])),
        "bq_p": _pcol(bqkv[0:1024]),
        "bk_p": _pcol(bqkv[1024:2048]),
        "bv_row": np.ascontiguousarray(bqkv[2048:3072]),
        "wp_t": _tile_om(f32(inputs["L1_attn_Wp"]).T),
        "bp_p": _pcol(f32(inputs["L1_attn_bp"])),
        "wup_n": f32(inputs["L1_ffn_w_up"]),
        "wgate_n": f32(inputs["L1_ffn_w_gate"]),
        "wdown_n": f32(inputs["L1_ffn_w_down"]),
        "ident_in": np.eye(128, dtype=np.float32),
        "cmask_in": cmask,
    }
    xTs = [np.ascontiguousarray(x[c].T) for c in range(NCORE)]
    return shared, xTs


def kernel(**inputs):
    nc, B = build()
    shared, xTs = stage_inputs(inputs)
    in_maps = [dict(shared, xT=xTs[c]) for c in range(NCORE)]
    res = run_bass_kernel_spmd(nc, in_maps, core_ids=list(range(NCORE)))
    return np.stack([np.asarray(res.results[c]["out"]).T for c in range(NCORE)])


# version 17 (final)
# speedup vs baseline: 1.0982x; 1.0982x over previous
import numpy as np
import concourse.bass as bass
import concourse.mybir as mybir
from concourse.bass_utils import run_bass_kernel_spmd

F32 = mybir.dt.float32
BF16 = mybir.dt.bfloat16
AF = mybir.ActivationFunctionType
ALU = mybir.AluOpType
AX = mybir.AxisListType

D = 1024
T = 2048
NFC = 8
NQT = 16
NQB = 4
MAGIC = 12582912.0
BIGNEG = -1.0e8
QMAX16 = 32767.0
QMAX4 = 7.0
NCORE = 8

SUMX, SUMSQ, M_, VAR, RSTD, MR, RSQ = 0, 1, 2, 3, 4, 5, 6
QSQ0 = 8
KSQ0 = 12
RS0 = 14

SC_CC0 = 0
SC_SX, SC_IX = 4, 5
SC_SU, SC_IU = 6, 7
SC_SG, SC_IG = 8, 9
SC_SGC = 10
SC_S2, SC_I2 = 11, 12
SC_C3 = 13
SC_SO, SC_IO = 14, 15


def _tile_om(wT):
    K_, M_ = wT.shape
    return np.ascontiguousarray(
        wT.reshape(K_ // 128, 128, M_ // 128, 128).transpose(1, 2, 0, 3))


def _tile_mv(wT):
    K_, N_ = wT.shape
    return np.ascontiguousarray(wT.reshape(K_ // 128, 128, N_).transpose(1, 0, 2))


def _pcol(v):
    return np.ascontiguousarray(v.reshape(-1, 128).T)


class Sem:
    def __init__(self, nc, name):
        self.h = nc.semaphore(name).__enter__()
        self.n = 0

    def inc(self, ins, k=1):
        ins.then_inc(self.h, k)
        self.n += k
        return self.n

    def dinc(self, eng, k=1):
        eng.maybe_drain_then_inc((self.h, k))
        self.n += k
        return self.n


class Builder:
    def __init__(self, debug_taps=(), n_cores=NCORE):
        self.n_cores = n_cores
        self.debug_taps = list(debug_taps)
        self.nc = bass.Bass("TRN2", target_bir_lowering=False,
                            detect_race_conditions=False)
        self.dbg = {}
        self._declare_dram()
        self._declare_sbuf()
        for name, shape, dt in self.debug_taps:
            self.dbg[name] = self.nc.dram_tensor(
                "dbg_" + name, list(shape), dt, kind="ExternalOutput").ap()
        self._block_cm = self.nc.Block()
        self.block = self._block_cm.__enter__()
        nc = self.nc
        self.sem_w = Sem(nc, "sw")
        self.sem_x = Sem(nc, "sx")
        self.sem_t = Sem(nc, "stp")
        self.sem_pe = Sem(nc, "spe")
        self.sem_v = Sem(nc, "sv")
        self.sem_s = Sem(nc, "ssc")
        self.sem_g = Sem(nc, "sg")
        self.sem_cc = Sem(nc, "scc")
        self.bank_free = [None] * 8

    def finish(self):
        self._block_cm.__exit__(None, None, None)
        return self.nc

    def _declare_dram(self):
        nc = self.nc

        def din(name, shape, dt=F32):
            setattr(self, name, nc.dram_tensor(name, list(shape), dt,
                                               kind="ExternalInput").ap())

        din("xT", (D, T))
        din("wq_t", (128, 8, 8, 128)); din("wk_t", (128, 4, 8, 128))
        din("wv_m", (128, 8, 512)); din("wo_t", (128, 8, 8, 128))
        din("nw_p", (128, 8)); din("gate_s", (1, 1)); din("ltemp_s", (1, 1))
        din("w1_t", (128, 32, 8, 128)); din("w2_t", (128, 32, 8, 128))
        din("h1_t", (128, 32, 32, 128)); din("h2_t", (128, 32, 32, 128))
        din("w3_t", (128, 8, 32, 128))
        din("s1_p", (128, 32)); din("s2_p", (128, 32))
        din("sh1_p", (128, 32)); din("sh2_p", (128, 32)); din("s3_p", (128, 8))
        din("qkvq_t", (128, 8, 8, 128)); din("qkvk_t", (128, 8, 8, 128))
        din("qkvv_m", (128, 8, 1024))
        din("bq_p", (128, 8)); din("bk_p", (128, 8)); din("bv_row", (1024,))
        din("wp_t", (128, 8, 8, 128)); din("bp_p", (128, 8))
        din("wup_n", (3072, 1024)); din("wgate_n", (3072, 1024))
        din("wdown_n", (1024, 3072))
        din("ident_in", (128, 128)); din("cmask_in", (128, 4, 512), BF16)

        self.out = nc.dram_tensor("out", [D, T], F32, kind="ExternalOutput").ap()

        def scr(name, shape, dt=F32):
            setattr(self, name, nc.dram_tensor(name, list(shape), dt).ap())

        scr("acc0_pk", (D, T)); scr("attn0_pk", (D, T), BF16)
        scr("xn0_pk", (D, T), BF16); scr("x1_pk", (D, T))
        scr("attn1_pk", (D, T), BF16); scr("xq_pk", (D, T), BF16)
        scr("q1_pk", (D, T), BF16)
        scr("upp_pk", (3072, T), BF16); scr("gatep_pk", (3072, T), BF16)
        scr("g2p_pk", (3072, T), BF16); scr("opp_pk", (3072, T), BF16)
        scr("oq_pk", (3072, T), BF16)
        scr("row_bnc", (8, T)); scr("cc_in", (1, 4)); scr("cc_out", (1, 4))
        scr("red_bnc", (4, 128))

    def _declare_sbuf(self):
        nc = self.nc

        def sb(name, shape, dt=F32):
            setattr(self, name, nc.alloc_sbuf_tensor(name, list(shape), dt).ap())

        sb("ident", (128, 128)); sb("cmask", (128, 4, 512), BF16)
        sb("nw_sb", (128, 8)); sb("s1_sb", (128, 32)); sb("s2_sb", (128, 32))
        sb("sh1_sb", (128, 32)); sb("sh2_sb", (128, 32)); sb("s3_sb", (128, 8))
        sb("bq_sb", (128, 8)); sb("bk_sb", (128, 8)); sb("bp_sb", (128, 8))
        sb("ones_sb", (128, 1)); sb("gate_sb", (128, 1))
        sb("tco", (128, 8))
        sb("scal_sb", (128, 16))
        sb("st_sb", (128, 16, 16))
        sb("rowst", (128, 128))
        sb("red_sb", (128, 8))
        sb("ccst", (128, 2))
        sb("dqs", (128, 96)); sb("dqi", (128, 96))
        sb("bv_sb", (128, 1024), BF16)
        sb("FS", (128, 4096))
        sb("XCH", (128, 2048))
        sb("BSH", (128, 2048))
        sb("BB_A", (128, 32768), BF16)
        sb("BB_B", (128, 32768), BF16)
        sb("BB_C", (128, 16384), BF16)
        sb("WB", (128, 2048), BF16)
        self.PS = nc.alloc_psum_tensor("PS", [128, 8, 512], F32).ap()
        self.PSrow = self.PS.rearrange("p a b -> p (a b)")

    def fs(self, lo, hi, shape=None):
        ap = self.FS[:, lo:hi]
        return ap if shape is None else _resh(ap, shape)

    def bba(self, lo, hi, shape=None):
        ap = self.BB_A[:, lo:hi]
        return ap if shape is None else _resh(ap, shape)

    def bbb(self, lo, hi, shape=None):
        ap = self.BB_B[:, lo:hi]
        return ap if shape is None else _resh(ap, shape)

    def bbc(self, lo, hi, shape=None):
        ap = self.BB_C[:, lo:hi]
        return ap if shape is None else _resh(ap, shape)

    def wb(self, lo, hi, shape=None):
        ap = self.WB[:, lo:hi]
        return ap if shape is None else _resh(ap, shape)

    def sec(self, eng):
        return getattr(self.block, eng)

    def tap(self, name, src_fn, dep_sem=None, dep_tok=None, chunks=None):
        if name in self.dbg:
            ap = self.dbg[name]
            if chunks is not None:
                ap = ap.rearrange("(c p) t -> p c t", p=128)
            sx = self.sem_x

            @self.sec("sync")
            def _(s):
                if dep_sem is not None:
                    s.wait_ge(dep_sem.h, dep_tok)
                sx.inc(s.dma_start(out=ap, in_=src_fn()), 16)

    def pe_guard(self, t, banks):
        for b in banks:
            if self.bank_free[b] is not None:
                sem, cnt = self.bank_free[b]
                t.wait_ge(sem.h, cnt)
                self.bank_free[b] = None

    def bank_done(self, banks, sem):
        for b in banks:
            self.bank_free[b] = (sem, sem.n)

    def bounce_rows(self, src_cols_ap, ncols, rows_at):
        PS, ident, rowst = self.PS, self.ident, self.rowst
        src2 = src_cols_ap.rearrange("p a b -> p (a b)")

        @self.sec("tensor")
        def _(t):
            t.wait_ge(self.sem_v.h, self.sem_v.n)
            t.wait_ge(self.sem_s.h, self.sem_s.n)
            t.wait_ge(self.sem_w.h, self.consts_tok)
            self.pe_guard(t, [5])
            t.transpose(PS[0:16 * ncols, 5, 0:128], src2, ident[:])
            self.sem_pe.dinc(t)
        tr = self.sem_pe.n

        @self.sec("vector")
        def _(v):
            v.wait_ge(self.sem_pe.h, tr)
            v.tensor_copy(rowst[0:16 * ncols, :], PS[0:16 * ncols, 5, 0:128])
            self.sem_v.dinc(v)
        self.bank_done([5], self.sem_v)
        cp = self.sem_v.n

        if rows_at == 0:
            self.tap("st3", lambda: src2, self.sem_pe, tr)
            self.tap("rowst48", lambda: rowst[0:48, :], self.sem_v, cp)

        @self.sec("sync")
        def _(s):
            s.wait_ge(self.sem_v.h, cp)
            for j in range(ncols):
                self.sem_x.inc(s.dma_start(
                    out=self.row_bnc[rows_at + j:rows_at + j + 1, :]
                        .rearrange("r (a b) -> (r a) b", b=128),
                    in_=rowst[16 * j:16 * (j + 1), :]), 16)
        return self.sem_x.n

    def load_bcast(self, dst_ap, row, width=T):
        @self.sec("sync")
        def _(s):
            s.wait_ge(self.sem_x.h, self.sem_x.n)
            self.sem_x.inc(s.dma_start(
                out=dst_ap,
                in_=self.row_bnc[row:row + 1, 0:width].broadcast_to([128, width])),
                16)
        return self.sem_x.n


def _resh(ap, shape):
    if len(shape) == 2:
        return ap
    letters = "abcde"[: len(shape) - 1]
    spec = "p (" + " ".join(letters) + ") -> p " + " ".join(letters)
    kw = {l: s for l, s in zip(letters, shape[1:])}
    return ap.rearrange(spec, **kw)


def _v_marker(B, sem, tok):
    @B.sec("vector")
    def _(v):
        v.wait_ge(sem.h, tok)
        B.sem_v.dinc(v)
    return B.sem_v.n



def p0_consts(B):
    sw = B.sem_w

    @B.sec("sync")
    def _(s):
        pairs = [(B.ident, B.ident_in), (B.cmask, B.cmask_in),
                 (B.nw_sb, B.nw_p), (B.s1_sb, B.s1_p), (B.s2_sb, B.s2_p),
                 (B.sh1_sb, B.sh1_p), (B.sh2_sb, B.sh2_p), (B.s3_sb, B.s3_p),
                 (B.bq_sb, B.bq_p), (B.bk_sb, B.bk_p), (B.bp_sb, B.bp_p)]
        for o, i in pairs:
            sw.inc(s.dma_start(out=o[:], in_=i[:]), 16)
        sw.inc(s.dma_start(out=B.gate_sb[:],
                           in_=B.gate_s[0:1, :].broadcast_to([128, 1])), 16)
        sw.inc(s.dma_start(out=B.tco[:, 0:1],
                           in_=B.ltemp_s[0:1, :].broadcast_to([128, 1])), 16)
    B.consts_tok = sw.n

    @B.sec("vector")
    def _(v):
        v.wait_ge(sw.h, B.consts_tok)
        v.memset(B.ones_sb[:], 1.0)
        v.memset(B.tco[:, 3:4], 8.0)
        v.memset(B.tco[:, 4:5], MAGIC)
        B.sem_v.dinc(v)
    B.ones_tok = B.sem_v.n

    @B.sec("scalar")
    def _(sc):
        sc.wait_ge(sw.h, B.consts_tok)
        sc.activation(out=B.tco[:, 1:2], in_=B.tco[:, 0:1], func=AF.Exp)
        B.sem_s.dinc(sc)
    t_tok = B.sem_s.n

    @B.sec("vector")
    def _(v):
        v.wait_ge(B.sem_s.h, t_tok)
        v.tensor_scalar_mul(B.tco[:, 2:3], B.tco[:, 1:2], -1.0)
        B.sem_v.dinc(v)
    B.negt_tok = B.sem_v.n

    @B.sec("gpsimd")
    def _(gp):
        B.sem_g.inc(gp.dma_start(
            out=B.bv_sb[:],
            in_=B.bv_row[None, :].broadcast_to([128, 1024])), 16)
        gp.wait_ge(sw.h, B.consts_tok)
    B.bv_tok = B.sem_g.n

    @B.sec("tensor")
    def _(t):
        t.wait_ge(sw.h, B.consts_tok)


def ln_stats(B, src_pk, layer, xwait=None):
    PS, st = B.PS, B.st_sb
    xch = B.XCH
    sq = B.fs(0, 2048)
    prev_pe = 0
    for fc in range(NFC):
        @B.sec("sync")
        def _(s, fc=fc, prev=prev_pe):
            if fc == 0 and xwait is not None:
                s.wait_ge(xwait[0].h, xwait[1])
            if prev:
                s.wait_ge(B.sem_pe.h, prev)
            B.sem_x.inc(s.dma_start(
                out=xch[:],
                in_=src_pk.rearrange("(c p) t -> p c t", p=128)[:, fc, :]), 16)
        ld = B.sem_x.n

        @B.sec("scalar")
        def _(sc, fc=fc, ld=ld, prev=prev_pe):
            sc.wait_ge(B.sem_x.h, ld)
            if prev:
                sc.wait_ge(B.sem_pe.h, prev)
            sc.activation(out=sq, in_=xch[:], func=AF.Square)
            B.sem_s.dinc(sc)
        sq_tok = B.sem_s.n

        @B.sec("tensor")
        def _(t, fc=fc, sq_tok=sq_tok, ld=ld):
            t.wait_ge(B.sem_s.h, sq_tok)
            t.wait_ge(B.sem_x.h, ld)
            if fc == 0:
                t.wait_ge(B.sem_v.h, B.ones_tok)
            B.pe_guard(t, [6, 7])
            ins = None
            for qt in range(NQT):
                t.matmul(PS[:, 6, qt:qt + 1], xch[:, qt * 128:(qt + 1) * 128],
                         B.ones_sb[:], start=True, stop=True)
                ins = t.matmul(PS[:, 7, qt:qt + 1], sq[:, qt * 128:(qt + 1) * 128],
                               B.ones_sb[:], start=True, stop=True)
            B.sem_pe.inc(ins)
        prev_pe = B.sem_pe.n

        @B.sec("vector")
        def _(v, fc=fc, pe=prev_pe):
            v.wait_ge(B.sem_pe.h, pe)
            if fc == 0:
                v.tensor_copy(st[:, SUMX, :], PS[:, 6, 0:16])
                v.tensor_copy(st[:, SUMSQ, :], PS[:, 7, 0:16])
                B.sem_v.dinc(v)
            else:
                v.tensor_add(st[:, SUMX, :], st[:, SUMX, :], PS[:, 6, 0:16])
                v.tensor_add(st[:, SUMSQ, :], st[:, SUMSQ, :],
                                         PS[:, 7, 0:16])
                B.sem_v.dinc(v)
        B.bank_done([6, 7], B.sem_v)

    @B.sec("vector")
    def _(v):
        v.tensor_scalar_mul(st[:, M_, :], st[:, SUMX, :], 1.0 / D)
        v.tensor_scalar_mul(st[:, VAR, :], st[:, SUMSQ, :], 1.0 / D)
        v.drain()
        v.tensor_mul(st[:, RSTD, :], st[:, M_, :], st[:, M_, :])
        v.drain()
        v.tensor_sub(st[:, VAR, :], st[:, VAR, :], st[:, RSTD, :])
        v.drain()
        v.tensor_scalar_add(st[:, RSTD, :], st[:, VAR, :], 1e-5)
        v.drain()
        v.reciprocal(st[:, MR, :], st[:, RSTD, :])
        B.sem_v.dinc(v)
    inv_tok = B.sem_v.n

    @B.sec("scalar")
    def _(sc):
        sc.wait_ge(B.sem_v.h, inv_tok)
        sc.activation(out=st[:, RSTD, :], in_=st[:, MR, :], func=AF.Sqrt)
        B.sem_s.dinc(sc)
    rstd_tok = B.sem_s.n

    if layer == 0:
        @B.sec("vector")
        def _(v):
            v.wait_ge(B.sem_s.h, rstd_tok)
            v.tensor_mul(st[:, RSQ, :], st[:, VAR, :], st[:, MR, :])
            v.drain()
            v.tensor_scalar_add(st[:, RSQ, :], st[:, RSQ, :], 1e-6)
            v.drain()
            v.reciprocal(st[:, RSQ, :], st[:, RSQ, :])
            B.sem_v.dinc(v)
        t3 = B.sem_v.n

        @B.sec("scalar")
        def _(sc):
            sc.wait_ge(B.sem_v.h, t3)
            sc.activation(out=st[:, RSQ, :], in_=st[:, RSQ, :], func=AF.Sqrt)
            B.sem_s.dinc(sc)

    @B.sec("vector")
    def _(v):
        v.wait_ge(B.sem_s.h, B.sem_s.n)
        v.tensor_mul(st[:, MR, :], st[:, M_, :], st[:, RSTD, :])
        B.sem_v.dinc(v)

    if layer == 0:
        B.tap("st8", lambda: st[:, 0:8, :].rearrange("p a b -> p (a b)"),
              B.sem_v, B.sem_v.n)
    ncols = 3 if layer == 0 else 2
    B.bounce_rows(st[:, RSTD:RSTD + ncols, :], ncols, 0)


def p1_xn_acc(B):
    chunk = B.XCH[:]
    xnf = B.fs(0, 2048)
    mr_b = B.fs(2048, 4096)
    r_b = B.BSH
    hh = B.bbc(0, 16384, [128, 8, 2048])
    B.h0 = hh
    xnst = [B.bbb(i * 2048, (i + 1) * 2048) for i in range(2)]

    B.load_bcast(r_b[:], 0)
    B.load_bcast(mr_b, 1)
    bc_tok = B.sem_x.n

    xn_st_tok = [0, 0]
    prev_v = 0
    for fc in range(NFC):
        @B.sec("sync")
        def _(s, fc=fc, pv=prev_v):
            s.wait_ge(B.sem_x.h, bc_tok)
            if pv:
                s.wait_ge(B.sem_v.h, pv)
            B.sem_x.inc(s.dma_start(
                out=chunk,
                in_=B.xT.rearrange("(c p) t -> p c t", p=128)[:, fc, :]), 16)
        ld = B.sem_x.n

        sl = fc % 2

        @B.sec("vector")
        def _(v, fc=fc, ld=ld, sl=sl, stt=xn_st_tok[sl]):
            v.wait_ge(B.sem_x.h, ld)
            if stt:
                v.wait_ge(B.sem_x.h, stt)
            v.tensor_mul(xnf, chunk, r_b[:])
            v.tensor_sub(xnf, xnf, mr_b)
            v.tensor_copy(xnst[sl], xnf)
            v.tensor_scalar_mul(hh[:, fc, :], xnf, B.nw_sb[:, fc:fc + 1])
            v.tensor_add(chunk, chunk, xnf)
            B.sem_v.dinc(v)
        prev_v = B.sem_v.n

        @B.sec("sync")
        def _(s, fc=fc, sl=sl, pv=prev_v):
            s.wait_ge(B.sem_v.h, pv)
            B.sem_x.inc(s.dma_start(
                out=B.xn0_pk.rearrange("(c p) t -> p c t", p=128)[:, fc, :],
                in_=xnst[sl]), 16)
            B.sem_x.inc(s.dma_start(
                out=B.acc0_pk.rearrange("(c p) t -> p c t", p=128)[:, fc, :],
                in_=chunk), 16)
        xn_st_tok[sl] = B.sem_x.n
    B.park_tok = B.sem_x.n
    B.h_ready_tok = B.sem_v.n
    B.tap("h0", lambda: B.h0[:], B.sem_v, B.h_ready_tok, chunks=8)


def mm_proj(B, w_dram, n_oc, out_writer, moving_fn, extra_wait=None,
            n_kc=NFC, nb=NQB, wname="w"):
    pe_start = B.sem_pe.n
    assert n_kc * 128 * 2 <= 2048
    wslots = [B.wb(i * n_kc * 128, (i + 1) * n_kc * 128, [128, n_kc, 128])
              for i in range(2)]
    slot_load = [0, 0]
    slot_use = [pe_start, pe_start]
    for oc in range(n_oc):
        sl = oc % 2

        @B.sec("gpsimd")
        def _(gp, oc=oc, sl=sl, u=slot_use[sl]):
            if u:
                gp.wait_ge(B.sem_pe.h, u)
            B.sem_g.inc(gp.dma_start(out=wslots[sl][:], in_=w_dram[:, oc, :, :]), 16)
        slot_load[sl] = B.sem_g.n

        for b in range(nb):
            bank = (oc * nb + b) % 4

            @B.sec("tensor")
            def _(t, oc=oc, b=b, bank=bank, sl=sl, ld=slot_load[sl]):
                t.wait_ge(B.sem_g.h, ld)
                if extra_wait is not None:
                    t.wait_ge(extra_wait[0].h, extra_wait[1])
                B.pe_guard(t, [bank])
                ins = None
                for kc in range(n_kc):
                    ins = t.matmul(B.PS[:, bank, :], wslots[sl][:, kc, :],
                                   moving_fn(kc, b),
                                   start=(kc == 0), stop=(kc == n_kc - 1))
                B.sem_pe.inc(ins)
            slot_use[sl] = B.sem_pe.n
            out_writer(oc, b, bank, B.sem_pe.n)


def p2_qkv(B):
    PS, st = B.PS, B.st_sb
    hh = B.h0
    q_bf = B.bba(0, 16384, [128, 8, 2048])
    k_bf = B.bba(16384, 24576, [128, 4, 2048])
    v_bf = B.bba(24576, 32768, [128, 16, 512])
    B.q0, B.k0, B.v0 = q_bf, k_bf, v_bf
    h_ready = (B.sem_v, B.h_ready_tok)

    rsq_b = B.BSH
    B.load_bcast(rsq_b[:], 2)
    rsq_tok = B.sem_x.n
    B.tap("rsqb", lambda: rsq_b[:], B.sem_x, rsq_tok)

    def evac_rsq(dst):
        def w(oc, b, bank, pe_tok):
            @B.sec("vector")
            def _(v, oc=oc, b=b, bank=bank, pe_tok=pe_tok):
                v.wait_ge(B.sem_pe.h, pe_tok)
                v.wait_ge(B.sem_x.h, rsq_tok)
                v.tensor_mul(dst(oc, b), PS[:, bank, :],
                                         rsq_b[:, b * 512:(b + 1) * 512])
                B.sem_v.dinc(v)
            B.bank_done([bank], B.sem_v)
        return w

    mm_proj(B, B.wq_t, 8,
            evac_rsq(lambda oc, b: q_bf[:, oc, b * 512:(b + 1) * 512]),
            lambda kc, b: hh[:, kc, b * 512:(b + 1) * 512],
            extra_wait=h_ready, wname="wq")
    mm_proj(B, B.wk_t, 4,
            evac_rsq(lambda oc, b: k_bf[:, oc, b * 512:(b + 1) * 512]),
            lambda kc, b: hh[:, kc, b * 512:(b + 1) * 512],
            extra_wait=h_ready, wname="wk")

    if getattr(B, "stop_after", None) == "min2":
        B.trq_tok = B.sem_v.n
        B.khat_tok = B.sem_v.n
        return
    wv_sb = B.bbb(14336, 18432, [128, 8, 512])
    wv_use = B.sem_pe.n

    @B.sec("gpsimd")
    def _(gp, u=wv_use):
        gp.wait_ge(B.sem_pe.h, u)
        B.sem_g.inc(gp.dma_start(out=wv_sb[:], in_=B.wv_m[:]), 16)
    wv_tok = B.sem_g.n

    for tt in range(NQT):
        bank = tt % 4

        @B.sec("tensor")
        def _(t, tt=tt, bank=bank):
            t.wait_ge(B.sem_g.h, wv_tok)
            B.pe_guard(t, [bank])
            ins = None
            for kc in range(NFC):
                ins = t.matmul(PS[:, bank, :], hh[:, kc, tt * 128:(tt + 1) * 128],
                               wv_sb[:, kc, :], start=(kc == 0), stop=(kc == NFC - 1))
            B.sem_pe.inc(ins)
        pe_tok = B.sem_pe.n

        @B.sec("vector")
        def _(v, tt=tt, bank=bank, pe_tok=pe_tok):
            v.wait_ge(B.sem_pe.h, pe_tok)
            v.tensor_scalar_mul(v_bf[:, tt, :], PS[:, bank, :],
                                            B.st_sb[:, RSQ, tt:tt + 1])
            B.sem_v.dinc(v)
        B.bank_done([bank], B.sem_v)
    qkv_tok = B.sem_v.n

    sq2 = B.BSH
    prev_stat_pe = 0
    for (which, n_oc, col0) in (("q", 8, QSQ0), ("k", 4, KSQ0)):
        src = q_bf if which == "q" else k_bf
        for oc in range(n_oc):
            @B.sec("scalar")
            def _(sc, oc=oc, src=src, prev=prev_stat_pe):
                if prev == 0:
                    sc.wait_ge(B.sem_v.h, qkv_tok)
                else:
                    sc.wait_ge(B.sem_pe.h, prev)
                sc.activation(out=sq2[:], in_=src[:, oc, :],
                                          func=AF.Square)
                B.sem_s.dinc(sc)
            sq_tok = B.sem_s.n

            @B.sec("tensor")
            def _(t, sq_tok=sq_tok):
                t.wait_ge(B.sem_s.h, sq_tok)
                B.pe_guard(t, [6])
                ins = None
                for qt in range(NQT):
                    ins = t.matmul(PS[:, 6, qt:qt + 1],
                                   sq2[:, qt * 128:(qt + 1) * 128], B.ones_sb[:],
                                   start=True, stop=True)
                B.sem_pe.inc(ins)
            prev_stat_pe = B.sem_pe.n

            @B.sec("vector")
            def _(v, oc=oc, col0=col0, pe_tok=prev_stat_pe):
                v.wait_ge(B.sem_pe.h, pe_tok)
                col = col0 + oc // 2
                if oc % 2 == 0:
                    v.tensor_copy(st[:, col, :], PS[:, 6, 0:16])
                    B.sem_v.dinc(v)
                else:
                    v.tensor_add(st[:, col, :], st[:, col, :],
                                             PS[:, 6, 0:16])
                    B.sem_v.dinc(v)
            B.bank_done([6], B.sem_v)
    nsq_tok = B.sem_v.n

    @B.sec("scalar")
    def _(sc):
        sc.wait_ge(B.sem_v.h, nsq_tok)
        a = st[:, QSQ0:QSQ0 + 6, :].rearrange("p a b -> p (a b)")
        sc.activation(out=a, in_=a, func=AF.Sqrt)
        B.sem_s.dinc(sc)
    sq_tok = B.sem_s.n

    @B.sec("vector")
    def _(v):
        v.wait_ge(B.sem_s.h, sq_tok)
        v.wait_ge(B.sem_v.h, B.negt_tok)
        a = st[:, QSQ0:QSQ0 + 6, :].rearrange("p a b -> p (a b)")
        v.tensor_scalar_max(a, a, 1e-12)
        v.drain()
        v.reciprocal(a, a)
        v.drain()
        qa = st[:, QSQ0:QSQ0 + 4, :].rearrange("p a b -> p (a b)")
        v.tensor_scalar_mul(qa, qa, B.tco[:, 1:2])
        B.sem_v.dinc(v)
    B.trq_tok = B.sem_v.n

    B.bounce_rows(st[:, KSQ0:KSQ0 + 2, :], 2, 3)
    for kv in range(2):
        rk_b = B.BSH
        B.load_bcast(rk_b[:], 3 + kv)
        bt = B.sem_x.n

        @B.sec("vector")
        def _(v, kv=kv, bt=bt):
            v.wait_ge(B.sem_x.h, bt)
            for j in range(2):
                v.tensor_mul(k_bf[:, 2 * kv + j, :],
                                         k_bf[:, 2 * kv + j, :], rk_b[:])
                B.sem_v.dinc(v)
    B.khat_tok = B.sem_v.n
    B.tap("q0", lambda: B.q0[:], B.sem_v, qkv_tok, chunks=8)
    B.tap("k0", lambda: B.k0[:], B.sem_v, B.khat_tok, chunks=4)
    B.tap("v0", lambda: B.v0[:], B.sem_v, qkv_tok, chunks=16)


def p2_attn(B):
    PS, PSrow, st = B.PS, B.PSrow, B.st_sb
    q_bf, k_bf, v_bf = B.q0, B.k0, B.v0
    Pt = [B.bbb(i * 2048, (i + 1) * 2048) for i in range(2)]
    PT = B.bbb(4096, 12288, [128, 16, 512])
    attn = B.bbc(0, 16384, [128, 8, 2048])
    B.attn0 = attn

    p_free = [0, 0]
    prev_pv = 0
    first = True
    only1 = getattr(B, "stop_after", None) in ("s00", "nostats", "min2")
    for h in range(1 if only1 else 4):
        kv = h // 2
        for qb in range(1 if only1 else NQB):
            nkc = 4 * qb + 4
            tr_last = 0
            for qi in range(1 if only1 else 4):
                qt = 4 * qb + qi
                nb = qt // 4 + 1
                psl = qt % 2
                if nb == 1:
                    off = qi % 4
                elif nb == 2:
                    off = (qi % 2) * 2
                else:
                    off = 0

                @B.sec("tensor")
                def _(t, qt=qt, nb=nb, off=off, h=h, kv=kv, first=first,
                      bar=(B.sem_x.n, B.sem_w.n, B.sem_t.n,
                           B.sem_s.n, B.sem_g.n)):
                    if first:
                        t.wait_ge(B.sem_v.h, B.khat_tok)
                        dbgbar = getattr(B, "dbg_bar", "")
                        for nm, sem, cnt in (("x", B.sem_x, bar[0]),
                                             ("w", B.sem_w, bar[1]),
                                             ("t", B.sem_t, bar[2]),
                                             ("s", B.sem_s, bar[3]),
                                             ("g", B.sem_g, bar[4])):
                            if nm in dbgbar:
                                t.wait_ge(sem.h, cnt)
                    B.pe_guard(t, list(range(off, off + nb)))
                    ins = None
                    for b in range(nb):
                        for j in range(2):
                            ins = t.matmul(PS[:, off + b, :],
                                           q_bf[:, 2 * h + j, qt * 128:(qt + 1) * 128],
                                           k_bf[:, 2 * kv + j, b * 512:(b + 1) * 512],
                                           start=(j == 0), stop=(j == 1))
                    B.sem_pe.inc(ins)
                s_tok = B.sem_pe.n

                if h == 0 and qb == 0 and qi == 0 and "S00pre" in B.dbg:
                    sstg2 = B.fs(3584, 4096)

                    @B.sec("vector")
                    def _(v, tok=s_tok):
                        v.wait_ge(B.sem_pe.h, tok)
                        v.tensor_copy(sstg2, PS[:, 0, :])
                        B.sem_v.dinc(v)
                    B.tap("S00pre", lambda: sstg2, B.sem_v, B.sem_v.n)

                @B.sec("vector")
                def _(v, qt=qt, nb=nb, off=off, tok=s_tok):
                    v.wait_ge(B.sem_pe.h, tok)
                    v.tensor_add(PS[:, off + nb - 1, :], PS[:, off + nb - 1, :],
                                 B.cmask[:, qt % 4, :])
                    B.sem_v.dinc(v)
                m_tok = B.sem_v.n
                if h == 0 and qb == 0 and qi == 0 and "S00" in B.dbg:
                    sstg = B.fs(3072, 3584)

                    @B.sec("vector")
                    def _(v):
                        v.tensor_copy(sstg, PS[:, 0, :])
                        B.sem_v.dinc(v)
                    m_tok = B.sem_v.n
                    B.tap("S00", lambda: sstg, B.sem_v, m_tok)

                @B.sec("scalar")
                def _(sc, qt=qt, qi=qi, psl=psl, nb=nb, off=off, h=h,
                      tok=m_tok, pp=p_free[psl]):
                    sc.wait_ge(B.sem_v.h, tok)
                    if pp:
                        sc.wait_ge(B.sem_t.h, pp)
                    sc.activation(
                        out=Pt[psl][:, 0:nb * 512],
                        in_=PSrow[:, off * 512:(off + nb) * 512],
                        func=AF.Exp, scale=st[:, QSQ0 + h, qt:qt + 1],
                        bias=B.tco[:, 2:3],
                        accum_out=st[:, RS0, qi:qi + 1])
                    B.sem_s.dinc(sc)
                e_tok = B.sem_s.n
                B.bank_done(list(range(off, off + nb)), B.sem_s)
                first = False

                @B.sec("vector")
                def _(v, qi=qi, psl=psl, nb=nb, tok=e_tok):
                    v.wait_ge(B.sem_s.h, tok)
                    v.reciprocal(st[:, RS0, qi:qi + 1], st[:, RS0, qi:qi + 1])
                    v.drain()
                    v.tensor_scalar_mul(Pt[psl][:, 0:nb * 512],
                                        Pt[psl][:, 0:nb * 512],
                                        st[:, RS0, qi:qi + 1])
                    B.sem_v.dinc(v)
                n_tok = B.sem_v.n

                @B.sec("vector")
                def _(v, qt=qt, qi=qi, nkc=nkc, pv=prev_pv):
                    if qi == 0 and pv:
                        v.wait_ge(B.sem_pe.h, pv)
                    for kc in range(qt + 1, nkc):
                        v.memset(PT[:, kc, qi * 128:(qi + 1) * 128], 0.0)
                    B.sem_v.dinc(v)
                z_tok = B.sem_v.n

                @B.sec("sync")
                def _(s, qt=qt, qi=qi, psl=psl, zt=z_tok, nt=n_tok):
                    s.wait_ge(B.sem_v.h, zt)
                    s.wait_ge(B.sem_v.h, nt)
                    B.sem_t.inc(s.dma_start_transpose(
                        PT[:, 0:qt + 1, qi * 128:(qi + 1) * 128],
                        Pt[psl][:, 0:(qt + 1) * 128]), 16)
                p_free[psl] = B.sem_t.n
                tr_last = B.sem_t.n
                if h == 0 and qb == 0 and qi == 0:
                    B.tap("P00e", lambda: Pt[0][:, 0:512], B.sem_s, e_tok)
                    B.tap("rs00", lambda: B.st_sb[:, RS0, :], B.sem_s, e_tok)
                    B.tap("trq2", lambda: B.st_sb[:, QSQ0:QSQ0 + 6, :]
                          .rearrange("p a b -> p (a b)"), B.sem_v, B.trq_tok)
                    B.tap("P00", lambda: Pt[0][:, 0:512], B.sem_v, n_tok)
                    B.tap("PT00", lambda: PT[:, 0, 0:128], B.sem_t, tr_last)
                    B.tap("rs0", lambda: B.st_sb[:, RS0, :], B.sem_s, e_tok)
                    B.tap("trq", lambda: B.st_sb[:, QSQ0:QSQ0 + 6, :]
                          .rearrange("p a b -> p (a b)"), B.sem_v, B.trq_tok)
                    B.tap("tco", lambda: B.tco[:, :], B.sem_v, B.negt_tok)
                    B.tap("cmask", lambda: B.cmask.rearrange("p a b -> p (a b)"),
                          B.sem_w, B.consts_tok)

            for vc in range(0 if only1 else 2):
                bank = 4 + (2 * qb + vc) % 2

                @B.sec("tensor")
                def _(t, h=h, qb=qb, vc=vc, bank=bank, nkc=nkc, tt=tr_last):
                    t.wait_ge(B.sem_t.h, tt)
                    B.pe_guard(t, [bank])
                    ins = None
                    for kc in range(nkc):
                        ins = t.matmul(PS[:, bank, :],
                                       v_bf[:, kc, vc * 128:(vc + 1) * 128],
                                       PT[:, kc, :],
                                       start=(kc == 0), stop=(kc == nkc - 1))
                    B.sem_pe.inc(ins)
                pv_tok = B.sem_pe.n
                prev_pv = pv_tok

                @B.sec("vector")
                def _(v, h=h, qb=qb, vc=vc, bank=bank, tok=pv_tok):
                    v.wait_ge(B.sem_pe.h, tok)
                    v.tensor_copy(
                        attn[:, 2 * h + vc, qb * 512:(qb + 1) * 512], PS[:, bank, :])
                    B.sem_v.dinc(v)
                B.bank_done([bank], B.sem_v)
    B.attn_tok = B.sem_v.n
    B.tap("attn0", lambda: B.attn0[:], B.sem_v, B.attn_tok, chunks=8)
    B.tap("k0b", lambda: B.k0[:], B.sem_v, B.attn_tok, chunks=4)
    B.tap("q0b", lambda: B.q0[:], B.sem_v, B.attn_tok, chunks=8)


def p3_wo_park(B):
    attn = B.attn0
    stg = [B.bbb(12288 + i * 512, 12288 + (i + 1) * 512) for i in range(4)]
    aw = (B.sem_v, B.attn_tok)
    stage_tok = [0, 0, 0, 0]

    def writer(oc, b, bank, pe_tok):
        sl = (oc * NQB + b) % 4

        @B.sec("vector")
        def _(v, bank=bank, pe_tok=pe_tok, sl=sl, stt=stage_tok[sl]):
            v.wait_ge(B.sem_pe.h, pe_tok)
            if stt:
                v.wait_ge(B.sem_x.h, stt)
            v.tensor_scalar_mul(stg[sl], B.PS[:, bank, :],
                                            B.gate_sb[:, 0:1])
            B.sem_v.dinc(v)
        B.bank_done([bank], B.sem_v)
        ev = B.sem_v.n

        @B.sec("sync")
        def _(s, oc=oc, b=b, sl=sl, ev=ev):
            s.wait_ge(B.sem_v.h, ev)
            B.sem_x.inc(s.dma_start(
                out=B.attn0_pk.rearrange("(c p) t -> p c t", p=128)[
                    :, oc, b * 512:(b + 1) * 512],
                in_=stg[sl]), 16)
        stage_tok[sl] = B.sem_x.n

    mm_proj(B, B.wo_t, 8, writer,
            lambda kc, b: attn[:, kc, b * 512:(b + 1) * 512],
            extra_wait=aw, wname="wo")
    B.attn_park_tok = B.sem_x.n


def p4_ffn0(B):
    PS = B.PS
    pe_entry = B.sem_pe.n
    hblk = B.bba(0, 16384, [128, 32, 512])
    gblk = B.bbb(0, 16384, [128, 32, 512])
    xnb = B.bbc(0, 4096, [128, 8, 512])
    attst = [B.bbc(4096 + i * 512, 4096 + (i + 1) * 512) for i in range(2)]
    silu = [B.fs(i * 512, (i + 1) * 512) for i in range(2)]
    accst = [B.fs(1024 + i * 512, 1024 + (i + 1) * 512) for i in range(2)]
    outst = [B.fs(2048 + i * 512, 2048 + (i + 1) * 512) for i in range(2)]
    w12s = [B.bbc(5120 + i * 1024, 5120 + (i + 1) * 1024, [128, 8, 128])
            for i in range(4)]
    h1s = [B.bbb(16384 + i * 4096, 16384 + (i + 1) * 4096, [128, 32, 128])
           for i in range(2)]

    w12_load = [0] * 4
    w12_use = [pe_entry] * 4
    h1_load = [0, 0]
    h1_use = [pe_entry, pe_entry]
    stage_tok = [0, 0]
    att_tok = [0, 0]
    xnb_prev_pe = pe_entry
    for bt in range(4):
        @B.sec("sync")
        def _(s, bt=bt, pp=xnb_prev_pe):
            s.wait_ge(B.sem_x.h, B.park_tok)
            s.wait_ge(B.sem_pe.h, pp)
            B.sem_x.inc(s.dma_start(
                out=xnb[:],
                in_=B.xn0_pk.rearrange("(c p) t -> p c t", p=128)[
                    :, :, bt * 512:(bt + 1) * 512]), 16)
        xnb_tok = B.sem_x.n

        for oc in range(32):
            for wi, wt in ((0, B.w1_t), (1, B.w2_t)):
                sl = wi * 2 + oc % 2

                @B.sec("gpsimd")
                def _(gp, oc=oc, sl=sl, wt=wt, u=w12_use[sl]):
                    if u:
                        gp.wait_ge(B.sem_pe.h, u)
                    B.sem_g.inc(gp.dma_start(out=w12s[sl][:],
                                             in_=wt[:, oc, :, :]), 16)
                w12_load[sl] = B.sem_g.n
            bank1 = (2 * oc) % 4
            bank2 = bank1 + 1
            sl1, sl2 = oc % 2, 2 + oc % 2

            @B.sec("tensor")
            def _(t, bank1=bank1, bank2=bank2, sl1=sl1, sl2=sl2,
                  xt=xnb_tok, l1=w12_load[sl1], l2=w12_load[sl2]):
                t.wait_ge(B.sem_g.h, max(l1, l2))
                t.wait_ge(B.sem_x.h, xt)
                B.pe_guard(t, [bank1, bank2])
                for kc in range(NFC):
                    t.matmul(PS[:, bank1, :], w12s[sl1][:, kc, :], xnb[:, kc, :],
                             start=(kc == 0), stop=(kc == NFC - 1))
                ins = None
                for kc in range(NFC):
                    ins = t.matmul(PS[:, bank2, :], w12s[sl2][:, kc, :],
                                   xnb[:, kc, :],
                                   start=(kc == 0), stop=(kc == NFC - 1))
                B.sem_pe.inc(ins)
            pe_tok = B.sem_pe.n
            w12_use[sl1] = pe_tok
            w12_use[sl2] = pe_tok
            ssl = oc % 2

            @B.sec("scalar")
            def _(sc, oc=oc, bank1=bank1, ssl=ssl, tok=pe_tok):
                sc.wait_ge(B.sem_pe.h, tok)
                sc.activation(out=silu[ssl], in_=PS[:, bank1, :],
                                          func=AF.Silu,
                                          scale=B.s1_sb[:, oc:oc + 1])
                B.sem_s.dinc(sc)
            si_tok = B.sem_s.n

            @B.sec("vector")
            def _(v, oc=oc, bank2=bank2, ssl=ssl, tok=si_tok):
                v.wait_ge(B.sem_s.h, tok)
                v.scalar_tensor_tensor(
                    hblk[:, oc, :], PS[:, bank2, :], B.s2_sb[:, oc:oc + 1],
                    silu[ssl], ALU.mult, ALU.mult)
                B.sem_v.dinc(v)
            B.bank_done([bank1], B.sem_s)
            B.bank_done([bank2], B.sem_v)
        h_tok = B.sem_v.n

        def big_mm(wt, src, dst_writer, n_oc, in_tok):
            for oc in range(n_oc):
                sl = oc % 2

                @B.sec("gpsimd")
                def _(gp, oc=oc, sl=sl, wt=wt, u=h1_use[sl]):
                    gp.wait_ge(B.sem_pe.h, u)
                    B.sem_g.inc(gp.dma_start(out=h1s[sl][:],
                                             in_=wt[:, oc, :, :]), 16)
                h1_load[sl] = B.sem_g.n
                bank = oc % 4

                @B.sec("tensor")
                def _(t, oc=oc, bank=bank, sl=sl, ld=h1_load[sl]):
                    t.wait_ge(B.sem_g.h, ld)
                    t.wait_ge(B.sem_v.h, in_tok)
                    B.pe_guard(t, [bank])
                    ins = None
                    for kc in range(32):
                        ins = t.matmul(PS[:, bank, :], h1s[sl][:, kc, :],
                                       src[:, kc, :],
                                       start=(kc == 0), stop=(kc == 31))
                    B.sem_pe.inc(ins)
                h1_use[sl] = B.sem_pe.n
                dst_writer(oc, bank, B.sem_pe.n)

        def gelu_writer(dst, ssb):
            def w(oc, bank, pe_tok):
                @B.sec("scalar")
                def _(sc, oc=oc, bank=bank, tok=pe_tok):
                    sc.wait_ge(B.sem_pe.h, tok)
                    sc.activation(
                        out=dst[:, oc, :], in_=PS[:, bank, :],
                        func=AF.Gelu, scale=ssb[:, oc:oc + 1])
                    B.sem_s.dinc(sc)
                B.bank_done([bank], B.sem_s)
            return w

        big_mm(B.h1_t, hblk, gelu_writer(gblk, B.sh1_sb), 32, h_tok)
        g_tok = _v_marker(B, B.sem_s, B.sem_s.n)
        big_mm(B.h2_t, gblk, gelu_writer(hblk, B.sh2_sb), 32, g_tok)
        g2_tok = _v_marker(B, B.sem_s, B.sem_s.n)

        def w3_writer(oc, bank, pe_tok):
            sl = oc % 2

            @B.sec("sync")
            def _(s, oc=oc, sl=sl, bt=bt, stt=stage_tok[sl], at=att_tok[sl]):
                if stt:
                    s.wait_ge(B.sem_v.h, stt)
                if at:
                    s.wait_ge(B.sem_v.h, at)
                col = bt * 512
                B.sem_x.inc(s.dma_start(
                    out=accst[sl],
                    in_=B.acc0_pk.rearrange("(c p) t -> p c t", p=128)[
                        :, oc, col:col + 512]), 16)
                B.sem_x.inc(s.dma_start(
                    out=attst[sl],
                    in_=B.attn0_pk.rearrange("(c p) t -> p c t", p=128)[
                        :, oc, col:col + 512]), 16)
            rb_tok = B.sem_x.n

            @B.sec("vector")
            def _(v, oc=oc, bank=bank, sl=sl, tok=pe_tok, rb=rb_tok):
                v.wait_ge(B.sem_pe.h, tok)
                v.wait_ge(B.sem_x.h, rb)
                v.scalar_tensor_tensor(outst[sl], PS[:, bank, :],
                                       B.s3_sb[:, oc:oc + 1], attst[sl],
                                       ALU.mult, ALU.add)
                v.tensor_add(outst[sl], outst[sl], accst[sl])
                B.sem_v.dinc(v)
            B.bank_done([bank], B.sem_v)
            ev = B.sem_v.n
            stage_tok[sl] = ev
            att_tok[sl] = ev

            @B.sec("sync")
            def _(s, oc=oc, sl=sl, bt=bt, ev=ev):
                s.wait_ge(B.sem_v.h, ev)
                col = bt * 512
                B.sem_x.inc(s.dma_start(
                    out=B.x1_pk.rearrange("(c p) t -> p c t", p=128)[
                        :, oc, col:col + 512],
                    in_=outst[sl]), 16)

        big_mm(B.w3_t, hblk, w3_writer, 8, g2_tok)
        xnb_prev_pe = B.sem_pe.n
    B.x1_tok = B.sem_x.n


def collective_max(B, src_col, ncols, wait_sem, wait_tok):
    PS = B.PS

    @B.sec("tensor")
    def _(t):
        t.wait_ge(wait_sem.h, wait_tok)
        B.pe_guard(t, [6])
        t.transpose(PS[0:ncols, 6, 0:128],
                    B.red_sb[:, src_col:src_col + ncols], B.ident[:])
        B.sem_pe.dinc(t)
    tr = B.sem_pe.n

    @B.sec("vector")
    def _(v):
        v.wait_ge(B.sem_pe.h, tr)
        v.memset(B.rowst[0:4, 0:1], 0.0)
        v.tensor_reduce(B.rowst[0:ncols, 0:1],
                        PS[0:ncols, 6, 0:128], AX.X, ALU.max)
        B.sem_v.dinc(v)
    B.bank_done([6], B.sem_v)
    red = B.sem_v.n

    @B.sec("gpsimd")
    def _(gp):
        gp.wait_ge(B.sem_v.h, red)
        B.sem_g.inc(gp.dma_start(out=B.cc_in[0:1, 0:4],
                                 in_=B.rowst[0:4, 0:1]), 16)
        gp.wait_ge(B.sem_g.h, B.sem_g.n)
        B.sem_cc.inc(gp.collective_compute(
            "AllReduce", ALU.max, replica_groups=[list(range(B.n_cores))],
            ins=[B.cc_in.opt()], outs=[B.cc_out.opt()]))
    cc_tok = B.sem_cc.n

    @B.sec("sync")
    def _(s):
        s.wait_ge(B.sem_cc.h, cc_tok)
        B.sem_x.inc(s.dma_start(
            out=B.scal_sb[:, SC_CC0:SC_CC0 + ncols],
            in_=B.cc_out[0:1, 0:ncols].broadcast_to([128, ncols])), 16)
    return B.sem_x.n


def p5_ln1(B):
    ln_stats(B, B.x1_pk, layer=1, xwait=(B.sem_x, B.x1_tok))

    xn1 = B.bba(0, 16384, [128, 8, 2048])
    B.xn1 = xn1
    chunk = B.XCH[:]
    xnf = B.fs(0, 2048)
    mr_b = B.fs(2048, 4096)
    r_b = B.BSH
    B.load_bcast(r_b[:], 0)
    B.load_bcast(mr_b, 1)
    bc_tok = B.sem_x.n

    pe_now = B.sem_pe.n
    prev_v = 0
    for fc in range(NFC):
        @B.sec("sync")
        def _(s, fc=fc, pv=prev_v):
            s.wait_ge(B.sem_x.h, bc_tok)
            if pv:
                s.wait_ge(B.sem_v.h, pv)
            B.sem_x.inc(s.dma_start(
                out=chunk,
                in_=B.x1_pk.rearrange("(c p) t -> p c t", p=128)[:, fc, :]), 16)
        ld = B.sem_x.n

        @B.sec("vector")
        def _(v, fc=fc, ld=ld, pe=pe_now):
            v.wait_ge(B.sem_x.h, ld)
            if fc == 0:
                v.wait_ge(B.sem_pe.h, pe)
            v.tensor_mul(xnf, chunk, r_b[:])
            v.tensor_sub(xnf, xnf, mr_b)
            v.tensor_copy(xn1[:, fc, :], xnf)
            v.tensor_reduce(B.red_sb[:, 7:8], xnf, AX.X, ALU.max,
                            apply_absolute_value=True)
            v.drain()
            if fc == 0:
                v.tensor_copy(B.red_sb[:, 0:1], B.red_sb[:, 7:8])
            else:
                v.tensor_max(B.red_sb[:, 0:1], B.red_sb[:, 0:1],
                             B.red_sb[:, 7:8])
            B.sem_v.dinc(v)
        prev_v = B.sem_v.n
    B.xn1_tok = B.sem_v.n
    B.tap("xn1", lambda: B.xn1[:], B.sem_v, B.xn1_tok, chunks=8)

    cx = collective_max(B, 0, 1, B.sem_v, B.xn1_tok)

    @B.sec("vector")
    def _(v):
        v.wait_ge(B.sem_x.h, cx)
        v.tensor_scalar(B.scal_sb[:, SC_SX:SC_SX + 1],
                        B.scal_sb[:, SC_CC0:SC_CC0 + 1],
                        1.0 / QMAX16, 1e-12, ALU.mult, ALU.max)
        v.drain()
        v.reciprocal(B.scal_sb[:, SC_IX:SC_IX + 1],
                     B.scal_sb[:, SC_SX:SC_SX + 1])
        B.sem_v.dinc(v)
    B.sx_tok = B.sem_v.n


def p6_attn1(B):
    PS, PSrow = B.PS, B.PSrow
    xn1 = B.xn1
    k1 = B.bbb(0, 16384, [128, 8, 2048])
    v1 = B.bba(16384, 32768, [128, 16, 1024])
    xw = (B.sem_v, B.xn1_tok)

    qstg = [B.bbb(16384 + i * 512, 16384 + (i + 1) * 512) for i in range(4)]
    qstg_tok = [0, 0, 0, 0]

    def q_writer(oc, b, bank, pe_tok):
        sl = (oc * NQB + b) % 4

        @B.sec("vector")
        def _(v, oc=oc, bank=bank, sl=sl, pe_tok=pe_tok, stt=qstg_tok[sl]):
            v.wait_ge(B.sem_pe.h, pe_tok)
            if stt:
                v.wait_ge(B.sem_x.h, stt)
            v.tensor_scalar_add(qstg[sl], B.PS[:, bank, :],
                                            B.bq_sb[:, oc:oc + 1])
            B.sem_v.dinc(v)
        B.bank_done([bank], B.sem_v)
        ev = B.sem_v.n

        @B.sec("sync")
        def _(s, oc=oc, b=b, sl=sl, ev=ev):
            s.wait_ge(B.sem_v.h, ev)
            B.sem_x.inc(s.dma_start(
                out=B.q1_pk.rearrange("(c p) t -> p c t", p=128)[
                    :, oc, b * 512:(b + 1) * 512],
                in_=qstg[sl]), 16)
        qstg_tok[sl] = B.sem_x.n

    mm_proj(B, B.qkvq_t, 8, q_writer,
            lambda kc, b: xn1[:, kc, b * 512:(b + 1) * 512],
            extra_wait=xw, wname="q1")

    def k_writer(oc, b, bank, pe_tok):
        @B.sec("vector")
        def _(v, oc=oc, b=b, bank=bank, pe_tok=pe_tok):
            v.wait_ge(B.sem_pe.h, pe_tok)
            v.tensor_scalar_add(
                k1[:, oc, b * 512:(b + 1) * 512], B.PS[:, bank, :],
                B.bk_sb[:, oc:oc + 1])
            B.sem_v.dinc(v)
        B.bank_done([bank], B.sem_v)

    mm_proj(B, B.qkvk_t, 8, k_writer,
            lambda kc, b: xn1[:, kc, b * 512:(b + 1) * 512],
            extra_wait=xw, wname="k1")

    wv1 = B.bbb(20480, 24576, [128, 8, 512])
    wv1_use = B.sem_pe.n
    for half in range(2):
        @B.sec("gpsimd")
        def _(gp, half=half, u=wv1_use):
            gp.wait_ge(B.sem_pe.h, u)
            B.sem_g.inc(gp.dma_start(out=wv1[:],
                                     in_=B.qkvv_m[:, :, half * 512:(half + 1) * 512]),
                        16)
        wv_tok = B.sem_g.n

        for tt in range(NQT):
            bank = tt % 4

            @B.sec("tensor")
            def _(t, tt=tt, bank=bank, wv_tok=wv_tok):
                t.wait_ge(B.sem_g.h, wv_tok)
                B.pe_guard(t, [bank])
                ins = None
                for kc in range(NFC):
                    ins = t.matmul(PS[:, bank, :],
                                   xn1[:, kc, tt * 128:(tt + 1) * 128],
                                   wv1[:, kc, :], start=(kc == 0),
                                   stop=(kc == NFC - 1))
                B.sem_pe.inc(ins)
            pe_tok = B.sem_pe.n
            wv1_use = pe_tok

            @B.sec("vector")
            def _(v, tt=tt, bank=bank, half=half, pe_tok=pe_tok):
                v.wait_ge(B.sem_pe.h, pe_tok)
                if tt == 0 and half == 0:
                    v.wait_ge(B.sem_g.h, B.bv_tok)
                v.tensor_add(v1[:, tt, half * 512:(half + 1) * 512],
                                         PS[:, bank, :],
                                         B.bv_sb[:, half * 512:(half + 1) * 512])
                B.sem_v.dinc(v)
            B.bank_done([bank], B.sem_v)
    B.qkv1_tok = B.sem_v.n

    xqst = [B.bbc(i * 2048, (i + 1) * 2048) for i in range(2)]
    fscr = B.BSH
    xq_st_tok = [0, 0]
    prev_a = 0
    for fc in range(NFC):
        @B.sec("scalar")
        def _(sc, fc=fc, pa=prev_a):
            if fc == 0:
                sc.wait_ge(B.sem_v.h, B.sx_tok)
            else:
                sc.wait_ge(B.sem_v.h, pa)
            sc.activation(out=fscr[:], in_=xn1[:, fc, :],
                                      func=AF.Identity,
                                      scale=B.scal_sb[:, SC_IX:SC_IX + 1],
                                      bias=B.tco[:, 4:5])
            B.sem_s.dinc(sc)
        a_tok = B.sem_s.n
        sl = fc % 2

        @B.sec("vector")
        def _(v, sl=sl, a_tok=a_tok, pstt=xq_st_tok[sl]):
            v.wait_ge(B.sem_s.h, a_tok)
            if pstt:
                v.wait_ge(B.sem_x.h, pstt)
            v.tensor_scalar_sub(fscr[:], fscr[:], MAGIC)
            v.tensor_scalar_mul(xqst[sl], fscr[:],
                                            B.scal_sb[:, SC_SX:SC_SX + 1])
            B.sem_v.dinc(v)
        prev_a = B.sem_v.n
        r_tok = B.sem_v.n

        @B.sec("sync")
        def _(s, fc=fc, sl=sl, r_tok=r_tok):
            s.wait_ge(B.sem_v.h, r_tok)
            B.sem_x.inc(s.dma_start(
                out=B.xq_pk.rearrange("(c p) t -> p c t", p=128)[:, fc, :],
                in_=xqst[sl]), 16)
        xq_st_tok[sl] = B.sem_x.n
    B.xq_park_tok = B.sem_x.n

    q1s = [B.bbb(16384 + i * 2048, 16384 + (i + 1) * 2048) for i in range(2)]
    Pc = [B.bbb(20480 + i * 2048, 20480 + (i + 1) * 2048) for i in range(2)]
    PT1 = B.bbb(24576, 32768, [128, 16, 512])
    pstg = [B.bbc(4096 + i * 512, 4096 + (i + 1) * 512) for i in range(4)]
    red = B.red_sb
    attn_pe0 = B.sem_pe.n
    q1_free = [attn_pe0, attn_pe0]
    p_free = [0, 0]
    prev_pv = 0
    first = True
    park_toks = [0, 0, 0, 0]
    for h in range(16):
        hp, hc = h % 2, h // 2
        qsl = h % 2

        @B.sec("sync")
        def _(s, hp=hp, hc=hc, qsl=qsl, u=q1_free[qsl]):
            s.wait_ge(B.sem_x.h, B.xq_park_tok)
            s.wait_ge(B.sem_pe.h, u)
            B.sem_x.inc(s.dma_start(
                out=q1s[qsl][hp * 64:(hp + 1) * 64, :],
                in_=B.q1_pk.rearrange("(c p) t -> p c t", p=128)[
                    hp * 64:(hp + 1) * 64, hc, :]), 16)
        q_ld = B.sem_x.n

        for qb in range(NQB):
            tr_last = 0
            for qi in range(4):
                qt = 4 * qb + qi
                psl = qt % 2

                @B.sec("tensor")
                def _(t, qt=qt, hp=hp, hc=hc, qsl=qsl, q_ld=q_ld, first=first):
                    t.wait_ge(B.sem_x.h, q_ld)
                    if first:
                        t.wait_ge(B.sem_v.h, B.qkv1_tok)
                    B.pe_guard(t, [0, 1, 2, 3])
                    ins = None
                    for b in range(4):
                        ins = t.matmul(PS[:, b, :],
                                       q1s[qsl][hp * 64:(hp + 1) * 64,
                                                qt * 128:(qt + 1) * 128],
                                       k1[hp * 64:(hp + 1) * 64, hc,
                                          b * 512:(b + 1) * 512],
                                       start=True, stop=True)
                    B.sem_pe.inc(ins)
                s_tok = B.sem_pe.n

                @B.sec("vector")
                def _(v, tok=s_tok):
                    v.wait_ge(B.sem_pe.h, tok)
                    v.tensor_reduce(red[:, 2:3], PSrow[:, 0:2048], AX.X, ALU.max)
                    v.drain()
                    v.tensor_scalar_mul(red[:, 3:4], red[:, 2:3], -8.0)
                    B.sem_v.dinc(v)
                m_tok = B.sem_v.n

                @B.sec("scalar")
                def _(sc, psl=psl, tok=m_tok, pp=p_free[psl]):
                    sc.wait_ge(B.sem_v.h, tok)
                    if pp:
                        sc.wait_ge(B.sem_t.h, pp)
                    sc.activation(
                        out=Pc[psl][:], in_=PSrow[:, 0:2048], func=AF.Exp,
                        scale=B.tco[:, 3:4], bias=red[:, 3:4],
                        accum_out=red[:, 5:6])
                    B.sem_s.dinc(sc)
                e_tok = B.sem_s.n
                B.bank_done([0, 1, 2, 3], B.sem_s)
                first = False

                @B.sec("vector")
                def _(v, psl=psl, tok=e_tok):
                    v.wait_ge(B.sem_s.h, tok)
                    v.reciprocal(red[:, 5:6], red[:, 5:6])
                    v.drain()
                    v.tensor_scalar_mul(Pc[psl][:], Pc[psl][:], red[:, 5:6])
                    B.sem_v.dinc(v)
                n_tok = B.sem_v.n

                @B.sec("sync")
                def _(s, qi=qi, psl=psl, nt=n_tok, pv=prev_pv):
                    s.wait_ge(B.sem_v.h, nt)
                    if qi == 0 and pv:
                        s.wait_ge(B.sem_pe.h, pv)
                    B.sem_t.inc(s.dma_start_transpose(
                        PT1[:, 0:16, qi * 128:(qi + 1) * 128],
                        Pc[psl][:, 0:2048]), 16)
                p_free[psl] = B.sem_t.n
                tr_last = B.sem_t.n
                
            bank = 4 + (h * 4 + qb) % 2
            sl = (h * 4 + qb) % 4

            @B.sec("tensor")
            def _(t, h=h, bank=bank, tt=tr_last):
                t.wait_ge(B.sem_t.h, tt)
                B.pe_guard(t, [bank])
                ins = None
                for kc in range(16):
                    ins = t.matmul(PS[0:64, bank, :],
                                   v1[:, kc, h * 64:(h + 1) * 64],
                                   PT1[:, kc, :], start=(kc == 0), stop=(kc == 15))
                B.sem_pe.inc(ins)
            pv_tok = B.sem_pe.n
            prev_pv = pv_tok
            q1_free[qsl] = pv_tok

            @B.sec("vector")
            def _(v, bank=bank, sl=sl, tok=pv_tok, pk=park_toks[sl]):
                v.wait_ge(B.sem_pe.h, tok)
                if pk:
                    v.wait_ge(B.sem_x.h, pk)
                v.tensor_copy(pstg[sl][0:64, :], PS[0:64, bank, :])
                B.sem_v.dinc(v)
            B.bank_done([bank], B.sem_v)
            ev = B.sem_v.n

            @B.sec("sync")
            def _(s, hp=hp, hc=hc, qb=qb, sl=sl, ev=ev):
                s.wait_ge(B.sem_v.h, ev)
                B.sem_x.inc(s.dma_start(
                    out=B.attn1_pk.rearrange("(c p) t -> p c t", p=128)[
                        hp * 64:(hp + 1) * 64, hc, qb * 512:(qb + 1) * 512],
                    in_=pstg[sl][0:64, :]), 16)
            park_toks[sl] = B.sem_x.n
    B.attn1_park = B.sem_x.n

    av = B.bba(0, 16384, [128, 8, 2048])
    last_pe = B.sem_pe.n

    @B.sec("sync")
    def _(s, lp=last_pe):
        s.wait_ge(B.sem_x.h, B.attn1_park)
        s.wait_ge(B.sem_pe.h, lp)
        B.sem_x.inc(s.dma_start(
            out=av[:], in_=B.attn1_pk.rearrange("(c p) t -> p c t", p=128)), 16)
    av_tok = B.sem_x.n

    x1st = [B.fs(i * 512, (i + 1) * 512) for i in range(2)]
    outst = [B.fs(1024 + i * 512, 1024 + (i + 1) * 512) for i in range(2)]
    stage_tok = [0, 0]

    def wp_writer(oc, b, bank, pe_tok):
        sl = (oc * NQB + b) % 2

        @B.sec("sync")
        def _(s, oc=oc, b=b, sl=sl, stt=stage_tok[sl]):
            if stt:
                s.wait_ge(B.sem_v.h, stt)
            B.sem_x.inc(s.dma_start(
                out=x1st[sl],
                in_=B.x1_pk.rearrange("(c p) t -> p c t", p=128)[
                    :, oc, b * 512:(b + 1) * 512]), 16)
        rb_tok = B.sem_x.n

        @B.sec("vector")
        def _(v, oc=oc, bank=bank, sl=sl, tok=pe_tok, rb=rb_tok):
            v.wait_ge(B.sem_pe.h, tok)
            v.wait_ge(B.sem_x.h, rb)
            v.scalar_tensor_tensor(
                outst[sl], B.PS[:, bank, :], B.bp_sb[:, oc:oc + 1],
                x1st[sl], ALU.add, ALU.add)
            B.sem_v.dinc(v)
        B.bank_done([bank], B.sem_v)
        ev = B.sem_v.n
        stage_tok[sl] = ev

        @B.sec("sync")
        def _(s, oc=oc, b=b, sl=sl, ev=ev):
            s.wait_ge(B.sem_v.h, ev)
            B.sem_x.inc(s.dma_start(
                out=B.x1_pk.rearrange("(c p) t -> p c t", p=128)[
                    :, oc, b * 512:(b + 1) * 512],
                in_=outst[sl]), 16)

    mm_proj(B, B.wp_t, 8, wp_writer,
            lambda kc, b: av[:, kc, b * 512:(b + 1) * 512],
            extra_wait=(B.sem_x, av_tok), wname="wp")
    B.acc1_tok = B.sem_x.n


def dequant_slice(B, nat, wqn, nblk, wait_sem, wait_tok, extra=None):
    n = nblk * 32

    @B.sec("vector")
    def _(v):
        v.wait_ge(wait_sem.h, wait_tok)
        if extra is not None and extra[1]:
            v.wait_ge(extra[0].h, extra[1])
        nat3 = nat[:, 0:n].rearrange("p (a b) -> p a b", b=32)
        wqn3 = wqn[:, 0:n].rearrange("p (a b) -> p a b", b=32)
        v.tensor_reduce(B.dqs[:, 0:nblk], nat3, AX.X, ALU.max,
                        apply_absolute_value=True)
        v.drain()
        v.tensor_scalar(B.dqs[:, 0:nblk], B.dqs[:, 0:nblk], 1.0 / QMAX4, 1e-12,
                        ALU.mult, ALU.max)
        v.drain()
        v.reciprocal(B.dqi[:, 0:nblk], B.dqs[:, 0:nblk])
        v.drain()
        v.tensor_mul(nat3, nat3,
                     B.dqi[:, 0:nblk].rearrange("p (a o) -> p a o", o=1)
                     .broadcast_to([128, nblk, 32]))
        v.tensor_scalar(nat[:, 0:n], nat[:, 0:n], MAGIC, MAGIC, ALU.add,
                        ALU.subtract)
        v.tensor_mul(
            wqn3, nat3,
            B.dqs[:, 0:nblk].rearrange("p (a o) -> p a o", o=1)
            .broadcast_to([128, nblk, 32]))
        B.sem_v.dinc(v)
    return B.sem_v.n


def p7_ffn1(B):
    PS = B.PS
    xq = B.bbc(0, 16384, [128, 8, 2048])
    wp_done_pe = B.sem_pe.n

    @B.sec("sync")
    def _(s):
        s.wait_ge(B.sem_pe.h, wp_done_pe)
        s.wait_ge(B.sem_x.h, B.xq_park_tok)
        B.sem_x.inc(s.dma_start(
            out=xq[:], in_=B.xq_pk.rearrange("(c p) t -> p c t", p=128)), 16)
    xq_tok = B.sem_x.n

    nat_s = [B.fs(i * 1024, (i + 1) * 1024) for i in range(2)]
    wqn_s = [B.wb(i * 1024, (i + 1) * 1024) for i in range(2)]
    wqt_s = [B.bbb(5120 + i * 1024, 5120 + (i + 1) * 1024, [128, 8, 128])
             for i in range(2)]
    stg_bf = [B.bbb(i * 512, (i + 1) * 512) for i in range(4)]

    vmark = _v_marker(B, B.sem_pe, wp_done_pe)
    nat_use = [vmark, vmark]
    wqn_use = [(B.sem_v, vmark), (B.sem_v, vmark)]
    wqt_use = [wp_done_pe, wp_done_pe]
    stg_use = [0, 0, 0, 0]
    for wi, (wdram, pk, redcol) in enumerate(
            ((B.wup_n, B.upp_pk, 1), (B.wgate_n, B.gatep_pk, 2))):
        for oc in range(24):
            sl = oc % 2

            @B.sec("sync")
            def _(s, oc=oc, sl=sl, wdram=wdram, u=nat_use[sl]):
                s.wait_ge(B.sem_v.h, u)
                B.sem_w.inc(s.dma_start(out=nat_s[sl],
                                        in_=wdram[oc * 128:(oc + 1) * 128, :]), 16)
            nat_ld = B.sem_w.n

            dq_tok = dequant_slice(B, nat_s[sl], wqn_s[sl], 32,
                                   B.sem_w, nat_ld, extra=wqn_use[sl])
            nat_use[sl] = dq_tok

            @B.sec("sync")
            def _(s, sl=sl, dq=dq_tok, u=wqt_use[sl]):
                s.wait_ge(B.sem_v.h, dq)
                s.wait_ge(B.sem_pe.h, u)
                B.sem_t.inc(s.dma_start_transpose(
                    wqt_s[sl][:, 0:8, :], wqn_s[sl][:, 0:1024]), 16)
            tr_tok = B.sem_t.n
            wqn_use[sl] = (B.sem_t, tr_tok)

            for b in range(NQB):
                bank = (oc * NQB + b) % 4

                @B.sec("tensor")
                def _(t, b=b, bank=bank, sl=sl, tr=tr_tok, wi=wi, oc=oc):
                    t.wait_ge(B.sem_t.h, tr)
                    if oc == 0 and b == 0 and wi == 0:
                        t.wait_ge(B.sem_x.h, xq_tok)
                    B.pe_guard(t, [bank])
                    ins = None
                    for kc in range(8):
                        ins = t.matmul(PS[:, bank, :], wqt_s[sl][:, kc, :],
                                       xq[:, kc, b * 512:(b + 1) * 512],
                                       start=(kc == 0), stop=(kc == 7))
                    B.sem_pe.inc(ins)
                pe_tok = B.sem_pe.n
                wqt_use[sl] = pe_tok
                ssl = (oc * NQB + b) % 4

                @B.sec("vector")
                def _(v, bank=bank, ssl=ssl, tok=pe_tok, rc=redcol,
                      firstc=(oc == 0 and b == 0), pk_tok=stg_use[ssl]):
                    v.wait_ge(B.sem_pe.h, tok)
                    if pk_tok:
                        v.wait_ge(B.sem_x.h, pk_tok)
                    v.tensor_copy(stg_bf[ssl], PS[:, bank, :])
                    v.tensor_reduce(B.red_sb[:, 7:8], PS[:, bank, :], AX.X,
                                    ALU.max, apply_absolute_value=True)
                    v.drain()
                    if firstc:
                        v.tensor_copy(B.red_sb[:, rc:rc + 1], B.red_sb[:, 7:8])
                    else:
                        v.tensor_max(B.red_sb[:, rc:rc + 1],
                                     B.red_sb[:, rc:rc + 1], B.red_sb[:, 7:8])
                    B.sem_v.dinc(v)
                B.bank_done([bank], B.sem_v)
                ev = B.sem_v.n

                @B.sec("sync")
                def _(s, oc=oc, b=b, ssl=ssl, ev=ev, pk=pk):
                    s.wait_ge(B.sem_v.h, ev)
                    B.sem_x.inc(s.dma_start(
                        out=pk.rearrange("(c p) t -> p c t", p=128)[
                            :, oc, b * 512:(b + 1) * 512],
                        in_=stg_bf[ssl]), 16)
                stg_use[ssl] = B.sem_x.n
    upg_tok = B.sem_v.n

    c2 = collective_max(B, 1, 2, B.sem_v, upg_tok)

    @B.sec("vector")
    def _(v):
        v.wait_ge(B.sem_x.h, c2)
        sc = B.scal_sb
        v.tensor_scalar(sc[:, SC_SU:SC_SU + 1], sc[:, SC_CC0:SC_CC0 + 1],
                        1.0 / QMAX16, 1e-12, ALU.mult, ALU.max)
        v.tensor_scalar(sc[:, SC_SG:SC_SG + 1], sc[:, SC_CC0 + 1:SC_CC0 + 2],
                        1.0 / QMAX16, 1e-12, ALU.mult, ALU.max)
        v.drain()
        v.reciprocal(sc[:, SC_IU:SC_IU + 1], sc[:, SC_SU:SC_SU + 1])
        v.drain()
        v.reciprocal(sc[:, SC_IG:SC_IG + 1], sc[:, SC_SG:SC_SG + 1])
        v.tensor_scalar_mul(sc[:, SC_SGC:SC_SGC + 1],
                            sc[:, SC_SG:SC_SG + 1], 1.0 / 65536.0)
        B.sem_v.dinc(v)
    c2s_tok = B.sem_v.n

    ctA = [B.bbb(8192 + i * 2048, 8192 + (i + 1) * 2048) for i in range(2)]
    ctB = [B.bbb(12288 + i * 2048, 12288 + (i + 1) * 2048) for i in range(2)]
    ots = [B.bbb(16384 + i * 2048, 16384 + (i + 1) * 2048) for i in range(2)]
    f0w = B.fs(2048, 4096)
    f1w = B.XCH[:]

    wb_ready = _v_marker(B, B.sem_pe, B.sem_pe.n)

    def chain_pass(emit_tile):
        for oc in range(24):
            emit_tile(oc, oc % 2)

    ldB_use = [wb_ready, wb_ready]
    stB_use = [0, 0]
    f0_guard = [0]

    def tileB(oc, sl):
        @B.sec("sync")
        def _(s, oc=oc, sl=sl, u=ldB_use[sl]):
            s.wait_ge(B.sem_v.h, u)
            B.sem_w.inc(s.dma_start(
                out=ctA[sl],
                in_=B.gatep_pk.rearrange("(c p) t -> p c t", p=128)[:, oc, :]),
                16)
        ld = B.sem_w.n

        @B.sec("vector")
        def _(v, sl=sl, ld=ld, fg=f0_guard[0], first=(oc == 0)):
            v.wait_ge(B.sem_w.h, ld)
            if first:
                v.wait_ge(B.sem_v.h, c2s_tok)
            if fg:
                v.wait_ge(B.sem_s.h, fg)
            v.tensor_scalar_mul(f0w, ctA[sl], B.scal_sb[:, SC_IG:SC_IG + 1])
            v.tensor_scalar_add(f0w, f0w, MAGIC)
            v.tensor_scalar_sub(f0w, f0w, MAGIC)
            B.sem_v.dinc(v)
        r1 = B.sem_v.n
        ldB_use[sl] = r1

        @B.sec("scalar")
        def _(sc_, r1=r1):
            sc_.wait_ge(B.sem_v.h, r1)
            sc_.activation(out=f1w, in_=f0w, func=AF.Sigmoid,
                           scale=B.scal_sb[:, SC_SG:SC_SG + 1])
            B.sem_s.dinc(sc_)
        a2 = B.sem_s.n
        f0_guard[0] = a2

        @B.sec("vector")
        def _(v, sl=sl, a2=a2, u=stB_use[sl], firstc=(oc == 0)):
            v.wait_ge(B.sem_s.h, a2)
            if u:
                v.wait_ge(B.sem_x.h, u)
            v.tensor_scalar(f1w, f1w, 65536.0, MAGIC, ALU.mult, ALU.add)
            v.tensor_scalar(f1w, f1w, MAGIC, 65535.0, ALU.subtract, ALU.min)
            v.scalar_tensor_tensor(ctB[sl], f0w,
                                   B.scal_sb[:, SC_SGC:SC_SGC + 1], f1w,
                                   ALU.mult, ALU.mult)
            v.tensor_reduce(B.red_sb[:, 7:8], ctB[sl], AX.X, ALU.max,
                            apply_absolute_value=True)
            v.drain()
            if firstc:
                v.tensor_copy(B.red_sb[:, 3:4], B.red_sb[:, 7:8])
            else:
                v.tensor_max(B.red_sb[:, 3:4], B.red_sb[:, 3:4],
                             B.red_sb[:, 7:8])
            B.sem_v.dinc(v)
        ev = B.sem_v.n

        @B.sec("sync")
        def _(s, oc=oc, sl=sl, ev=ev):
            s.wait_ge(B.sem_v.h, ev)
            B.sem_x.inc(s.dma_start(
                out=B.g2p_pk.rearrange("(c p) t -> p c t", p=128)[:, oc, :],
                in_=ctB[sl]), 16)
        stB_use[sl] = B.sem_x.n

    chain_pass(tileB)
    c3 = collective_max(B, 3, 1, B.sem_v, B.sem_v.n)

    @B.sec("vector")
    def _(v):
        v.wait_ge(B.sem_x.h, c3)
        sc = B.scal_sb
        v.tensor_scalar(sc[:, SC_S2:SC_S2 + 1], sc[:, SC_CC0:SC_CC0 + 1],
                        1.0 / QMAX16, 1e-12, ALU.mult, ALU.max)
        v.drain()
        v.reciprocal(sc[:, SC_I2:SC_I2 + 1], sc[:, SC_S2:SC_S2 + 1])
        v.tensor_mul(sc[:, SC_C3:SC_C3 + 1], sc[:, SC_S2:SC_S2 + 1],
                     sc[:, SC_SU:SC_SU + 1])
        B.sem_v.dinc(v)
    c3s_tok = B.sem_v.n

    ldC_use = [B.sem_v.n, B.sem_v.n]
    stC_use = [0, 0]

    def tileC(oc, sl):
        @B.sec("sync")
        def _(s, oc=oc, sl=sl, u=ldC_use[sl]):
            s.wait_ge(B.sem_v.h, u)
            B.sem_w.inc(s.dma_start(
                out=ctA[sl],
                in_=B.upp_pk.rearrange("(c p) t -> p c t", p=128)[:, oc, :]),
                16)
            B.sem_w.inc(s.dma_start(
                out=ctB[sl],
                in_=B.g2p_pk.rearrange("(c p) t -> p c t", p=128)[:, oc, :]),
                16)
        ld = B.sem_w.n

        @B.sec("vector")
        def _(v, sl=sl, ld=ld, u=stC_use[sl], firstc=(oc == 0)):
            v.wait_ge(B.sem_w.h, ld)
            if firstc:
                v.wait_ge(B.sem_v.h, c3s_tok)
            if u:
                v.wait_ge(B.sem_x.h, u)
            v.tensor_scalar_mul(f0w, ctA[sl], B.scal_sb[:, SC_IU:SC_IU + 1])
            v.tensor_scalar_add(f0w, f0w, MAGIC)
            v.tensor_scalar_sub(f0w, f0w, MAGIC)
            v.tensor_scalar_mul(f1w, ctB[sl], B.scal_sb[:, SC_I2:SC_I2 + 1])
            v.tensor_scalar_add(f1w, f1w, MAGIC)
            v.tensor_scalar_sub(f1w, f1w, MAGIC)
            v.scalar_tensor_tensor(ctB[sl], f1w,
                                   B.scal_sb[:, SC_C3:SC_C3 + 1], f0w,
                                   ALU.mult, ALU.mult)
            v.tensor_reduce(B.red_sb[:, 7:8], ctB[sl], AX.X, ALU.max,
                            apply_absolute_value=True)
            v.drain()
            if firstc:
                v.tensor_copy(B.red_sb[:, 4:5], B.red_sb[:, 7:8])
            else:
                v.tensor_max(B.red_sb[:, 4:5], B.red_sb[:, 4:5],
                             B.red_sb[:, 7:8])
            B.sem_v.dinc(v)
        ev = B.sem_v.n
        ldC_use[sl] = ev

        @B.sec("sync")
        def _(s, oc=oc, sl=sl, ev=ev):
            s.wait_ge(B.sem_v.h, ev)
            B.sem_x.inc(s.dma_start(
                out=B.opp_pk.rearrange("(c p) t -> p c t", p=128)[:, oc, :],
                in_=ctB[sl]), 16)
        stC_use[sl] = B.sem_x.n

    chain_pass(tileC)
    c4 = collective_max(B, 4, 1, B.sem_v, B.sem_v.n)

    @B.sec("vector")
    def _(v):
        v.wait_ge(B.sem_x.h, c4)
        sc = B.scal_sb
        v.tensor_scalar(sc[:, SC_SO:SC_SO + 1], sc[:, SC_CC0:SC_CC0 + 1],
                        1.0 / QMAX16, 1e-12, ALU.mult, ALU.max)
        v.drain()
        v.reciprocal(sc[:, SC_IO:SC_IO + 1], sc[:, SC_SO:SC_SO + 1])
        B.sem_v.dinc(v)
    c4s_tok = B.sem_v.n
    B.tap("scal", lambda: B.scal_sb[0:1, :], B.sem_v, c4s_tok)
    B.tap("red", lambda: B.red_sb[:, :], B.sem_v, c4s_tok)

    ldD_use = [B.sem_v.n, B.sem_v.n]
    stD_use = [0, 0]

    def tileD(oc, sl):
        @B.sec("sync")
        def _(s, oc=oc, sl=sl, u=ldD_use[sl]):
            s.wait_ge(B.sem_v.h, u)
            B.sem_w.inc(s.dma_start(
                out=ctA[sl],
                in_=B.opp_pk.rearrange("(c p) t -> p c t", p=128)[:, oc, :]),
                16)
        ld = B.sem_w.n

        @B.sec("vector")
        def _(v, sl=sl, ld=ld, u=stD_use[sl], first=(oc == 0)):
            v.wait_ge(B.sem_w.h, ld)
            if first:
                v.wait_ge(B.sem_v.h, c4s_tok)
            if u:
                v.wait_ge(B.sem_x.h, u)
            v.tensor_scalar_mul(f0w, ctA[sl], B.scal_sb[:, SC_IO:SC_IO + 1])
            v.tensor_scalar_add(f0w, f0w, MAGIC)
            v.tensor_scalar_sub(ots[sl], f0w, MAGIC)
            B.sem_v.dinc(v)
        ev = B.sem_v.n
        ldD_use[sl] = ev

        @B.sec("sync")
        def _(s, oc=oc, sl=sl, ev=ev):
            s.wait_ge(B.sem_v.h, ev)
            B.sem_x.inc(s.dma_start(
                out=B.oq_pk.rearrange("(c p) t -> p c t", p=128)[:, oc, :],
                in_=ots[sl]), 16)
        stD_use[sl] = B.sem_x.n

    chain_pass(tileD)
    oq_tok = B.sem_x.n

    wdT = B.bba(0, 24576, [128, 8, 24, 128])
    attn_done_pe = B.sem_pe.n
    nat_use2 = [B.sem_v.n, B.sem_v.n]
    wqn_use2 = [(B.sem_v, B.sem_v.n), (B.sem_v, B.sem_v.n)]
    first_tr = True
    for oc in range(8):
        for third in range(3):
            sl = (oc * 3 + third) % 2

            @B.sec("sync")
            def _(s, oc=oc, third=third, sl=sl, u=nat_use2[sl]):
                s.wait_ge(B.sem_v.h, u)
                B.sem_w.inc(s.dma_start(
                    out=nat_s[sl],
                    in_=B.wdown_n[oc * 128:(oc + 1) * 128,
                                  third * 1024:(third + 1) * 1024]), 16)
            ld = B.sem_w.n

            dq_tok = dequant_slice(B, nat_s[sl], wqn_s[sl], 32, B.sem_w, ld,
                                   extra=wqn_use2[sl])
            nat_use2[sl] = dq_tok

            @B.sec("sync")
            def _(s, oc=oc, third=third, sl=sl, dq=dq_tok, ft=first_tr,
                  ap=attn_done_pe):
                s.wait_ge(B.sem_v.h, dq)
                if ft:
                    s.wait_ge(B.sem_pe.h, ap)
                B.sem_t.inc(s.dma_start_transpose(
                    wdT[:, oc, third * 8:(third + 1) * 8, :],
                    wqn_s[sl][:, 0:1024]), 16)
            wqn_use2[sl] = (B.sem_t, B.sem_t.n)
            first_tr = False
    wdT_tok = B.sem_t.n

    x1st2 = [B.XCH[:, i * 512:(i + 1) * 512] for i in range(2)]
    outst2 = [B.XCH[:, 1024 + i * 512:1024 + (i + 1) * 512] for i in range(2)]
    oqst = [B.bbb(i * 512, (i + 1) * 512) for i in range(2)]
    ost_use = [0, 0]
    x1_use = [0, 0]
    ot_use = [B.sem_pe.n, B.sem_pe.n]
    for nb in range(NQB):
        for kc in range(24):
            sl = kc % 2

            @B.sec("sync")
            def _(s, nb=nb, kc=kc, sl=sl, u=ot_use[sl], oqt=oq_tok):
                s.wait_ge(B.sem_pe.h, u)
                if kc == 0 and nb == 0:
                    s.wait_ge(B.sem_x.h, oqt)
                B.sem_w.inc(s.dma_start(
                    out=oqst[sl],
                    in_=B.oq_pk.rearrange("(c p) t -> p c t", p=128)[
                        :, kc, nb * 512:(nb + 1) * 512]), 16)
            ld = B.sem_w.n

            @B.sec("tensor")
            def _(t, nb=nb, kc=kc, sl=sl, ld=ld, wt=wdT_tok):
                t.wait_ge(B.sem_w.h, ld)
                if kc == 0 and nb == 0:
                    t.wait_ge(B.sem_t.h, wt)
                if kc == 0:
                    B.pe_guard(t, list(range(8)))
                ins = None
                for oc in range(8):
                    ins = t.matmul(PS[:, oc, :], wdT[:, oc, kc, :], oqst[sl],
                                   start=(kc == 0), stop=(kc == 23))
                B.sem_pe.inc(ins)
            ot_use[sl] = B.sem_pe.n
        mm_last = B.sem_pe.n

        for oc in range(8):
            sl = oc % 2

            @B.sec("sync")
            def _(s, nb=nb, oc=oc, sl=sl, u=x1_use[sl]):
                if u:
                    s.wait_ge(B.sem_v.h, u)
                B.sem_x.inc(s.dma_start(
                    out=x1st2[sl],
                    in_=B.x1_pk.rearrange("(c p) t -> p c t", p=128)[
                        :, oc, nb * 512:(nb + 1) * 512]), 16)
            rb_tok = B.sem_x.n

            @B.sec("vector")
            def _(v, oc=oc, sl=sl, mm=mm_last, rb=rb_tok, u=ost_use[sl]):
                v.wait_ge(B.sem_pe.h, mm)
                v.wait_ge(B.sem_x.h, rb)
                if u:
                    v.wait_ge(B.sem_x.h, u)
                v.scalar_tensor_tensor(
                    outst2[sl], PS[:, oc, :], B.scal_sb[:, SC_SO:SC_SO + 1],
                    x1st2[sl], ALU.mult, ALU.add)
                B.sem_v.dinc(v)
            B.bank_done([oc], B.sem_v)
            ev = B.sem_v.n
            x1_use[sl] = ev

            @B.sec("sync")
            def _(s, oc=oc, nb=nb, sl=sl, ev=ev):
                s.wait_ge(B.sem_v.h, ev)
                B.sem_x.inc(s.dma_start(
                    out=B.out.rearrange("(c p) t -> p c t", p=128)[
                        :, oc, nb * 512:(nb + 1) * 512],
                    in_=outst2[sl]), 16)
            ost_use[sl] = B.sem_x.n

    @B.sec("sync")
    def _(s):
        s.wait_ge(B.sem_x.h, B.sem_x.n)



_CACHE = {}


def build(debug_taps=(), n_cores=NCORE, stop_after=None, dbg_bar=""):
    key = (tuple(t[0] for t in debug_taps), n_cores, stop_after, dbg_bar)
    if key in _CACHE:
        return _CACHE[key]
    B = Builder(debug_taps, n_cores)
    B.stop_after = stop_after
    B.dbg_bar = dbg_bar
    p0_consts(B)
    ln_stats(B, B.xT, layer=0)
    p1_xn_acc(B)
    p2_qkv(B)
    p2_attn(B)
    if stop_after in ("nostats", "min2"):
        pass
    elif stop_after is None:
        p3_wo_park(B)
        p4_ffn0(B)
        p5_ln1(B)
        p6_attn1(B)
        p7_ffn1(B)
    @B.sec("sync")
    def _(s):
        s.wait_ge(B.sem_x.h, B.sem_x.n)
    nc = B.finish()
    _CACHE[key] = (nc, B)
    return nc, B


def stage_inputs(inputs):
    f32 = lambda a: np.ascontiguousarray(np.asarray(a, dtype=np.float32))
    x = f32(inputs["L0_attn_x"])
    qkvT = np.transpose(f32(inputs["L1_attn_Wqkv"]), (1, 0, 2)).reshape(1024, 3072)
    bqkv = f32(inputs["L1_attn_bqkv"]).reshape(3072)
    n_idx = np.arange(512, dtype=np.float32)[None, None, :]
    p_idx = np.arange(128, dtype=np.float32)[:, None, None]
    r_idx = np.arange(4, dtype=np.float32)[None, :, None]
    import ml_dtypes
    cmask = np.where(n_idx <= 128.0 * r_idx + p_idx, 0.0, BIGNEG).astype(
        ml_dtypes.bfloat16)
    shared = {
        "wq_t": _tile_om(f32(inputs["L0_attn_Wq"])),
        "wk_t": _tile_om(f32(inputs["L0_attn_Wk"])),
        "wv_m": _tile_mv(f32(inputs["L0_attn_Wv"])),
        "wo_t": _tile_om(f32(inputs["L0_attn_Wo"])),
        "nw_p": _pcol(f32(inputs["L0_attn_norm_w"])),
        "gate_s": f32(inputs["L0_attn_gate"]).reshape(1, 1),
        "ltemp_s": f32(inputs["L0_attn_log_temperature"]).reshape(1, 1),
        "w1_t": _tile_om(f32(inputs["L0_ffn_w1_w"]).T),
        "w2_t": _tile_om(f32(inputs["L0_ffn_w2_w"]).T),
        "h1_t": _tile_om(f32(inputs["L0_ffn_h1_w"]).T),
        "h2_t": _tile_om(f32(inputs["L0_ffn_h2_w"]).T),
        "w3_t": _tile_om(f32(inputs["L0_ffn_w3_w"]).T),
        "s1_p": _pcol(f32(inputs["L0_ffn_w1_s"])),
        "s2_p": _pcol(f32(inputs["L0_ffn_w2_s"])),
        "sh1_p": _pcol(f32(inputs["L0_ffn_h1_s"])),
        "sh2_p": _pcol(f32(inputs["L0_ffn_h2_s"])),
        "s3_p": _pcol(f32(inputs["L0_ffn_w3_s"])),
        "qkvq_t": _tile_om(np.ascontiguousarray(qkvT[:, 0:1024])),
        "qkvk_t": _tile_om(np.ascontiguousarray(qkvT[:, 1024:2048])),
        "qkvv_m": _tile_mv(np.ascontiguousarray(qkvT[:, 2048:3072])),
        "bq_p": _pcol(bqkv[0:1024]),
        "bk_p": _pcol(bqkv[1024:2048]),
        "bv_row": np.ascontiguousarray(bqkv[2048:3072]),
        "wp_t": _tile_om(f32(inputs["L1_attn_Wp"]).T),
        "bp_p": _pcol(f32(inputs["L1_attn_bp"])),
        "wup_n": f32(inputs["L1_ffn_w_up"]),
        "wgate_n": f32(inputs["L1_ffn_w_gate"]),
        "wdown_n": f32(inputs["L1_ffn_w_down"]),
        "ident_in": np.eye(128, dtype=np.float32),
        "cmask_in": cmask,
    }
    xTs = [np.ascontiguousarray(x[c].T) for c in range(NCORE)]
    return shared, xTs


def kernel(**inputs):
    nc, B = build()
    shared, xTs = stage_inputs(inputs)
    in_maps = [dict(shared, xT=xTs[c]) for c in range(NCORE)]
    res = run_bass_kernel_spmd(nc, in_maps, core_ids=list(range(NCORE)))
    return np.stack([np.asarray(res.results[c]["out"]).T for c in range(NCORE)])


# version 18
# speedup vs baseline: 1.1003x; 1.0020x over previous
import numpy as np
import concourse.bass as bass
import concourse.mybir as mybir
from concourse.bass_utils import run_bass_kernel_spmd

F32 = mybir.dt.float32
BF16 = mybir.dt.bfloat16
AF = mybir.ActivationFunctionType
ALU = mybir.AluOpType
AX = mybir.AxisListType

D = 1024
T = 2048
NFC = 8
NQT = 16
NQB = 4
MAGIC = 12582912.0
BIGNEG = -1.0e8
QMAX16 = 32767.0
QMAX4 = 7.0
NCORE = 8

SUMX, SUMSQ, M_, VAR, RSTD, MR, RSQ = 0, 1, 2, 3, 4, 5, 6
QSQ0 = 8
KSQ0 = 12
RS0 = 14

SC_CC0 = 0
SC_SX, SC_IX = 4, 5
SC_SU, SC_IU = 6, 7
SC_SG, SC_IG = 8, 9
SC_SGC = 10
SC_S2, SC_I2 = 11, 12
SC_C3 = 13
SC_SO, SC_IO = 14, 15


def _tile_om(wT):
    K_, M_ = wT.shape
    return np.ascontiguousarray(
        wT.reshape(K_ // 128, 128, M_ // 128, 128).transpose(1, 2, 0, 3))


def _tile_mv(wT):
    K_, N_ = wT.shape
    return np.ascontiguousarray(wT.reshape(K_ // 128, 128, N_).transpose(1, 0, 2))


def _pcol(v):
    return np.ascontiguousarray(v.reshape(-1, 128).T)


class Sem:
    def __init__(self, nc, name):
        self.h = nc.semaphore(name).__enter__()
        self.n = 0

    def inc(self, ins, k=1):
        ins.then_inc(self.h, k)
        self.n += k
        return self.n

    def dinc(self, eng, k=1):
        eng.maybe_drain_then_inc((self.h, k))
        self.n += k
        return self.n


class Builder:
    def __init__(self, debug_taps=(), n_cores=NCORE):
        self.n_cores = n_cores
        self.debug_taps = list(debug_taps)
        self.nc = bass.Bass("TRN2", target_bir_lowering=False,
                            detect_race_conditions=False)
        self.dbg = {}
        self._declare_dram()
        self._declare_sbuf()
        for name, shape, dt in self.debug_taps:
            self.dbg[name] = self.nc.dram_tensor(
                "dbg_" + name, list(shape), dt, kind="ExternalOutput").ap()
        self._block_cm = self.nc.Block()
        self.block = self._block_cm.__enter__()
        nc = self.nc
        self.sem_w = Sem(nc, "sw")
        self.sem_x = Sem(nc, "sx")
        self.sem_t = Sem(nc, "stp")
        self.sem_pe = Sem(nc, "spe")
        self.sem_v = Sem(nc, "sv")
        self.sem_s = Sem(nc, "ssc")
        self.sem_g = Sem(nc, "sg")
        self.sem_cc = Sem(nc, "scc")
        self.bank_free = [None] * 8

    def finish(self):
        self._block_cm.__exit__(None, None, None)
        return self.nc

    def _declare_dram(self):
        nc = self.nc

        def din(name, shape, dt=F32):
            setattr(self, name, nc.dram_tensor(name, list(shape), dt,
                                               kind="ExternalInput").ap())

        din("xT", (D, T))
        din("wq_t", (128, 8, 8, 128)); din("wk_t", (128, 4, 8, 128))
        din("wv_m", (128, 8, 512)); din("wo_t", (128, 8, 8, 128))
        din("nw_p", (128, 8)); din("gate_s", (1, 1)); din("ltemp_s", (1, 1))
        din("w1_t", (128, 32, 8, 128)); din("w2_t", (128, 32, 8, 128))
        din("h1_t", (128, 32, 32, 128)); din("h2_t", (128, 32, 32, 128))
        din("w3_t", (128, 8, 32, 128))
        din("s1_p", (128, 32)); din("s2_p", (128, 32))
        din("sh1_p", (128, 32)); din("sh2_p", (128, 32)); din("s3_p", (128, 8))
        din("qkvq_t", (128, 8, 8, 128)); din("qkvk_t", (128, 8, 8, 128))
        din("qkvv_m", (128, 8, 1024))
        din("bq_p", (128, 8)); din("bk_p", (128, 8)); din("bv_row", (1024,))
        din("wp_t", (128, 8, 8, 128)); din("bp_p", (128, 8))
        din("wup_n", (3072, 1024)); din("wgate_n", (3072, 1024))
        din("wdown_n", (1024, 3072))
        din("ident_in", (128, 128)); din("cmask_in", (128, 4, 512), BF16)

        self.out = nc.dram_tensor("out", [D, T], F32, kind="ExternalOutput").ap()

        def scr(name, shape, dt=F32):
            setattr(self, name, nc.dram_tensor(name, list(shape), dt).ap())

        scr("acc0_pk", (D, T)); scr("attn0_pk", (D, T), BF16)
        scr("xn0_pk", (D, T), BF16); scr("x1_pk", (D, T))
        scr("attn1_pk", (D, T), BF16); scr("xq_pk", (D, T), BF16)
        scr("q1_pk", (D, T), BF16)
        scr("upp_pk", (3072, T), BF16); scr("gatep_pk", (3072, T), BF16)
        scr("g2p_pk", (3072, T), BF16); scr("opp_pk", (3072, T), BF16)
        scr("oq_pk", (3072, T), BF16)
        scr("row_bnc", (8, T)); scr("cc_in", (1, 4)); scr("cc_out", (1, 4))
        scr("red_bnc", (4, 128))

    def _declare_sbuf(self):
        nc = self.nc

        def sb(name, shape, dt=F32):
            setattr(self, name, nc.alloc_sbuf_tensor(name, list(shape), dt).ap())

        sb("ident", (128, 128)); sb("cmask", (128, 4, 512), BF16)
        sb("nw_sb", (128, 8)); sb("s1_sb", (128, 32)); sb("s2_sb", (128, 32))
        sb("sh1_sb", (128, 32)); sb("sh2_sb", (128, 32)); sb("s3_sb", (128, 8))
        sb("bq_sb", (128, 8)); sb("bk_sb", (128, 8)); sb("bp_sb", (128, 8))
        sb("ones_sb", (128, 1)); sb("gate_sb", (128, 1))
        sb("tco", (128, 8))
        sb("scal_sb", (128, 16))
        sb("st_sb", (128, 16, 16))
        sb("rowst", (128, 128))
        sb("red_sb", (128, 8))
        sb("ccst", (128, 2))
        sb("dqs", (128, 96)); sb("dqi", (128, 96))
        sb("bv_sb", (128, 1024), BF16)
        sb("FS", (128, 4096))
        sb("XCH", (128, 2048))
        sb("BSH", (128, 2048))
        sb("BB_A", (128, 32768), BF16)
        sb("BB_B", (128, 32768), BF16)
        sb("BB_C", (128, 16384), BF16)
        sb("WB", (128, 2048), BF16)
        self.PS = nc.alloc_psum_tensor("PS", [128, 8, 512], F32).ap()
        self.PSrow = self.PS.rearrange("p a b -> p (a b)")

    def fs(self, lo, hi, shape=None):
        ap = self.FS[:, lo:hi]
        return ap if shape is None else _resh(ap, shape)

    def bba(self, lo, hi, shape=None):
        ap = self.BB_A[:, lo:hi]
        return ap if shape is None else _resh(ap, shape)

    def bbb(self, lo, hi, shape=None):
        ap = self.BB_B[:, lo:hi]
        return ap if shape is None else _resh(ap, shape)

    def bbc(self, lo, hi, shape=None):
        ap = self.BB_C[:, lo:hi]
        return ap if shape is None else _resh(ap, shape)

    def wb(self, lo, hi, shape=None):
        ap = self.WB[:, lo:hi]
        return ap if shape is None else _resh(ap, shape)

    def sec(self, eng):
        return getattr(self.block, eng)

    def tap(self, name, src_fn, dep_sem=None, dep_tok=None, chunks=None):
        if name in self.dbg:
            ap = self.dbg[name]
            if chunks is not None:
                ap = ap.rearrange("(c p) t -> p c t", p=128)
            sx = self.sem_x

            @self.sec("sync")
            def _(s):
                if dep_sem is not None:
                    s.wait_ge(dep_sem.h, dep_tok)
                sx.inc(s.dma_start(out=ap, in_=src_fn()), 16)

    def pe_guard(self, t, banks):
        for b in banks:
            if self.bank_free[b] is not None:
                sem, cnt = self.bank_free[b]
                t.wait_ge(sem.h, cnt)
                self.bank_free[b] = None

    def bank_done(self, banks, sem):
        for b in banks:
            self.bank_free[b] = (sem, sem.n)

    def bounce_rows(self, src_cols_ap, ncols, rows_at):
        PS, ident, rowst = self.PS, self.ident, self.rowst
        src2 = src_cols_ap.rearrange("p a b -> p (a b)")

        @self.sec("tensor")
        def _(t):
            t.wait_ge(self.sem_v.h, self.sem_v.n)
            t.wait_ge(self.sem_s.h, self.sem_s.n)
            t.wait_ge(self.sem_w.h, self.consts_tok)
            self.pe_guard(t, [5])
            t.transpose(PS[0:16 * ncols, 5, 0:128], src2, ident[:])
            self.sem_pe.dinc(t)
        tr = self.sem_pe.n

        @self.sec("vector")
        def _(v):
            v.wait_ge(self.sem_pe.h, tr)
            v.tensor_copy(rowst[0:16 * ncols, :], PS[0:16 * ncols, 5, 0:128])
            self.sem_v.dinc(v)
        self.bank_done([5], self.sem_v)
        cp = self.sem_v.n

        if rows_at == 0:
            self.tap("st3", lambda: src2, self.sem_pe, tr)
            self.tap("rowst48", lambda: rowst[0:48, :], self.sem_v, cp)

        @self.sec("sync")
        def _(s):
            s.wait_ge(self.sem_v.h, cp)
            for j in range(ncols):
                self.sem_x.inc(s.dma_start(
                    out=self.row_bnc[rows_at + j:rows_at + j + 1, :]
                        .rearrange("r (a b) -> (r a) b", b=128),
                    in_=rowst[16 * j:16 * (j + 1), :]), 16)
        return self.sem_x.n

    def load_bcast(self, dst_ap, row, width=T):
        @self.sec("sync")
        def _(s):
            s.wait_ge(self.sem_x.h, self.sem_x.n)
            self.sem_x.inc(s.dma_start(
                out=dst_ap,
                in_=self.row_bnc[row:row + 1, 0:width].broadcast_to([128, width])),
                16)
        return self.sem_x.n


def _resh(ap, shape):
    if len(shape) == 2:
        return ap
    letters = "abcde"[: len(shape) - 1]
    spec = "p (" + " ".join(letters) + ") -> p " + " ".join(letters)
    kw = {l: s for l, s in zip(letters, shape[1:])}
    return ap.rearrange(spec, **kw)


def _v_marker(B, sem, tok):
    @B.sec("vector")
    def _(v):
        v.wait_ge(sem.h, tok)
        B.sem_v.dinc(v)
    return B.sem_v.n



def p0_consts(B):
    sw = B.sem_w

    @B.sec("sync")
    def _(s):
        pairs = [(B.ident, B.ident_in), (B.cmask, B.cmask_in),
                 (B.nw_sb, B.nw_p), (B.s1_sb, B.s1_p), (B.s2_sb, B.s2_p),
                 (B.sh1_sb, B.sh1_p), (B.sh2_sb, B.sh2_p), (B.s3_sb, B.s3_p),
                 (B.bq_sb, B.bq_p), (B.bk_sb, B.bk_p), (B.bp_sb, B.bp_p)]
        for o, i in pairs:
            sw.inc(s.dma_start(out=o[:], in_=i[:]), 16)
        sw.inc(s.dma_start(out=B.gate_sb[:],
                           in_=B.gate_s[0:1, :].broadcast_to([128, 1])), 16)
        sw.inc(s.dma_start(out=B.tco[:, 0:1],
                           in_=B.ltemp_s[0:1, :].broadcast_to([128, 1])), 16)
    B.consts_tok = sw.n

    @B.sec("vector")
    def _(v):
        v.wait_ge(sw.h, B.consts_tok)
        v.memset(B.ones_sb[:], 1.0)
        v.memset(B.tco[:, 3:4], 8.0)
        v.memset(B.tco[:, 4:5], MAGIC)
        B.sem_v.dinc(v)
    B.ones_tok = B.sem_v.n

    @B.sec("scalar")
    def _(sc):
        sc.wait_ge(sw.h, B.consts_tok)
        sc.activation(out=B.tco[:, 1:2], in_=B.tco[:, 0:1], func=AF.Exp)
        B.sem_s.dinc(sc)
    t_tok = B.sem_s.n

    @B.sec("vector")
    def _(v):
        v.wait_ge(B.sem_s.h, t_tok)
        v.tensor_scalar_mul(B.tco[:, 2:3], B.tco[:, 1:2], -1.0)
        B.sem_v.dinc(v)
    B.negt_tok = B.sem_v.n

    @B.sec("gpsimd")
    def _(gp):
        B.sem_g.inc(gp.dma_start(
            out=B.bv_sb[:],
            in_=B.bv_row[None, :].broadcast_to([128, 1024])), 16)
        gp.wait_ge(sw.h, B.consts_tok)
    B.bv_tok = B.sem_g.n

    @B.sec("tensor")
    def _(t):
        t.wait_ge(sw.h, B.consts_tok)


def ln_stats(B, src_pk, layer, xwait=None):
    PS, st = B.PS, B.st_sb
    xch = B.XCH
    sq = B.fs(0, 2048)
    prev_pe = 0
    for fc in range(NFC):
        @B.sec("sync")
        def _(s, fc=fc, prev=prev_pe):
            if fc == 0 and xwait is not None:
                s.wait_ge(xwait[0].h, xwait[1])
            if prev:
                s.wait_ge(B.sem_pe.h, prev)
            B.sem_x.inc(s.dma_start(
                out=xch[:],
                in_=src_pk.rearrange("(c p) t -> p c t", p=128)[:, fc, :]), 16)
        ld = B.sem_x.n

        @B.sec("scalar")
        def _(sc, fc=fc, ld=ld, prev=prev_pe):
            sc.wait_ge(B.sem_x.h, ld)
            if prev:
                sc.wait_ge(B.sem_pe.h, prev)
            sc.activation(out=sq, in_=xch[:], func=AF.Square)
            B.sem_s.dinc(sc)
        sq_tok = B.sem_s.n

        @B.sec("tensor")
        def _(t, fc=fc, sq_tok=sq_tok, ld=ld):
            t.wait_ge(B.sem_s.h, sq_tok)
            t.wait_ge(B.sem_x.h, ld)
            if fc == 0:
                t.wait_ge(B.sem_v.h, B.ones_tok)
            B.pe_guard(t, [6, 7])
            ins = None
            for qt in range(NQT):
                t.matmul(PS[:, 6, qt:qt + 1], xch[:, qt * 128:(qt + 1) * 128],
                         B.ones_sb[:], start=True, stop=True)
                ins = t.matmul(PS[:, 7, qt:qt + 1], sq[:, qt * 128:(qt + 1) * 128],
                               B.ones_sb[:], start=True, stop=True)
            B.sem_pe.inc(ins)
        prev_pe = B.sem_pe.n

        @B.sec("vector")
        def _(v, fc=fc, pe=prev_pe):
            v.wait_ge(B.sem_pe.h, pe)
            if fc == 0:
                v.tensor_copy(st[:, SUMX, :], PS[:, 6, 0:16])
                v.tensor_copy(st[:, SUMSQ, :], PS[:, 7, 0:16])
                B.sem_v.dinc(v)
            else:
                v.tensor_add(st[:, SUMX, :], st[:, SUMX, :], PS[:, 6, 0:16])
                v.tensor_add(st[:, SUMSQ, :], st[:, SUMSQ, :],
                                         PS[:, 7, 0:16])
                B.sem_v.dinc(v)
        B.bank_done([6, 7], B.sem_v)

    @B.sec("vector")
    def _(v):
        v.tensor_scalar_mul(st[:, M_, :], st[:, SUMX, :], 1.0 / D)
        v.tensor_scalar_mul(st[:, VAR, :], st[:, SUMSQ, :], 1.0 / D)
        v.drain()
        v.tensor_mul(st[:, RSTD, :], st[:, M_, :], st[:, M_, :])
        v.drain()
        v.tensor_sub(st[:, VAR, :], st[:, VAR, :], st[:, RSTD, :])
        v.drain()
        v.tensor_scalar_add(st[:, RSTD, :], st[:, VAR, :], 1e-5)
        v.drain()
        v.reciprocal(st[:, MR, :], st[:, RSTD, :])
        B.sem_v.dinc(v)
    inv_tok = B.sem_v.n

    @B.sec("scalar")
    def _(sc):
        sc.wait_ge(B.sem_v.h, inv_tok)
        sc.activation(out=st[:, RSTD, :], in_=st[:, MR, :], func=AF.Sqrt)
        B.sem_s.dinc(sc)
    rstd_tok = B.sem_s.n

    if layer == 0:
        @B.sec("vector")
        def _(v):
            v.wait_ge(B.sem_s.h, rstd_tok)
            v.tensor_mul(st[:, RSQ, :], st[:, VAR, :], st[:, MR, :])
            v.drain()
            v.tensor_scalar_add(st[:, RSQ, :], st[:, RSQ, :], 1e-6)
            v.drain()
            v.reciprocal(st[:, RSQ, :], st[:, RSQ, :])
            B.sem_v.dinc(v)
        t3 = B.sem_v.n

        @B.sec("scalar")
        def _(sc):
            sc.wait_ge(B.sem_v.h, t3)
            sc.activation(out=st[:, RSQ, :], in_=st[:, RSQ, :], func=AF.Sqrt)
            B.sem_s.dinc(sc)

    @B.sec("vector")
    def _(v):
        v.wait_ge(B.sem_s.h, B.sem_s.n)
        v.tensor_mul(st[:, MR, :], st[:, M_, :], st[:, RSTD, :])
        B.sem_v.dinc(v)

    if layer == 0:
        B.tap("st8", lambda: st[:, 0:8, :].rearrange("p a b -> p (a b)"),
              B.sem_v, B.sem_v.n)
    ncols = 3 if layer == 0 else 2
    B.bounce_rows(st[:, RSTD:RSTD + ncols, :], ncols, 0)


def p1_xn_acc(B):
    chunk = B.XCH[:]
    xnf = B.fs(0, 2048)
    mr_b = B.fs(2048, 4096)
    r_b = B.BSH
    hh = B.bbc(0, 16384, [128, 8, 2048])
    B.h0 = hh
    xnst = [B.bbb(i * 2048, (i + 1) * 2048) for i in range(2)]

    B.load_bcast(r_b[:], 0)
    B.load_bcast(mr_b, 1)
    bc_tok = B.sem_x.n

    xn_st_tok = [0, 0]
    prev_v = 0
    for fc in range(NFC):
        @B.sec("sync")
        def _(s, fc=fc, pv=prev_v):
            s.wait_ge(B.sem_x.h, bc_tok)
            if pv:
                s.wait_ge(B.sem_v.h, pv)
            B.sem_x.inc(s.dma_start(
                out=chunk,
                in_=B.xT.rearrange("(c p) t -> p c t", p=128)[:, fc, :]), 16)
        ld = B.sem_x.n

        sl = fc % 2

        @B.sec("vector")
        def _(v, fc=fc, ld=ld, sl=sl, stt=xn_st_tok[sl]):
            v.wait_ge(B.sem_x.h, ld)
            if stt:
                v.wait_ge(B.sem_x.h, stt)
            v.tensor_mul(xnf, chunk, r_b[:])
            v.tensor_sub(xnf, xnf, mr_b)
            v.tensor_copy(xnst[sl], xnf)
            v.tensor_scalar_mul(hh[:, fc, :], xnf, B.nw_sb[:, fc:fc + 1])
            v.tensor_add(chunk, chunk, xnf)
            B.sem_v.dinc(v)
        prev_v = B.sem_v.n

        @B.sec("sync")
        def _(s, fc=fc, sl=sl, pv=prev_v):
            s.wait_ge(B.sem_v.h, pv)
            B.sem_x.inc(s.dma_start(
                out=B.xn0_pk.rearrange("(c p) t -> p c t", p=128)[:, fc, :],
                in_=xnst[sl]), 16)
            B.sem_x.inc(s.dma_start(
                out=B.acc0_pk.rearrange("(c p) t -> p c t", p=128)[:, fc, :],
                in_=chunk), 16)
        xn_st_tok[sl] = B.sem_x.n
    B.park_tok = B.sem_x.n
    B.h_ready_tok = B.sem_v.n
    B.tap("h0", lambda: B.h0[:], B.sem_v, B.h_ready_tok, chunks=8)


def mm_proj(B, w_dram, n_oc, out_writer, moving_fn, extra_wait=None,
            n_kc=NFC, nb=NQB, wname="w"):
    pe_start = B.sem_pe.n
    assert n_kc * 128 * 2 <= 2048
    wslots = [B.wb(i * n_kc * 128, (i + 1) * n_kc * 128, [128, n_kc, 128])
              for i in range(2)]
    slot_load = [0, 0]
    slot_use = [pe_start, pe_start]
    for oc in range(n_oc):
        sl = oc % 2

        @B.sec("gpsimd")
        def _(gp, oc=oc, sl=sl, u=slot_use[sl]):
            if u:
                gp.wait_ge(B.sem_pe.h, u)
            B.sem_g.inc(gp.dma_start(out=wslots[sl][:], in_=w_dram[:, oc, :, :]), 16)
        slot_load[sl] = B.sem_g.n

        for b in range(nb):
            bank = (oc * nb + b) % 6

            @B.sec("tensor")
            def _(t, oc=oc, b=b, bank=bank, sl=sl, ld=slot_load[sl]):
                t.wait_ge(B.sem_g.h, ld)
                if extra_wait is not None:
                    t.wait_ge(extra_wait[0].h, extra_wait[1])
                B.pe_guard(t, [bank])
                ins = None
                for kc in range(n_kc):
                    ins = t.matmul(B.PS[:, bank, :], wslots[sl][:, kc, :],
                                   moving_fn(kc, b),
                                   start=(kc == 0), stop=(kc == n_kc - 1))
                B.sem_pe.inc(ins)
            slot_use[sl] = B.sem_pe.n
            out_writer(oc, b, bank, B.sem_pe.n)


def p2_qkv(B):
    PS, st = B.PS, B.st_sb
    hh = B.h0
    q_bf = B.bba(0, 16384, [128, 8, 2048])
    k_bf = B.bba(16384, 24576, [128, 4, 2048])
    v_bf = B.bba(24576, 32768, [128, 16, 512])
    B.q0, B.k0, B.v0 = q_bf, k_bf, v_bf
    h_ready = (B.sem_v, B.h_ready_tok)

    rsq_b = B.BSH
    B.load_bcast(rsq_b[:], 2)
    rsq_tok = B.sem_x.n
    B.tap("rsqb", lambda: rsq_b[:], B.sem_x, rsq_tok)

    def evac_rsq(dst):
        def w(oc, b, bank, pe_tok):
            @B.sec("vector")
            def _(v, oc=oc, b=b, bank=bank, pe_tok=pe_tok):
                v.wait_ge(B.sem_pe.h, pe_tok)
                v.wait_ge(B.sem_x.h, rsq_tok)
                v.tensor_mul(dst(oc, b), PS[:, bank, :],
                                         rsq_b[:, b * 512:(b + 1) * 512])
                B.sem_v.dinc(v)
            B.bank_done([bank], B.sem_v)
        return w

    mm_proj(B, B.wq_t, 8,
            evac_rsq(lambda oc, b: q_bf[:, oc, b * 512:(b + 1) * 512]),
            lambda kc, b: hh[:, kc, b * 512:(b + 1) * 512],
            extra_wait=h_ready, wname="wq")
    mm_proj(B, B.wk_t, 4,
            evac_rsq(lambda oc, b: k_bf[:, oc, b * 512:(b + 1) * 512]),
            lambda kc, b: hh[:, kc, b * 512:(b + 1) * 512],
            extra_wait=h_ready, wname="wk")

    if getattr(B, "stop_after", None) == "min2":
        B.trq_tok = B.sem_v.n
        B.khat_tok = B.sem_v.n
        return
    wv_sb = B.bbb(14336, 18432, [128, 8, 512])
    wv_use = B.sem_pe.n

    @B.sec("gpsimd")
    def _(gp, u=wv_use):
        gp.wait_ge(B.sem_pe.h, u)
        B.sem_g.inc(gp.dma_start(out=wv_sb[:], in_=B.wv_m[:]), 16)
    wv_tok = B.sem_g.n

    for tt in range(NQT):
        bank = tt % 4

        @B.sec("tensor")
        def _(t, tt=tt, bank=bank):
            t.wait_ge(B.sem_g.h, wv_tok)
            B.pe_guard(t, [bank])
            ins = None
            for kc in range(NFC):
                ins = t.matmul(PS[:, bank, :], hh[:, kc, tt * 128:(tt + 1) * 128],
                               wv_sb[:, kc, :], start=(kc == 0), stop=(kc == NFC - 1))
            B.sem_pe.inc(ins)
        pe_tok = B.sem_pe.n

        @B.sec("vector")
        def _(v, tt=tt, bank=bank, pe_tok=pe_tok):
            v.wait_ge(B.sem_pe.h, pe_tok)
            v.tensor_scalar_mul(v_bf[:, tt, :], PS[:, bank, :],
                                            B.st_sb[:, RSQ, tt:tt + 1])
            B.sem_v.dinc(v)
        B.bank_done([bank], B.sem_v)
    qkv_tok = B.sem_v.n

    sq2 = B.BSH
    prev_stat_pe = 0
    for (which, n_oc, col0) in (("q", 8, QSQ0), ("k", 4, KSQ0)):
        src = q_bf if which == "q" else k_bf
        for oc in range(n_oc):
            @B.sec("scalar")
            def _(sc, oc=oc, src=src, prev=prev_stat_pe):
                if prev == 0:
                    sc.wait_ge(B.sem_v.h, qkv_tok)
                else:
                    sc.wait_ge(B.sem_pe.h, prev)
                sc.activation(out=sq2[:], in_=src[:, oc, :],
                                          func=AF.Square)
                B.sem_s.dinc(sc)
            sq_tok = B.sem_s.n

            @B.sec("tensor")
            def _(t, sq_tok=sq_tok):
                t.wait_ge(B.sem_s.h, sq_tok)
                B.pe_guard(t, [6])
                ins = None
                for qt in range(NQT):
                    ins = t.matmul(PS[:, 6, qt:qt + 1],
                                   sq2[:, qt * 128:(qt + 1) * 128], B.ones_sb[:],
                                   start=True, stop=True)
                B.sem_pe.inc(ins)
            prev_stat_pe = B.sem_pe.n

            @B.sec("vector")
            def _(v, oc=oc, col0=col0, pe_tok=prev_stat_pe):
                v.wait_ge(B.sem_pe.h, pe_tok)
                col = col0 + oc // 2
                if oc % 2 == 0:
                    v.tensor_copy(st[:, col, :], PS[:, 6, 0:16])
                    B.sem_v.dinc(v)
                else:
                    v.tensor_add(st[:, col, :], st[:, col, :],
                                             PS[:, 6, 0:16])
                    B.sem_v.dinc(v)
            B.bank_done([6], B.sem_v)
    nsq_tok = B.sem_v.n

    @B.sec("scalar")
    def _(sc):
        sc.wait_ge(B.sem_v.h, nsq_tok)
        a = st[:, QSQ0:QSQ0 + 6, :].rearrange("p a b -> p (a b)")
        sc.activation(out=a, in_=a, func=AF.Sqrt)
        B.sem_s.dinc(sc)
    sq_tok = B.sem_s.n

    @B.sec("vector")
    def _(v):
        v.wait_ge(B.sem_s.h, sq_tok)
        v.wait_ge(B.sem_v.h, B.negt_tok)
        a = st[:, QSQ0:QSQ0 + 6, :].rearrange("p a b -> p (a b)")
        v.tensor_scalar_max(a, a, 1e-12)
        v.drain()
        v.reciprocal(a, a)
        v.drain()
        qa = st[:, QSQ0:QSQ0 + 4, :].rearrange("p a b -> p (a b)")
        v.tensor_scalar_mul(qa, qa, B.tco[:, 1:2])
        B.sem_v.dinc(v)
    B.trq_tok = B.sem_v.n

    B.bounce_rows(st[:, KSQ0:KSQ0 + 2, :], 2, 3)
    for kv in range(2):
        rk_b = B.BSH
        B.load_bcast(rk_b[:], 3 + kv)
        bt = B.sem_x.n

        @B.sec("vector")
        def _(v, kv=kv, bt=bt):
            v.wait_ge(B.sem_x.h, bt)
            for j in range(2):
                v.tensor_mul(k_bf[:, 2 * kv + j, :],
                                         k_bf[:, 2 * kv + j, :], rk_b[:])
                B.sem_v.dinc(v)
    B.khat_tok = B.sem_v.n
    B.tap("q0", lambda: B.q0[:], B.sem_v, qkv_tok, chunks=8)
    B.tap("k0", lambda: B.k0[:], B.sem_v, B.khat_tok, chunks=4)
    B.tap("v0", lambda: B.v0[:], B.sem_v, qkv_tok, chunks=16)


def p2_attn(B):
    PS, PSrow, st = B.PS, B.PSrow, B.st_sb
    q_bf, k_bf, v_bf = B.q0, B.k0, B.v0
    Pt = [B.bbb(i * 2048, (i + 1) * 2048) for i in range(2)]
    PT = B.bbb(4096, 12288, [128, 16, 512])
    attn = B.bbc(0, 16384, [128, 8, 2048])
    B.attn0 = attn

    p_free = [0, 0]
    prev_pv = 0
    first = True
    only1 = getattr(B, "stop_after", None) in ("s00", "nostats", "min2")
    for h in range(1 if only1 else 4):
        kv = h // 2
        for qb in range(1 if only1 else NQB):
            nkc = 4 * qb + 4
            tr_last = 0
            for qi in range(1 if only1 else 4):
                qt = 4 * qb + qi
                nb = qt // 4 + 1
                psl = qt % 2
                if nb == 1:
                    off = qi % 4
                elif nb == 2:
                    off = (qi % 2) * 2
                else:
                    off = 0

                @B.sec("tensor")
                def _(t, qt=qt, nb=nb, off=off, h=h, kv=kv, first=first,
                      bar=(B.sem_x.n, B.sem_w.n, B.sem_t.n,
                           B.sem_s.n, B.sem_g.n)):
                    if first:
                        t.wait_ge(B.sem_v.h, B.khat_tok)
                        dbgbar = getattr(B, "dbg_bar", "")
                        for nm, sem, cnt in (("x", B.sem_x, bar[0]),
                                             ("w", B.sem_w, bar[1]),
                                             ("t", B.sem_t, bar[2]),
                                             ("s", B.sem_s, bar[3]),
                                             ("g", B.sem_g, bar[4])):
                            if nm in dbgbar:
                                t.wait_ge(sem.h, cnt)
                    B.pe_guard(t, list(range(off, off + nb)))
                    ins = None
                    for b in range(nb):
                        for j in range(2):
                            ins = t.matmul(PS[:, off + b, :],
                                           q_bf[:, 2 * h + j, qt * 128:(qt + 1) * 128],
                                           k_bf[:, 2 * kv + j, b * 512:(b + 1) * 512],
                                           start=(j == 0), stop=(j == 1))
                    B.sem_pe.inc(ins)
                s_tok = B.sem_pe.n

                if h == 0 and qb == 0 and qi == 0 and "S00pre" in B.dbg:
                    sstg2 = B.fs(3584, 4096)

                    @B.sec("vector")
                    def _(v, tok=s_tok):
                        v.wait_ge(B.sem_pe.h, tok)
                        v.tensor_copy(sstg2, PS[:, 0, :])
                        B.sem_v.dinc(v)
                    B.tap("S00pre", lambda: sstg2, B.sem_v, B.sem_v.n)

                @B.sec("vector")
                def _(v, qt=qt, nb=nb, off=off, tok=s_tok):
                    v.wait_ge(B.sem_pe.h, tok)
                    v.tensor_add(PS[:, off + nb - 1, :], PS[:, off + nb - 1, :],
                                 B.cmask[:, qt % 4, :])
                    B.sem_v.dinc(v)
                m_tok = B.sem_v.n
                if h == 0 and qb == 0 and qi == 0 and "S00" in B.dbg:
                    sstg = B.fs(3072, 3584)

                    @B.sec("vector")
                    def _(v):
                        v.tensor_copy(sstg, PS[:, 0, :])
                        B.sem_v.dinc(v)
                    m_tok = B.sem_v.n
                    B.tap("S00", lambda: sstg, B.sem_v, m_tok)

                @B.sec("scalar")
                def _(sc, qt=qt, qi=qi, psl=psl, nb=nb, off=off, h=h,
                      tok=m_tok, pp=p_free[psl]):
                    sc.wait_ge(B.sem_v.h, tok)
                    if pp:
                        sc.wait_ge(B.sem_t.h, pp)
                    sc.activation(
                        out=Pt[psl][:, 0:nb * 512],
                        in_=PSrow[:, off * 512:(off + nb) * 512],
                        func=AF.Exp, scale=st[:, QSQ0 + h, qt:qt + 1],
                        bias=B.tco[:, 2:3],
                        accum_out=st[:, RS0, qi:qi + 1])
                    B.sem_s.dinc(sc)
                e_tok = B.sem_s.n
                B.bank_done(list(range(off, off + nb)), B.sem_s)
                first = False

                @B.sec("vector")
                def _(v, qi=qi, psl=psl, nb=nb, tok=e_tok):
                    v.wait_ge(B.sem_s.h, tok)
                    v.reciprocal(st[:, RS0, qi:qi + 1], st[:, RS0, qi:qi + 1])
                    v.drain()
                    v.tensor_scalar_mul(Pt[psl][:, 0:nb * 512],
                                        Pt[psl][:, 0:nb * 512],
                                        st[:, RS0, qi:qi + 1])
                    B.sem_v.dinc(v)
                n_tok = B.sem_v.n

                @B.sec("vector")
                def _(v, qt=qt, qi=qi, nkc=nkc, pv=prev_pv):
                    if qi == 0 and pv:
                        v.wait_ge(B.sem_pe.h, pv)
                    for kc in range(qt + 1, nkc):
                        v.memset(PT[:, kc, qi * 128:(qi + 1) * 128], 0.0)
                    B.sem_v.dinc(v)
                z_tok = B.sem_v.n

                @B.sec("sync")
                def _(s, qt=qt, qi=qi, psl=psl, zt=z_tok, nt=n_tok):
                    s.wait_ge(B.sem_v.h, zt)
                    s.wait_ge(B.sem_v.h, nt)
                    B.sem_t.inc(s.dma_start_transpose(
                        PT[:, 0:qt + 1, qi * 128:(qi + 1) * 128],
                        Pt[psl][:, 0:(qt + 1) * 128]), 16)
                p_free[psl] = B.sem_t.n
                tr_last = B.sem_t.n
                if h == 0 and qb == 0 and qi == 0:
                    B.tap("P00e", lambda: Pt[0][:, 0:512], B.sem_s, e_tok)
                    B.tap("rs00", lambda: B.st_sb[:, RS0, :], B.sem_s, e_tok)
                    B.tap("trq2", lambda: B.st_sb[:, QSQ0:QSQ0 + 6, :]
                          .rearrange("p a b -> p (a b)"), B.sem_v, B.trq_tok)
                    B.tap("P00", lambda: Pt[0][:, 0:512], B.sem_v, n_tok)
                    B.tap("PT00", lambda: PT[:, 0, 0:128], B.sem_t, tr_last)
                    B.tap("rs0", lambda: B.st_sb[:, RS0, :], B.sem_s, e_tok)
                    B.tap("trq", lambda: B.st_sb[:, QSQ0:QSQ0 + 6, :]
                          .rearrange("p a b -> p (a b)"), B.sem_v, B.trq_tok)
                    B.tap("tco", lambda: B.tco[:, :], B.sem_v, B.negt_tok)
                    B.tap("cmask", lambda: B.cmask.rearrange("p a b -> p (a b)"),
                          B.sem_w, B.consts_tok)

            for vc in range(0 if only1 else 2):
                bank = 4 + (2 * qb + vc) % 2

                @B.sec("tensor")
                def _(t, h=h, qb=qb, vc=vc, bank=bank, nkc=nkc, tt=tr_last):
                    t.wait_ge(B.sem_t.h, tt)
                    B.pe_guard(t, [bank])
                    ins = None
                    for kc in range(nkc):
                        ins = t.matmul(PS[:, bank, :],
                                       v_bf[:, kc, vc * 128:(vc + 1) * 128],
                                       PT[:, kc, :],
                                       start=(kc == 0), stop=(kc == nkc - 1))
                    B.sem_pe.inc(ins)
                pv_tok = B.sem_pe.n
                prev_pv = pv_tok

                @B.sec("vector")
                def _(v, h=h, qb=qb, vc=vc, bank=bank, tok=pv_tok):
                    v.wait_ge(B.sem_pe.h, tok)
                    v.tensor_copy(
                        attn[:, 2 * h + vc, qb * 512:(qb + 1) * 512], PS[:, bank, :])
                    B.sem_v.dinc(v)
                B.bank_done([bank], B.sem_v)
    B.attn_tok = B.sem_v.n
    B.tap("attn0", lambda: B.attn0[:], B.sem_v, B.attn_tok, chunks=8)
    B.tap("k0b", lambda: B.k0[:], B.sem_v, B.attn_tok, chunks=4)
    B.tap("q0b", lambda: B.q0[:], B.sem_v, B.attn_tok, chunks=8)


def p3_wo_park(B):
    attn = B.attn0
    stg = [B.bbb(12288 + i * 512, 12288 + (i + 1) * 512) for i in range(4)]
    aw = (B.sem_v, B.attn_tok)
    stage_tok = [0, 0, 0, 0]

    def writer(oc, b, bank, pe_tok):
        sl = (oc * NQB + b) % 4

        @B.sec("vector")
        def _(v, bank=bank, pe_tok=pe_tok, sl=sl, stt=stage_tok[sl]):
            v.wait_ge(B.sem_pe.h, pe_tok)
            if stt:
                v.wait_ge(B.sem_x.h, stt)
            v.tensor_scalar_mul(stg[sl], B.PS[:, bank, :],
                                            B.gate_sb[:, 0:1])
            B.sem_v.dinc(v)
        B.bank_done([bank], B.sem_v)
        ev = B.sem_v.n

        @B.sec("sync")
        def _(s, oc=oc, b=b, sl=sl, ev=ev):
            s.wait_ge(B.sem_v.h, ev)
            B.sem_x.inc(s.dma_start(
                out=B.attn0_pk.rearrange("(c p) t -> p c t", p=128)[
                    :, oc, b * 512:(b + 1) * 512],
                in_=stg[sl]), 16)
        stage_tok[sl] = B.sem_x.n

    mm_proj(B, B.wo_t, 8, writer,
            lambda kc, b: attn[:, kc, b * 512:(b + 1) * 512],
            extra_wait=aw, wname="wo")
    B.attn_park_tok = B.sem_x.n


def p4_ffn0(B):
    PS = B.PS
    pe_entry = B.sem_pe.n
    hblk = B.bba(0, 16384, [128, 32, 512])
    gblk = B.bbb(0, 16384, [128, 32, 512])
    xnb = B.bbc(0, 4096, [128, 8, 512])
    attst = [B.bbc(4096 + i * 512, 4096 + (i + 1) * 512) for i in range(2)]
    silu = [B.fs(i * 512, (i + 1) * 512) for i in range(2)]
    accst = [B.fs(1024 + i * 512, 1024 + (i + 1) * 512) for i in range(2)]
    outst = [B.fs(2048 + i * 512, 2048 + (i + 1) * 512) for i in range(2)]
    w12s = [B.bbc(5120 + i * 1024, 5120 + (i + 1) * 1024, [128, 8, 128])
            for i in range(4)]
    h1s = [B.bbb(16384 + i * 4096, 16384 + (i + 1) * 4096, [128, 32, 128])
           for i in range(2)]

    w12_load = [0] * 4
    w12_use = [pe_entry] * 4
    h1_load = [0, 0]
    h1_use = [pe_entry, pe_entry]
    stage_tok = [0, 0]
    att_tok = [0, 0]
    xnb_prev_pe = pe_entry
    for bt in range(4):
        @B.sec("sync")
        def _(s, bt=bt, pp=xnb_prev_pe):
            s.wait_ge(B.sem_x.h, B.park_tok)
            s.wait_ge(B.sem_pe.h, pp)
            B.sem_x.inc(s.dma_start(
                out=xnb[:],
                in_=B.xn0_pk.rearrange("(c p) t -> p c t", p=128)[
                    :, :, bt * 512:(bt + 1) * 512]), 16)
        xnb_tok = B.sem_x.n

        for oc in range(32):
            for wi, wt in ((0, B.w1_t), (1, B.w2_t)):
                sl = wi * 2 + oc % 2

                @B.sec("gpsimd")
                def _(gp, oc=oc, sl=sl, wt=wt, u=w12_use[sl]):
                    if u:
                        gp.wait_ge(B.sem_pe.h, u)
                    B.sem_g.inc(gp.dma_start(out=w12s[sl][:],
                                             in_=wt[:, oc, :, :]), 16)
                w12_load[sl] = B.sem_g.n
            bank1 = (2 * oc) % 4
            bank2 = bank1 + 1
            sl1, sl2 = oc % 2, 2 + oc % 2

            @B.sec("tensor")
            def _(t, bank1=bank1, bank2=bank2, sl1=sl1, sl2=sl2,
                  xt=xnb_tok, l1=w12_load[sl1], l2=w12_load[sl2]):
                t.wait_ge(B.sem_g.h, max(l1, l2))
                t.wait_ge(B.sem_x.h, xt)
                B.pe_guard(t, [bank1, bank2])
                for kc in range(NFC):
                    t.matmul(PS[:, bank1, :], w12s[sl1][:, kc, :], xnb[:, kc, :],
                             start=(kc == 0), stop=(kc == NFC - 1))
                ins = None
                for kc in range(NFC):
                    ins = t.matmul(PS[:, bank2, :], w12s[sl2][:, kc, :],
                                   xnb[:, kc, :],
                                   start=(kc == 0), stop=(kc == NFC - 1))
                B.sem_pe.inc(ins)
            pe_tok = B.sem_pe.n
            w12_use[sl1] = pe_tok
            w12_use[sl2] = pe_tok
            ssl = oc % 2

            @B.sec("scalar")
            def _(sc, oc=oc, bank1=bank1, ssl=ssl, tok=pe_tok):
                sc.wait_ge(B.sem_pe.h, tok)
                sc.activation(out=silu[ssl], in_=PS[:, bank1, :],
                                          func=AF.Silu,
                                          scale=B.s1_sb[:, oc:oc + 1])
                B.sem_s.dinc(sc)
            si_tok = B.sem_s.n

            @B.sec("vector")
            def _(v, oc=oc, bank2=bank2, ssl=ssl, tok=si_tok):
                v.wait_ge(B.sem_s.h, tok)
                v.scalar_tensor_tensor(
                    hblk[:, oc, :], PS[:, bank2, :], B.s2_sb[:, oc:oc + 1],
                    silu[ssl], ALU.mult, ALU.mult)
                B.sem_v.dinc(v)
            B.bank_done([bank1], B.sem_s)
            B.bank_done([bank2], B.sem_v)
        h_tok = B.sem_v.n

        def big_mm(wt, src, dst_writer, n_oc, in_tok):
            for oc in range(n_oc):
                sl = oc % 2

                @B.sec("gpsimd")
                def _(gp, oc=oc, sl=sl, wt=wt, u=h1_use[sl]):
                    gp.wait_ge(B.sem_pe.h, u)
                    B.sem_g.inc(gp.dma_start(out=h1s[sl][:],
                                             in_=wt[:, oc, :, :]), 16)
                h1_load[sl] = B.sem_g.n
                bank = oc % 4

                @B.sec("tensor")
                def _(t, oc=oc, bank=bank, sl=sl, ld=h1_load[sl]):
                    t.wait_ge(B.sem_g.h, ld)
                    t.wait_ge(B.sem_v.h, in_tok)
                    B.pe_guard(t, [bank])
                    ins = None
                    for kc in range(32):
                        ins = t.matmul(PS[:, bank, :], h1s[sl][:, kc, :],
                                       src[:, kc, :],
                                       start=(kc == 0), stop=(kc == 31))
                    B.sem_pe.inc(ins)
                h1_use[sl] = B.sem_pe.n
                dst_writer(oc, bank, B.sem_pe.n)

        def gelu_writer(dst, ssb):
            def w(oc, bank, pe_tok):
                @B.sec("scalar")
                def _(sc, oc=oc, bank=bank, tok=pe_tok):
                    sc.wait_ge(B.sem_pe.h, tok)
                    sc.activation(
                        out=dst[:, oc, :], in_=PS[:, bank, :],
                        func=AF.Gelu, scale=ssb[:, oc:oc + 1])
                    B.sem_s.dinc(sc)
                B.bank_done([bank], B.sem_s)
            return w

        big_mm(B.h1_t, hblk, gelu_writer(gblk, B.sh1_sb), 32, h_tok)
        g_tok = _v_marker(B, B.sem_s, B.sem_s.n)
        big_mm(B.h2_t, gblk, gelu_writer(hblk, B.sh2_sb), 32, g_tok)
        g2_tok = _v_marker(B, B.sem_s, B.sem_s.n)

        def w3_writer(oc, bank, pe_tok):
            sl = oc % 2

            @B.sec("sync")
            def _(s, oc=oc, sl=sl, bt=bt, stt=stage_tok[sl], at=att_tok[sl]):
                if stt:
                    s.wait_ge(B.sem_v.h, stt)
                if at:
                    s.wait_ge(B.sem_v.h, at)
                col = bt * 512
                B.sem_x.inc(s.dma_start(
                    out=accst[sl],
                    in_=B.acc0_pk.rearrange("(c p) t -> p c t", p=128)[
                        :, oc, col:col + 512]), 16)
                B.sem_x.inc(s.dma_start(
                    out=attst[sl],
                    in_=B.attn0_pk.rearrange("(c p) t -> p c t", p=128)[
                        :, oc, col:col + 512]), 16)
            rb_tok = B.sem_x.n

            @B.sec("vector")
            def _(v, oc=oc, bank=bank, sl=sl, tok=pe_tok, rb=rb_tok):
                v.wait_ge(B.sem_pe.h, tok)
                v.wait_ge(B.sem_x.h, rb)
                v.scalar_tensor_tensor(outst[sl], PS[:, bank, :],
                                       B.s3_sb[:, oc:oc + 1], attst[sl],
                                       ALU.mult, ALU.add)
                v.tensor_add(outst[sl], outst[sl], accst[sl])
                B.sem_v.dinc(v)
            B.bank_done([bank], B.sem_v)
            ev = B.sem_v.n
            stage_tok[sl] = ev
            att_tok[sl] = ev

            @B.sec("sync")
            def _(s, oc=oc, sl=sl, bt=bt, ev=ev):
                s.wait_ge(B.sem_v.h, ev)
                col = bt * 512
                B.sem_x.inc(s.dma_start(
                    out=B.x1_pk.rearrange("(c p) t -> p c t", p=128)[
                        :, oc, col:col + 512],
                    in_=outst[sl]), 16)

        big_mm(B.w3_t, hblk, w3_writer, 8, g2_tok)
        xnb_prev_pe = B.sem_pe.n
    B.x1_tok = B.sem_x.n


def collective_max(B, src_col, ncols, wait_sem, wait_tok):
    PS = B.PS

    @B.sec("tensor")
    def _(t):
        t.wait_ge(wait_sem.h, wait_tok)
        B.pe_guard(t, [6])
        t.transpose(PS[0:ncols, 6, 0:128],
                    B.red_sb[:, src_col:src_col + ncols], B.ident[:])
        B.sem_pe.dinc(t)
    tr = B.sem_pe.n

    @B.sec("vector")
    def _(v):
        v.wait_ge(B.sem_pe.h, tr)
        v.memset(B.rowst[0:4, 0:1], 0.0)
        v.tensor_reduce(B.rowst[0:ncols, 0:1],
                        PS[0:ncols, 6, 0:128], AX.X, ALU.max)
        B.sem_v.dinc(v)
    B.bank_done([6], B.sem_v)
    red = B.sem_v.n

    @B.sec("gpsimd")
    def _(gp):
        gp.wait_ge(B.sem_v.h, red)
        B.sem_g.inc(gp.dma_start(out=B.cc_in[0:1, 0:4],
                                 in_=B.rowst[0:4, 0:1]), 16)
        gp.wait_ge(B.sem_g.h, B.sem_g.n)
        B.sem_cc.inc(gp.collective_compute(
            "AllReduce", ALU.max, replica_groups=[list(range(B.n_cores))],
            ins=[B.cc_in.opt()], outs=[B.cc_out.opt()]))
    cc_tok = B.sem_cc.n

    @B.sec("sync")
    def _(s):
        s.wait_ge(B.sem_cc.h, cc_tok)
        B.sem_x.inc(s.dma_start(
            out=B.scal_sb[:, SC_CC0:SC_CC0 + ncols],
            in_=B.cc_out[0:1, 0:ncols].broadcast_to([128, ncols])), 16)
    return B.sem_x.n


def p5_ln1(B):
    ln_stats(B, B.x1_pk, layer=1, xwait=(B.sem_x, B.x1_tok))

    xn1 = B.bba(0, 16384, [128, 8, 2048])
    B.xn1 = xn1
    chunk = B.XCH[:]
    xnf = B.fs(0, 2048)
    mr_b = B.fs(2048, 4096)
    r_b = B.BSH
    B.load_bcast(r_b[:], 0)
    B.load_bcast(mr_b, 1)
    bc_tok = B.sem_x.n

    pe_now = B.sem_pe.n
    prev_v = 0
    for fc in range(NFC):
        @B.sec("sync")
        def _(s, fc=fc, pv=prev_v):
            s.wait_ge(B.sem_x.h, bc_tok)
            if pv:
                s.wait_ge(B.sem_v.h, pv)
            B.sem_x.inc(s.dma_start(
                out=chunk,
                in_=B.x1_pk.rearrange("(c p) t -> p c t", p=128)[:, fc, :]), 16)
        ld = B.sem_x.n

        @B.sec("vector")
        def _(v, fc=fc, ld=ld, pe=pe_now):
            v.wait_ge(B.sem_x.h, ld)
            if fc == 0:
                v.wait_ge(B.sem_pe.h, pe)
            v.tensor_mul(xnf, chunk, r_b[:])
            v.tensor_sub(xnf, xnf, mr_b)
            v.tensor_copy(xn1[:, fc, :], xnf)
            v.tensor_reduce(B.red_sb[:, 7:8], xnf, AX.X, ALU.max,
                            apply_absolute_value=True)
            v.drain()
            if fc == 0:
                v.tensor_copy(B.red_sb[:, 0:1], B.red_sb[:, 7:8])
            else:
                v.tensor_max(B.red_sb[:, 0:1], B.red_sb[:, 0:1],
                             B.red_sb[:, 7:8])
            B.sem_v.dinc(v)
        prev_v = B.sem_v.n
    B.xn1_tok = B.sem_v.n
    B.tap("xn1", lambda: B.xn1[:], B.sem_v, B.xn1_tok, chunks=8)

    cx = collective_max(B, 0, 1, B.sem_v, B.xn1_tok)

    @B.sec("vector")
    def _(v):
        v.wait_ge(B.sem_x.h, cx)
        v.tensor_scalar(B.scal_sb[:, SC_SX:SC_SX + 1],
                        B.scal_sb[:, SC_CC0:SC_CC0 + 1],
                        1.0 / QMAX16, 1e-12, ALU.mult, ALU.max)
        v.drain()
        v.reciprocal(B.scal_sb[:, SC_IX:SC_IX + 1],
                     B.scal_sb[:, SC_SX:SC_SX + 1])
        B.sem_v.dinc(v)
    B.sx_tok = B.sem_v.n


def p6_attn1(B):
    PS, PSrow = B.PS, B.PSrow
    xn1 = B.xn1
    k1 = B.bbb(0, 16384, [128, 8, 2048])
    v1 = B.bba(16384, 32768, [128, 16, 1024])
    xw = (B.sem_v, B.xn1_tok)

    qstg = [B.bbb(16384 + i * 512, 16384 + (i + 1) * 512) for i in range(4)]
    qstg_tok = [0, 0, 0, 0]

    def q_writer(oc, b, bank, pe_tok):
        sl = (oc * NQB + b) % 4

        @B.sec("vector")
        def _(v, oc=oc, bank=bank, sl=sl, pe_tok=pe_tok, stt=qstg_tok[sl]):
            v.wait_ge(B.sem_pe.h, pe_tok)
            if stt:
                v.wait_ge(B.sem_x.h, stt)
            v.tensor_scalar_add(qstg[sl], B.PS[:, bank, :],
                                            B.bq_sb[:, oc:oc + 1])
            B.sem_v.dinc(v)
        B.bank_done([bank], B.sem_v)
        ev = B.sem_v.n

        @B.sec("sync")
        def _(s, oc=oc, b=b, sl=sl, ev=ev):
            s.wait_ge(B.sem_v.h, ev)
            B.sem_x.inc(s.dma_start(
                out=B.q1_pk.rearrange("(c p) t -> p c t", p=128)[
                    :, oc, b * 512:(b + 1) * 512],
                in_=qstg[sl]), 16)
        qstg_tok[sl] = B.sem_x.n

    mm_proj(B, B.qkvq_t, 8, q_writer,
            lambda kc, b: xn1[:, kc, b * 512:(b + 1) * 512],
            extra_wait=xw, wname="q1")

    def k_writer(oc, b, bank, pe_tok):
        @B.sec("vector")
        def _(v, oc=oc, b=b, bank=bank, pe_tok=pe_tok):
            v.wait_ge(B.sem_pe.h, pe_tok)
            v.tensor_scalar_add(
                k1[:, oc, b * 512:(b + 1) * 512], B.PS[:, bank, :],
                B.bk_sb[:, oc:oc + 1])
            B.sem_v.dinc(v)
        B.bank_done([bank], B.sem_v)

    mm_proj(B, B.qkvk_t, 8, k_writer,
            lambda kc, b: xn1[:, kc, b * 512:(b + 1) * 512],
            extra_wait=xw, wname="k1")

    wv1 = B.bbb(20480, 24576, [128, 8, 512])
    wv1_use = B.sem_pe.n
    for half in range(2):
        @B.sec("gpsimd")
        def _(gp, half=half, u=wv1_use):
            gp.wait_ge(B.sem_pe.h, u)
            B.sem_g.inc(gp.dma_start(out=wv1[:],
                                     in_=B.qkvv_m[:, :, half * 512:(half + 1) * 512]),
                        16)
        wv_tok = B.sem_g.n

        for tt in range(NQT):
            bank = tt % 4

            @B.sec("tensor")
            def _(t, tt=tt, bank=bank, wv_tok=wv_tok):
                t.wait_ge(B.sem_g.h, wv_tok)
                B.pe_guard(t, [bank])
                ins = None
                for kc in range(NFC):
                    ins = t.matmul(PS[:, bank, :],
                                   xn1[:, kc, tt * 128:(tt + 1) * 128],
                                   wv1[:, kc, :], start=(kc == 0),
                                   stop=(kc == NFC - 1))
                B.sem_pe.inc(ins)
            pe_tok = B.sem_pe.n
            wv1_use = pe_tok

            @B.sec("vector")
            def _(v, tt=tt, bank=bank, half=half, pe_tok=pe_tok):
                v.wait_ge(B.sem_pe.h, pe_tok)
                if tt == 0 and half == 0:
                    v.wait_ge(B.sem_g.h, B.bv_tok)
                v.tensor_add(v1[:, tt, half * 512:(half + 1) * 512],
                                         PS[:, bank, :],
                                         B.bv_sb[:, half * 512:(half + 1) * 512])
                B.sem_v.dinc(v)
            B.bank_done([bank], B.sem_v)
    B.qkv1_tok = B.sem_v.n

    xqst = [B.bbc(i * 2048, (i + 1) * 2048) for i in range(2)]
    fscr = B.BSH
    xq_st_tok = [0, 0]
    prev_a = 0
    for fc in range(NFC):
        @B.sec("scalar")
        def _(sc, fc=fc, pa=prev_a):
            if fc == 0:
                sc.wait_ge(B.sem_v.h, B.sx_tok)
            else:
                sc.wait_ge(B.sem_v.h, pa)
            sc.activation(out=fscr[:], in_=xn1[:, fc, :],
                                      func=AF.Identity,
                                      scale=B.scal_sb[:, SC_IX:SC_IX + 1],
                                      bias=B.tco[:, 4:5])
            B.sem_s.dinc(sc)
        a_tok = B.sem_s.n
        sl = fc % 2

        @B.sec("vector")
        def _(v, sl=sl, a_tok=a_tok, pstt=xq_st_tok[sl]):
            v.wait_ge(B.sem_s.h, a_tok)
            if pstt:
                v.wait_ge(B.sem_x.h, pstt)
            v.tensor_scalar_sub(fscr[:], fscr[:], MAGIC)
            v.tensor_scalar_mul(xqst[sl], fscr[:],
                                            B.scal_sb[:, SC_SX:SC_SX + 1])
            B.sem_v.dinc(v)
        prev_a = B.sem_v.n
        r_tok = B.sem_v.n

        @B.sec("sync")
        def _(s, fc=fc, sl=sl, r_tok=r_tok):
            s.wait_ge(B.sem_v.h, r_tok)
            B.sem_x.inc(s.dma_start(
                out=B.xq_pk.rearrange("(c p) t -> p c t", p=128)[:, fc, :],
                in_=xqst[sl]), 16)
        xq_st_tok[sl] = B.sem_x.n
    B.xq_park_tok = B.sem_x.n

    q1s = [B.bbb(16384 + i * 2048, 16384 + (i + 1) * 2048) for i in range(2)]
    Pc = [B.bbb(20480 + i * 2048, 20480 + (i + 1) * 2048) for i in range(2)]
    PT1 = B.bbb(24576, 32768, [128, 16, 512])
    pstg = [B.bbc(4096 + i * 512, 4096 + (i + 1) * 512) for i in range(4)]
    red = B.red_sb
    attn_pe0 = B.sem_pe.n
    q1_free = [attn_pe0, attn_pe0]
    p_free = [0, 0]
    prev_pv = 0
    first = True
    park_toks = [0, 0, 0, 0]
    for h in range(16):
        hp, hc = h % 2, h // 2
        qsl = h % 2

        @B.sec("sync")
        def _(s, hp=hp, hc=hc, qsl=qsl, u=q1_free[qsl]):
            s.wait_ge(B.sem_x.h, B.xq_park_tok)
            s.wait_ge(B.sem_pe.h, u)
            B.sem_x.inc(s.dma_start(
                out=q1s[qsl][hp * 64:(hp + 1) * 64, :],
                in_=B.q1_pk.rearrange("(c p) t -> p c t", p=128)[
                    hp * 64:(hp + 1) * 64, hc, :]), 16)
        q_ld = B.sem_x.n

        for qb in range(NQB):
            tr_last = 0
            for qi in range(4):
                qt = 4 * qb + qi
                psl = qt % 2

                @B.sec("tensor")
                def _(t, qt=qt, hp=hp, hc=hc, qsl=qsl, q_ld=q_ld, first=first):
                    t.wait_ge(B.sem_x.h, q_ld)
                    if first:
                        t.wait_ge(B.sem_v.h, B.qkv1_tok)
                    B.pe_guard(t, [0, 1, 2, 3])
                    ins = None
                    for b in range(4):
                        ins = t.matmul(PS[:, b, :],
                                       q1s[qsl][hp * 64:(hp + 1) * 64,
                                                qt * 128:(qt + 1) * 128],
                                       k1[hp * 64:(hp + 1) * 64, hc,
                                          b * 512:(b + 1) * 512],
                                       start=True, stop=True)
                    B.sem_pe.inc(ins)
                s_tok = B.sem_pe.n

                @B.sec("vector")
                def _(v, tok=s_tok):
                    v.wait_ge(B.sem_pe.h, tok)
                    v.tensor_reduce(red[:, 2:3], PSrow[:, 0:2048], AX.X, ALU.max)
                    v.drain()
                    v.tensor_scalar_mul(red[:, 3:4], red[:, 2:3], -8.0)
                    B.sem_v.dinc(v)
                m_tok = B.sem_v.n

                @B.sec("scalar")
                def _(sc, psl=psl, tok=m_tok, pp=p_free[psl]):
                    sc.wait_ge(B.sem_v.h, tok)
                    if pp:
                        sc.wait_ge(B.sem_t.h, pp)
                    sc.activation(
                        out=Pc[psl][:], in_=PSrow[:, 0:2048], func=AF.Exp,
                        scale=B.tco[:, 3:4], bias=red[:, 3:4],
                        accum_out=red[:, 5:6])
                    B.sem_s.dinc(sc)
                e_tok = B.sem_s.n
                B.bank_done([0, 1, 2, 3], B.sem_s)
                first = False

                @B.sec("vector")
                def _(v, psl=psl, tok=e_tok):
                    v.wait_ge(B.sem_s.h, tok)
                    v.reciprocal(red[:, 5:6], red[:, 5:6])
                    v.drain()
                    v.tensor_scalar_mul(Pc[psl][:], Pc[psl][:], red[:, 5:6])
                    B.sem_v.dinc(v)
                n_tok = B.sem_v.n

                @B.sec("sync")
                def _(s, qi=qi, psl=psl, nt=n_tok, pv=prev_pv):
                    s.wait_ge(B.sem_v.h, nt)
                    if qi == 0 and pv:
                        s.wait_ge(B.sem_pe.h, pv)
                    B.sem_t.inc(s.dma_start_transpose(
                        PT1[:, 0:16, qi * 128:(qi + 1) * 128],
                        Pc[psl][:, 0:2048]), 16)
                p_free[psl] = B.sem_t.n
                tr_last = B.sem_t.n
                
            bank = 4 + (h * 4 + qb) % 2
            sl = (h * 4 + qb) % 4

            @B.sec("tensor")
            def _(t, h=h, bank=bank, tt=tr_last):
                t.wait_ge(B.sem_t.h, tt)
                B.pe_guard(t, [bank])
                ins = None
                for kc in range(16):
                    ins = t.matmul(PS[0:64, bank, :],
                                   v1[:, kc, h * 64:(h + 1) * 64],
                                   PT1[:, kc, :], start=(kc == 0), stop=(kc == 15))
                B.sem_pe.inc(ins)
            pv_tok = B.sem_pe.n
            prev_pv = pv_tok
            q1_free[qsl] = pv_tok

            @B.sec("vector")
            def _(v, bank=bank, sl=sl, tok=pv_tok, pk=park_toks[sl]):
                v.wait_ge(B.sem_pe.h, tok)
                if pk:
                    v.wait_ge(B.sem_x.h, pk)
                v.tensor_copy(pstg[sl][0:64, :], PS[0:64, bank, :])
                B.sem_v.dinc(v)
            B.bank_done([bank], B.sem_v)
            ev = B.sem_v.n

            @B.sec("sync")
            def _(s, hp=hp, hc=hc, qb=qb, sl=sl, ev=ev):
                s.wait_ge(B.sem_v.h, ev)
                B.sem_x.inc(s.dma_start(
                    out=B.attn1_pk.rearrange("(c p) t -> p c t", p=128)[
                        hp * 64:(hp + 1) * 64, hc, qb * 512:(qb + 1) * 512],
                    in_=pstg[sl][0:64, :]), 16)
            park_toks[sl] = B.sem_x.n
    B.attn1_park = B.sem_x.n

    av = B.bba(0, 16384, [128, 8, 2048])
    last_pe = B.sem_pe.n

    @B.sec("sync")
    def _(s, lp=last_pe):
        s.wait_ge(B.sem_x.h, B.attn1_park)
        s.wait_ge(B.sem_pe.h, lp)
        B.sem_x.inc(s.dma_start(
            out=av[:], in_=B.attn1_pk.rearrange("(c p) t -> p c t", p=128)), 16)
    av_tok = B.sem_x.n

    x1st = [B.fs(i * 512, (i + 1) * 512) for i in range(2)]
    outst = [B.fs(1024 + i * 512, 1024 + (i + 1) * 512) for i in range(2)]
    stage_tok = [0, 0]

    def wp_writer(oc, b, bank, pe_tok):
        sl = (oc * NQB + b) % 2

        @B.sec("sync")
        def _(s, oc=oc, b=b, sl=sl, stt=stage_tok[sl]):
            if stt:
                s.wait_ge(B.sem_v.h, stt)
            B.sem_x.inc(s.dma_start(
                out=x1st[sl],
                in_=B.x1_pk.rearrange("(c p) t -> p c t", p=128)[
                    :, oc, b * 512:(b + 1) * 512]), 16)
        rb_tok = B.sem_x.n

        @B.sec("vector")
        def _(v, oc=oc, bank=bank, sl=sl, tok=pe_tok, rb=rb_tok):
            v.wait_ge(B.sem_pe.h, tok)
            v.wait_ge(B.sem_x.h, rb)
            v.scalar_tensor_tensor(
                outst[sl], B.PS[:, bank, :], B.bp_sb[:, oc:oc + 1],
                x1st[sl], ALU.add, ALU.add)
            B.sem_v.dinc(v)
        B.bank_done([bank], B.sem_v)
        ev = B.sem_v.n
        stage_tok[sl] = ev

        @B.sec("sync")
        def _(s, oc=oc, b=b, sl=sl, ev=ev):
            s.wait_ge(B.sem_v.h, ev)
            B.sem_x.inc(s.dma_start(
                out=B.x1_pk.rearrange("(c p) t -> p c t", p=128)[
                    :, oc, b * 512:(b + 1) * 512],
                in_=outst[sl]), 16)

    mm_proj(B, B.wp_t, 8, wp_writer,
            lambda kc, b: av[:, kc, b * 512:(b + 1) * 512],
            extra_wait=(B.sem_x, av_tok), wname="wp")
    B.acc1_tok = B.sem_x.n


def dequant_slice(B, nat, wqn, nblk, wait_sem, wait_tok, extra=None):
    n = nblk * 32

    @B.sec("vector")
    def _(v):
        v.wait_ge(wait_sem.h, wait_tok)
        if extra is not None and extra[1]:
            v.wait_ge(extra[0].h, extra[1])
        nat3 = nat[:, 0:n].rearrange("p (a b) -> p a b", b=32)
        wqn3 = wqn[:, 0:n].rearrange("p (a b) -> p a b", b=32)
        v.tensor_reduce(B.dqs[:, 0:nblk], nat3, AX.X, ALU.max,
                        apply_absolute_value=True)
        v.drain()
        v.tensor_scalar(B.dqs[:, 0:nblk], B.dqs[:, 0:nblk], 1.0 / QMAX4, 1e-12,
                        ALU.mult, ALU.max)
        v.drain()
        v.reciprocal(B.dqi[:, 0:nblk], B.dqs[:, 0:nblk])
        v.drain()
        v.tensor_mul(nat3, nat3,
                     B.dqi[:, 0:nblk].rearrange("p (a o) -> p a o", o=1)
                     .broadcast_to([128, nblk, 32]))
        v.tensor_scalar(nat[:, 0:n], nat[:, 0:n], MAGIC, MAGIC, ALU.add,
                        ALU.subtract)
        v.tensor_mul(
            wqn3, nat3,
            B.dqs[:, 0:nblk].rearrange("p (a o) -> p a o", o=1)
            .broadcast_to([128, nblk, 32]))
        B.sem_v.dinc(v)
    return B.sem_v.n


def p7_ffn1(B):
    PS = B.PS
    xq = B.bbc(0, 16384, [128, 8, 2048])
    wp_done_pe = B.sem_pe.n

    @B.sec("sync")
    def _(s):
        s.wait_ge(B.sem_pe.h, wp_done_pe)
        s.wait_ge(B.sem_x.h, B.xq_park_tok)
        B.sem_x.inc(s.dma_start(
            out=xq[:], in_=B.xq_pk.rearrange("(c p) t -> p c t", p=128)), 16)
    xq_tok = B.sem_x.n

    nat_s = [B.fs(i * 1024, (i + 1) * 1024) for i in range(2)]
    wqn_s = [B.wb(i * 1024, (i + 1) * 1024) for i in range(2)]
    wqt_s = [B.bbb(5120 + i * 1024, 5120 + (i + 1) * 1024, [128, 8, 128])
             for i in range(2)]
    stg_bf = [B.bbb(i * 512, (i + 1) * 512) for i in range(4)]

    vmark = _v_marker(B, B.sem_pe, wp_done_pe)
    nat_use = [vmark, vmark]
    wqn_use = [(B.sem_v, vmark), (B.sem_v, vmark)]
    wqt_use = [wp_done_pe, wp_done_pe]
    stg_use = [0, 0, 0, 0]
    for wi, (wdram, pk, redcol) in enumerate(
            ((B.wup_n, B.upp_pk, 1), (B.wgate_n, B.gatep_pk, 2))):
        for oc in range(24):
            sl = oc % 2

            @B.sec("sync")
            def _(s, oc=oc, sl=sl, wdram=wdram, u=nat_use[sl]):
                s.wait_ge(B.sem_v.h, u)
                B.sem_w.inc(s.dma_start(out=nat_s[sl],
                                        in_=wdram[oc * 128:(oc + 1) * 128, :]), 16)
            nat_ld = B.sem_w.n

            dq_tok = dequant_slice(B, nat_s[sl], wqn_s[sl], 32,
                                   B.sem_w, nat_ld, extra=wqn_use[sl])
            nat_use[sl] = dq_tok

            @B.sec("sync")
            def _(s, sl=sl, dq=dq_tok, u=wqt_use[sl]):
                s.wait_ge(B.sem_v.h, dq)
                s.wait_ge(B.sem_pe.h, u)
                B.sem_t.inc(s.dma_start_transpose(
                    wqt_s[sl][:, 0:8, :], wqn_s[sl][:, 0:1024]), 16)
            tr_tok = B.sem_t.n
            wqn_use[sl] = (B.sem_t, tr_tok)

            for b in range(NQB):
                bank = (oc * NQB + b) % 4

                @B.sec("tensor")
                def _(t, b=b, bank=bank, sl=sl, tr=tr_tok, wi=wi, oc=oc):
                    t.wait_ge(B.sem_t.h, tr)
                    if oc == 0 and b == 0 and wi == 0:
                        t.wait_ge(B.sem_x.h, xq_tok)
                    B.pe_guard(t, [bank])
                    ins = None
                    for kc in range(8):
                        ins = t.matmul(PS[:, bank, :], wqt_s[sl][:, kc, :],
                                       xq[:, kc, b * 512:(b + 1) * 512],
                                       start=(kc == 0), stop=(kc == 7))
                    B.sem_pe.inc(ins)
                pe_tok = B.sem_pe.n
                wqt_use[sl] = pe_tok
                ssl = (oc * NQB + b) % 4

                @B.sec("vector")
                def _(v, bank=bank, ssl=ssl, tok=pe_tok, rc=redcol,
                      firstc=(oc == 0 and b == 0), pk_tok=stg_use[ssl]):
                    v.wait_ge(B.sem_pe.h, tok)
                    if pk_tok:
                        v.wait_ge(B.sem_x.h, pk_tok)
                    v.tensor_copy(stg_bf[ssl], PS[:, bank, :])
                    v.tensor_reduce(B.red_sb[:, 7:8], PS[:, bank, :], AX.X,
                                    ALU.max, apply_absolute_value=True)
                    v.drain()
                    if firstc:
                        v.tensor_copy(B.red_sb[:, rc:rc + 1], B.red_sb[:, 7:8])
                    else:
                        v.tensor_max(B.red_sb[:, rc:rc + 1],
                                     B.red_sb[:, rc:rc + 1], B.red_sb[:, 7:8])
                    B.sem_v.dinc(v)
                B.bank_done([bank], B.sem_v)
                ev = B.sem_v.n

                @B.sec("sync")
                def _(s, oc=oc, b=b, ssl=ssl, ev=ev, pk=pk):
                    s.wait_ge(B.sem_v.h, ev)
                    B.sem_x.inc(s.dma_start(
                        out=pk.rearrange("(c p) t -> p c t", p=128)[
                            :, oc, b * 512:(b + 1) * 512],
                        in_=stg_bf[ssl]), 16)
                stg_use[ssl] = B.sem_x.n
    upg_tok = B.sem_v.n

    c2 = collective_max(B, 1, 2, B.sem_v, upg_tok)

    @B.sec("vector")
    def _(v):
        v.wait_ge(B.sem_x.h, c2)
        sc = B.scal_sb
        v.tensor_scalar(sc[:, SC_SU:SC_SU + 1], sc[:, SC_CC0:SC_CC0 + 1],
                        1.0 / QMAX16, 1e-12, ALU.mult, ALU.max)
        v.tensor_scalar(sc[:, SC_SG:SC_SG + 1], sc[:, SC_CC0 + 1:SC_CC0 + 2],
                        1.0 / QMAX16, 1e-12, ALU.mult, ALU.max)
        v.drain()
        v.reciprocal(sc[:, SC_IU:SC_IU + 1], sc[:, SC_SU:SC_SU + 1])
        v.drain()
        v.reciprocal(sc[:, SC_IG:SC_IG + 1], sc[:, SC_SG:SC_SG + 1])
        v.tensor_scalar_mul(sc[:, SC_SGC:SC_SGC + 1],
                            sc[:, SC_SG:SC_SG + 1], 1.0 / 65536.0)
        B.sem_v.dinc(v)
    c2s_tok = B.sem_v.n

    ctA = [B.bbb(8192 + i * 2048, 8192 + (i + 1) * 2048) for i in range(2)]
    ctB = [B.bbb(12288 + i * 2048, 12288 + (i + 1) * 2048) for i in range(2)]
    ots = [B.bbb(16384 + i * 2048, 16384 + (i + 1) * 2048) for i in range(2)]
    f0w = B.fs(2048, 4096)
    f1w = B.XCH[:]

    wb_ready = _v_marker(B, B.sem_pe, B.sem_pe.n)

    def chain_pass(emit_tile):
        for oc in range(24):
            emit_tile(oc, oc % 2)

    ldB_use = [wb_ready, wb_ready]
    stB_use = [0, 0]
    f0_guard = [0]

    def tileB(oc, sl):
        @B.sec("sync")
        def _(s, oc=oc, sl=sl, u=ldB_use[sl]):
            s.wait_ge(B.sem_v.h, u)
            B.sem_w.inc(s.dma_start(
                out=ctA[sl],
                in_=B.gatep_pk.rearrange("(c p) t -> p c t", p=128)[:, oc, :]),
                16)
        ld = B.sem_w.n

        @B.sec("vector")
        def _(v, sl=sl, ld=ld, fg=f0_guard[0], first=(oc == 0)):
            v.wait_ge(B.sem_w.h, ld)
            if first:
                v.wait_ge(B.sem_v.h, c2s_tok)
            if fg:
                v.wait_ge(B.sem_s.h, fg)
            v.tensor_scalar_mul(f0w, ctA[sl], B.scal_sb[:, SC_IG:SC_IG + 1])
            v.tensor_scalar_add(f0w, f0w, MAGIC)
            v.tensor_scalar_sub(f0w, f0w, MAGIC)
            B.sem_v.dinc(v)
        r1 = B.sem_v.n
        ldB_use[sl] = r1

        @B.sec("scalar")
        def _(sc_, r1=r1):
            sc_.wait_ge(B.sem_v.h, r1)
            sc_.activation(out=f1w, in_=f0w, func=AF.Sigmoid,
                           scale=B.scal_sb[:, SC_SG:SC_SG + 1])
            B.sem_s.dinc(sc_)
        a2 = B.sem_s.n
        f0_guard[0] = a2

        @B.sec("vector")
        def _(v, sl=sl, a2=a2, u=stB_use[sl], firstc=(oc == 0)):
            v.wait_ge(B.sem_s.h, a2)
            if u:
                v.wait_ge(B.sem_x.h, u)
            v.tensor_scalar(f1w, f1w, 65536.0, MAGIC, ALU.mult, ALU.add)
            v.tensor_scalar(f1w, f1w, MAGIC, 65535.0, ALU.subtract, ALU.min)
            v.scalar_tensor_tensor(ctB[sl], f0w,
                                   B.scal_sb[:, SC_SGC:SC_SGC + 1], f1w,
                                   ALU.mult, ALU.mult)
            v.tensor_reduce(B.red_sb[:, 7:8], ctB[sl], AX.X, ALU.max,
                            apply_absolute_value=True)
            v.drain()
            if firstc:
                v.tensor_copy(B.red_sb[:, 3:4], B.red_sb[:, 7:8])
            else:
                v.tensor_max(B.red_sb[:, 3:4], B.red_sb[:, 3:4],
                             B.red_sb[:, 7:8])
            B.sem_v.dinc(v)
        ev = B.sem_v.n

        @B.sec("sync")
        def _(s, oc=oc, sl=sl, ev=ev):
            s.wait_ge(B.sem_v.h, ev)
            B.sem_x.inc(s.dma_start(
                out=B.g2p_pk.rearrange("(c p) t -> p c t", p=128)[:, oc, :],
                in_=ctB[sl]), 16)
        stB_use[sl] = B.sem_x.n

    chain_pass(tileB)
    c3 = collective_max(B, 3, 1, B.sem_v, B.sem_v.n)

    @B.sec("vector")
    def _(v):
        v.wait_ge(B.sem_x.h, c3)
        sc = B.scal_sb
        v.tensor_scalar(sc[:, SC_S2:SC_S2 + 1], sc[:, SC_CC0:SC_CC0 + 1],
                        1.0 / QMAX16, 1e-12, ALU.mult, ALU.max)
        v.drain()
        v.reciprocal(sc[:, SC_I2:SC_I2 + 1], sc[:, SC_S2:SC_S2 + 1])
        v.tensor_mul(sc[:, SC_C3:SC_C3 + 1], sc[:, SC_S2:SC_S2 + 1],
                     sc[:, SC_SU:SC_SU + 1])
        B.sem_v.dinc(v)
    c3s_tok = B.sem_v.n

    ldC_use = [B.sem_v.n, B.sem_v.n]
    stC_use = [0, 0]

    def tileC(oc, sl):
        @B.sec("sync")
        def _(s, oc=oc, sl=sl, u=ldC_use[sl]):
            s.wait_ge(B.sem_v.h, u)
            B.sem_w.inc(s.dma_start(
                out=ctA[sl],
                in_=B.upp_pk.rearrange("(c p) t -> p c t", p=128)[:, oc, :]),
                16)
            B.sem_w.inc(s.dma_start(
                out=ctB[sl],
                in_=B.g2p_pk.rearrange("(c p) t -> p c t", p=128)[:, oc, :]),
                16)
        ld = B.sem_w.n

        @B.sec("vector")
        def _(v, sl=sl, ld=ld, u=stC_use[sl], firstc=(oc == 0)):
            v.wait_ge(B.sem_w.h, ld)
            if firstc:
                v.wait_ge(B.sem_v.h, c3s_tok)
            if u:
                v.wait_ge(B.sem_x.h, u)
            v.tensor_scalar_mul(f0w, ctA[sl], B.scal_sb[:, SC_IU:SC_IU + 1])
            v.tensor_scalar_add(f0w, f0w, MAGIC)
            v.tensor_scalar_sub(f0w, f0w, MAGIC)
            v.tensor_scalar_mul(f1w, ctB[sl], B.scal_sb[:, SC_I2:SC_I2 + 1])
            v.tensor_scalar_add(f1w, f1w, MAGIC)
            v.tensor_scalar_sub(f1w, f1w, MAGIC)
            v.scalar_tensor_tensor(ctB[sl], f1w,
                                   B.scal_sb[:, SC_C3:SC_C3 + 1], f0w,
                                   ALU.mult, ALU.mult)
            v.tensor_reduce(B.red_sb[:, 7:8], ctB[sl], AX.X, ALU.max,
                            apply_absolute_value=True)
            v.drain()
            if firstc:
                v.tensor_copy(B.red_sb[:, 4:5], B.red_sb[:, 7:8])
            else:
                v.tensor_max(B.red_sb[:, 4:5], B.red_sb[:, 4:5],
                             B.red_sb[:, 7:8])
            B.sem_v.dinc(v)
        ev = B.sem_v.n
        ldC_use[sl] = ev

        @B.sec("sync")
        def _(s, oc=oc, sl=sl, ev=ev):
            s.wait_ge(B.sem_v.h, ev)
            B.sem_x.inc(s.dma_start(
                out=B.opp_pk.rearrange("(c p) t -> p c t", p=128)[:, oc, :],
                in_=ctB[sl]), 16)
        stC_use[sl] = B.sem_x.n

    chain_pass(tileC)
    c4 = collective_max(B, 4, 1, B.sem_v, B.sem_v.n)

    @B.sec("vector")
    def _(v):
        v.wait_ge(B.sem_x.h, c4)
        sc = B.scal_sb
        v.tensor_scalar(sc[:, SC_SO:SC_SO + 1], sc[:, SC_CC0:SC_CC0 + 1],
                        1.0 / QMAX16, 1e-12, ALU.mult, ALU.max)
        v.drain()
        v.reciprocal(sc[:, SC_IO:SC_IO + 1], sc[:, SC_SO:SC_SO + 1])
        B.sem_v.dinc(v)
    c4s_tok = B.sem_v.n
    B.tap("scal", lambda: B.scal_sb[0:1, :], B.sem_v, c4s_tok)
    B.tap("red", lambda: B.red_sb[:, :], B.sem_v, c4s_tok)

    ldD_use = [B.sem_v.n, B.sem_v.n]
    stD_use = [0, 0]

    def tileD(oc, sl):
        @B.sec("sync")
        def _(s, oc=oc, sl=sl, u=ldD_use[sl]):
            s.wait_ge(B.sem_v.h, u)
            B.sem_w.inc(s.dma_start(
                out=ctA[sl],
                in_=B.opp_pk.rearrange("(c p) t -> p c t", p=128)[:, oc, :]),
                16)
        ld = B.sem_w.n

        @B.sec("vector")
        def _(v, sl=sl, ld=ld, u=stD_use[sl], first=(oc == 0)):
            v.wait_ge(B.sem_w.h, ld)
            if first:
                v.wait_ge(B.sem_v.h, c4s_tok)
            if u:
                v.wait_ge(B.sem_x.h, u)
            v.tensor_scalar_mul(f0w, ctA[sl], B.scal_sb[:, SC_IO:SC_IO + 1])
            v.tensor_scalar_add(f0w, f0w, MAGIC)
            v.tensor_scalar_sub(ots[sl], f0w, MAGIC)
            B.sem_v.dinc(v)
        ev = B.sem_v.n
        ldD_use[sl] = ev

        @B.sec("sync")
        def _(s, oc=oc, sl=sl, ev=ev):
            s.wait_ge(B.sem_v.h, ev)
            B.sem_x.inc(s.dma_start(
                out=B.oq_pk.rearrange("(c p) t -> p c t", p=128)[:, oc, :],
                in_=ots[sl]), 16)
        stD_use[sl] = B.sem_x.n

    chain_pass(tileD)
    oq_tok = B.sem_x.n

    wdT = B.bba(0, 24576, [128, 8, 24, 128])
    attn_done_pe = B.sem_pe.n
    nat_use2 = [B.sem_v.n, B.sem_v.n]
    wqn_use2 = [(B.sem_v, B.sem_v.n), (B.sem_v, B.sem_v.n)]
    first_tr = True
    for oc in range(8):
        for third in range(3):
            sl = (oc * 3 + third) % 2

            @B.sec("sync")
            def _(s, oc=oc, third=third, sl=sl, u=nat_use2[sl]):
                s.wait_ge(B.sem_v.h, u)
                B.sem_w.inc(s.dma_start(
                    out=nat_s[sl],
                    in_=B.wdown_n[oc * 128:(oc + 1) * 128,
                                  third * 1024:(third + 1) * 1024]), 16)
            ld = B.sem_w.n

            dq_tok = dequant_slice(B, nat_s[sl], wqn_s[sl], 32, B.sem_w, ld,
                                   extra=wqn_use2[sl])
            nat_use2[sl] = dq_tok

            @B.sec("sync")
            def _(s, oc=oc, third=third, sl=sl, dq=dq_tok, ft=first_tr,
                  ap=attn_done_pe):
                s.wait_ge(B.sem_v.h, dq)
                if ft:
                    s.wait_ge(B.sem_pe.h, ap)
                B.sem_t.inc(s.dma_start_transpose(
                    wdT[:, oc, third * 8:(third + 1) * 8, :],
                    wqn_s[sl][:, 0:1024]), 16)
            wqn_use2[sl] = (B.sem_t, B.sem_t.n)
            first_tr = False
    wdT_tok = B.sem_t.n

    x1st2 = [B.XCH[:, i * 512:(i + 1) * 512] for i in range(2)]
    outst2 = [B.XCH[:, 1024 + i * 512:1024 + (i + 1) * 512] for i in range(2)]
    oqst = [B.bbb(i * 512, (i + 1) * 512) for i in range(2)]
    ost_use = [0, 0]
    x1_use = [0, 0]
    ot_use = [B.sem_pe.n, B.sem_pe.n]
    for nb in range(NQB):
        for kc in range(24):
            sl = kc % 2

            @B.sec("sync")
            def _(s, nb=nb, kc=kc, sl=sl, u=ot_use[sl], oqt=oq_tok):
                s.wait_ge(B.sem_pe.h, u)
                if kc == 0 and nb == 0:
                    s.wait_ge(B.sem_x.h, oqt)
                B.sem_w.inc(s.dma_start(
                    out=oqst[sl],
                    in_=B.oq_pk.rearrange("(c p) t -> p c t", p=128)[
                        :, kc, nb * 512:(nb + 1) * 512]), 16)
            ld = B.sem_w.n

            @B.sec("tensor")
            def _(t, nb=nb, kc=kc, sl=sl, ld=ld, wt=wdT_tok):
                t.wait_ge(B.sem_w.h, ld)
                if kc == 0 and nb == 0:
                    t.wait_ge(B.sem_t.h, wt)
                if kc == 0:
                    B.pe_guard(t, list(range(8)))
                ins = None
                for oc in range(8):
                    ins = t.matmul(PS[:, oc, :], wdT[:, oc, kc, :], oqst[sl],
                                   start=(kc == 0), stop=(kc == 23))
                B.sem_pe.inc(ins)
            ot_use[sl] = B.sem_pe.n
        mm_last = B.sem_pe.n

        for oc in range(8):
            sl = oc % 2

            @B.sec("sync")
            def _(s, nb=nb, oc=oc, sl=sl, u=x1_use[sl]):
                if u:
                    s.wait_ge(B.sem_v.h, u)
                B.sem_x.inc(s.dma_start(
                    out=x1st2[sl],
                    in_=B.x1_pk.rearrange("(c p) t -> p c t", p=128)[
                        :, oc, nb * 512:(nb + 1) * 512]), 16)
            rb_tok = B.sem_x.n

            @B.sec("vector")
            def _(v, oc=oc, sl=sl, mm=mm_last, rb=rb_tok, u=ost_use[sl]):
                v.wait_ge(B.sem_pe.h, mm)
                v.wait_ge(B.sem_x.h, rb)
                if u:
                    v.wait_ge(B.sem_x.h, u)
                v.scalar_tensor_tensor(
                    outst2[sl], PS[:, oc, :], B.scal_sb[:, SC_SO:SC_SO + 1],
                    x1st2[sl], ALU.mult, ALU.add)
                B.sem_v.dinc(v)
            B.bank_done([oc], B.sem_v)
            ev = B.sem_v.n
            x1_use[sl] = ev

            @B.sec("sync")
            def _(s, oc=oc, nb=nb, sl=sl, ev=ev):
                s.wait_ge(B.sem_v.h, ev)
                B.sem_x.inc(s.dma_start(
                    out=B.out.rearrange("(c p) t -> p c t", p=128)[
                        :, oc, nb * 512:(nb + 1) * 512],
                    in_=outst2[sl]), 16)
            ost_use[sl] = B.sem_x.n

    @B.sec("sync")
    def _(s):
        s.wait_ge(B.sem_x.h, B.sem_x.n)



_CACHE = {}


def build(debug_taps=(), n_cores=NCORE, stop_after=None, dbg_bar=""):
    key = (tuple(t[0] for t in debug_taps), n_cores, stop_after, dbg_bar)
    if key in _CACHE:
        return _CACHE[key]
    B = Builder(debug_taps, n_cores)
    B.stop_after = stop_after
    B.dbg_bar = dbg_bar
    p0_consts(B)
    ln_stats(B, B.xT, layer=0)
    p1_xn_acc(B)
    p2_qkv(B)
    p2_attn(B)
    if stop_after in ("nostats", "min2"):
        pass
    elif stop_after is None:
        p3_wo_park(B)
        p4_ffn0(B)
        p5_ln1(B)
        p6_attn1(B)
        p7_ffn1(B)
    @B.sec("sync")
    def _(s):
        s.wait_ge(B.sem_x.h, B.sem_x.n)
    nc = B.finish()
    _CACHE[key] = (nc, B)
    return nc, B


def stage_inputs(inputs):
    f32 = lambda a: np.ascontiguousarray(np.asarray(a, dtype=np.float32))
    x = f32(inputs["L0_attn_x"])
    qkvT = np.transpose(f32(inputs["L1_attn_Wqkv"]), (1, 0, 2)).reshape(1024, 3072)
    bqkv = f32(inputs["L1_attn_bqkv"]).reshape(3072)
    n_idx = np.arange(512, dtype=np.float32)[None, None, :]
    p_idx = np.arange(128, dtype=np.float32)[:, None, None]
    r_idx = np.arange(4, dtype=np.float32)[None, :, None]
    import ml_dtypes
    cmask = np.where(n_idx <= 128.0 * r_idx + p_idx, 0.0, BIGNEG).astype(
        ml_dtypes.bfloat16)
    shared = {
        "wq_t": _tile_om(f32(inputs["L0_attn_Wq"])),
        "wk_t": _tile_om(f32(inputs["L0_attn_Wk"])),
        "wv_m": _tile_mv(f32(inputs["L0_attn_Wv"])),
        "wo_t": _tile_om(f32(inputs["L0_attn_Wo"])),
        "nw_p": _pcol(f32(inputs["L0_attn_norm_w"])),
        "gate_s": f32(inputs["L0_attn_gate"]).reshape(1, 1),
        "ltemp_s": f32(inputs["L0_attn_log_temperature"]).reshape(1, 1),
        "w1_t": _tile_om(f32(inputs["L0_ffn_w1_w"]).T),
        "w2_t": _tile_om(f32(inputs["L0_ffn_w2_w"]).T),
        "h1_t": _tile_om(f32(inputs["L0_ffn_h1_w"]).T),
        "h2_t": _tile_om(f32(inputs["L0_ffn_h2_w"]).T),
        "w3_t": _tile_om(f32(inputs["L0_ffn_w3_w"]).T),
        "s1_p": _pcol(f32(inputs["L0_ffn_w1_s"])),
        "s2_p": _pcol(f32(inputs["L0_ffn_w2_s"])),
        "sh1_p": _pcol(f32(inputs["L0_ffn_h1_s"])),
        "sh2_p": _pcol(f32(inputs["L0_ffn_h2_s"])),
        "s3_p": _pcol(f32(inputs["L0_ffn_w3_s"])),
        "qkvq_t": _tile_om(np.ascontiguousarray(qkvT[:, 0:1024])),
        "qkvk_t": _tile_om(np.ascontiguousarray(qkvT[:, 1024:2048])),
        "qkvv_m": _tile_mv(np.ascontiguousarray(qkvT[:, 2048:3072])),
        "bq_p": _pcol(bqkv[0:1024]),
        "bk_p": _pcol(bqkv[1024:2048]),
        "bv_row": np.ascontiguousarray(bqkv[2048:3072]),
        "wp_t": _tile_om(f32(inputs["L1_attn_Wp"]).T),
        "bp_p": _pcol(f32(inputs["L1_attn_bp"])),
        "wup_n": f32(inputs["L1_ffn_w_up"]),
        "wgate_n": f32(inputs["L1_ffn_w_gate"]),
        "wdown_n": f32(inputs["L1_ffn_w_down"]),
        "ident_in": np.eye(128, dtype=np.float32),
        "cmask_in": cmask,
    }
    xTs = [np.ascontiguousarray(x[c].T) for c in range(NCORE)]
    return shared, xTs


def kernel(**inputs):
    nc, B = build()
    shared, xTs = stage_inputs(inputs)
    in_maps = [dict(shared, xT=xTs[c]) for c in range(NCORE)]
    res = run_bass_kernel_spmd(nc, in_maps, core_ids=list(range(NCORE)))
    return np.stack([np.asarray(res.results[c]["out"]).T for c in range(NCORE)])
